# version 15
# baseline (speedup 1.0000x reference)
import math
import numpy as np
import ml_dtypes
import concourse.bass as bass
import concourse.mybir as mybir
from concourse.bass_utils import run_bass_kernel_spmd

F32 = mybir.dt.float32
BF16 = mybir.dt.bfloat16
AF = mybir.ActivationFunctionType
ALU = mybir.AluOpType
AX = mybir.AxisListType

S = 2048
D = 1024
NT = 16
NC = 8
NB = 2
HD = 64
NH = 8
FK = 4
FDH = 64
LN_EPS = 1e-5
RMS_EPS = float(np.finfo(np.float32).eps)
LAMBDA_INIT = 0.8 - 0.6 * math.exp(-0.3 * 4)
ONE_MINUS_LI = 1.0 - LAMBDA_INIT
NEG_BIG = -30000.0


def _rope_tables():
    dim = HD
    inv_freq = 1.0 / 10000.0 ** (np.arange(0, dim, 2, dtype=np.float32) / dim)
    t = np.arange(S, dtype=np.float32)
    freqs = np.outer(t, inv_freq)
    return np.cos(freqs).astype(np.float32), np.sin(freqs).astype(np.float32)


def _consts():
    c = {}
    cos, sin = _rope_tables()
    c["rope_cos"] = cos
    c["rope_sin"] = sin
    c["ident_bf"] = np.eye(128, dtype=np.float32).astype(ml_dtypes.bfloat16)
    c["ident_f32"] = np.eye(128, dtype=np.float32)
    m = np.zeros((128, 128), dtype=np.float32)
    for k in range(128):
        m[k, :k] = NEG_BIG
    c["diag_bf"] = m.astype(ml_dtypes.bfloat16)
    e = np.zeros((16, NC * 128), dtype=np.float32)
    for cc in range(NC):
        for p in range(128):
            e[2 * cc + p // 64, cc * 128 + p] = 1.0
    c["e64_bf"] = e.astype(ml_dtypes.bfloat16)
    ik = np.zeros((128, FK * FDH), dtype=np.float32)
    for p in range(128):
        for k in range(FK):
            ik[p, k * FDH + (p % 64)] = 1.0
    c["ikron_bf"] = ik.astype(ml_dtypes.bfloat16)
    c["kcols"] = np.tile(np.arange(FK, dtype=np.float32)[None, :], (16, 1))
    lt = np.zeros((16, 16), dtype=np.float32)
    for p in range(16):
        lt[p, p + 1:] = 1.0
    c["lt16"] = lt
    return c


class Sem:
    def __init__(self, handle):
        self.h = handle
        self.n = 0

    def inc(self, k=1):
        self.n += k
        return self.n


class MK:
    ENGINES = ("pe", "act", "dve", "gps", "sp")

    def __init__(self, taps=()):
        self.taps = set(taps)
        nc = self.nc = bass.Bass("TRN2", target_bir_lowering=False)
        self.q = {e: [] for e in self.ENGINES}
        self.tap_shapes = {}
        C = _consts()

        self.x_io = nc.dram_tensor("x", [NB * S, D], F32, kind="ExternalInput").ap()
        self.y_io = nc.dram_tensor("y", [NB * S, D], F32, kind="ExternalOutput").ap()
        self.w = {}
        for name, shape in [
            ("attn_wqT", [D, D]), ("attn_wkT", [D, D]), ("attn_wvT", [D, D]),
            ("attn_wpT", [D, D]),
            ("ffn_wqT", [D, D]), ("ffn_wkT", [D, D]), ("ffn_wvT", [D, D]),
            ("ffn_wr", [D, 16]), ("ffn_br", [16]),
            ("ffn_bq", [D]), ("ffn_bk", [D]), ("ffn_bv", [D]),
            ("ffn_wo", [256, D]), ("ffn_bo", [D]),
            ("lq1", [HD]), ("lk1", [HD]), ("lq2", [HD]), ("lk2", [HD]),
        ]:
            self.w[name] = nc.dram_tensor(name, shape, F32, kind="ExternalInput").ap()
        self.cst = {k: nc.inline_tensor(np.asarray(v), f"c_{k}").ap() for k, v in C.items()}

        self.lnx_d = nc.dram_tensor("lnx_d", [S, D], BF16, kind="Internal").ap()
        self.qhat_d = nc.dram_tensor("qhat_d", [S, D], BF16, kind="Internal").ap()
        self.khat_d = nc.dram_tensor("khat_d", [S, D], BF16, kind="Internal").ap()
        self.vhat_d = nc.dram_tensor("vhat_d", [S, NH * 129], BF16, kind="Internal").ap()
        self.zhat_d = nc.dram_tensor("zhat_d", [NH * S, 128], BF16, kind="Internal").ap()
        self.zffn_d = nc.dram_tensor("zffn_d", [S, 256], BF16, kind="Internal").ap()

        sbuf = lambda n, s, d: nc.alloc_sbuf_tensor(n, list(s), d).ap()
        self.x_res = sbuf("x_res", [128, NT * D], F32)
        self.reg_a = sbuf("reg_a", [128, NC * S], BF16)
        self.big_b = sbuf("big_b", [128, NC * S], BF16)
        self.qk_strm = sbuf("qk_strm", [128, 8192], BF16)
        self.wselg = sbuf("wselg", [128, 8192], BF16)
        self.vstrips = sbuf("vstrips", [128, 4160], BF16)
        self.vprod = sbuf("vprod", [128, 1032], BF16)
        self.scr = sbuf("scr", [128, D], F32)
        self.scr2 = sbuf("scr2", [128, D], F32)
        self.qk_bf = sbuf("qk_bf", [128, D], BF16)
        self.pt = sbuf("pt", [128, 2048], BF16)
        self.zstrip = sbuf("zstrip", [128, NT * 256], BF16)
        self.stats = sbuf("stats", [128, 304], F32)
        self.rowbuf = sbuf("rowbuf", [128, 16], F32)
        self.cosb_bf = sbuf("cosb_bf", [128, NT * 32], BF16)
        self.sinb_bf = sbuf("sinb_bf", [128, NT * 32], BF16)
        self.cbf = sbuf("cbf", [128, 648], BF16)
        self.c32 = sbuf("c32", [128, 272], F32)
        self.e64 = sbuf("e64", [16, NC * 128], BF16)
        self.sm = sbuf("sm", [16, 384], F32)
        self.rows = sbuf("rows", [1, 420], F32)
        self.rowsb = sbuf("rowsb", [1, 1024], BF16)

        self.pwide = [nc.alloc_psum_tensor(f"pw{i}", [128, 1024], F32).ap() for i in range(4)]
        self.pb = [self.pwide[i // 2][:, (i % 2) * 512:(i % 2) * 512 + 512] for i in range(8)]
        self.pbig = [self.pwide[2], self.pwide[3]]

        self.qsem = {
            "sp": [Sem(nc.alloc_semaphore(f"d_sp{i}")) for i in range(8)],
            "gps": [Sem(nc.alloc_semaphore(f"d_gp{i}")) for i in range(4)],
            "act": [Sem(nc.alloc_semaphore(f"d_ac{i}")) for i in range(4)],
        }
        self.qrr = {"sp": 0, "gps": 0, "act": 0}
        self.PE = Sem(nc.alloc_semaphore("s_pe"))
        self.ACT = Sem(nc.alloc_semaphore("s_act"))
        self.DVE = Sem(nc.alloc_semaphore("s_dve"))
        self.GPS = Sem(nc.alloc_semaphore("s_gps"))
        self.ALLSEM = ([self.PE, self.ACT, self.DVE, self.GPS]
                       + self.qsem["sp"] + self.qsem["gps"] + self.qsem["act"])

        self.ident_bf = self.cbf[:, 0:128]
        self.diag_bf = self.cbf[:, 128:256]
        self.ikron = self.cbf[:, 256:512]
        self.ones_row_bf = self.cbf[0:1, 516:644]
        self.ident_f32 = self.c32[:, 0:128]
        self.inv2048_col = self.c32[:, 128:129]
        self.lneps_col = self.c32[:, 129:130]
        self.rmseps_col = self.c32[:, 130:131]
        self.one_col_f32 = self.c32[:, 131:132]
        self.neg8_col = self.c32[:, 132:133]
        self.neglam_col = self.c32[:, 133:134]
        self.lnomli_col = self.c32[:, 134:135]
        self.lnzero_col = self.c32[:, 135:136]
        self.ones_row_f32 = self.c32[0:1, 140:268]

        self.wsel = self.wselg[:, 0:6144]
        self.gsel = self.wselg[:, 6144:8192]
        self.qsel = self.qk_strm[:, 0:4096]
        self.ksel = self.qk_strm[:, 4096:8192]

    def on(self, eng, fn):
        self.q[eng].append(fn)

    def wait(self, eng, sem, val):
        if val > 0:
            self.q[eng].append(lambda e, s=sem, v=val: e.wait_ge(s.h, v))

    def barrier_into(self, eng):
        for s in self.ALLSEM:
            self.wait(eng, s, s.n)

    def barrier(self):
        for e in self.ENGINES:
            self.barrier_into(e)

    def _chan(self, queue):
        chans = self.qsem[queue]
        ch = chans[self.qrr[queue] % len(chans)]
        self.qrr[queue] += 1
        return ch

    def dma(self, queue, out, in_, **kw):
        ch = self._chan(queue)
        v0 = ch.n
        self.on(queue, lambda e, o=out, i=in_, s=ch, w=v0, kw=kw: (
            e.wait_ge(s.h, w), e.dma_start(out=o, in_=i, **kw).then_inc(s.h, 16)))
        return ch, ch.inc(16)

    def dmat(self, queue, out, in_):
        ch = self._chan(queue)
        v0 = ch.n
        self.on(queue, lambda e, o=out, i=in_, s=ch, w=v0: (
            e.wait_ge(s.h, w), e.dma_start_transpose(o, i).then_inc(s.h, 16)))
        return ch, ch.inc(16)

    def wait_q(self, eng, queue):
        for ch in self.qsem[queue]:
            self.wait(eng, ch, ch.n)

    def _mk(self, eng, sem, fn, inc):
        w = sem.n

        def step(e, f=fn, s=sem, w=w):
            e.wait_ge(s.h, w)
            f(e).then_inc(s.h, 1)
        self.on(eng, step)
        return sem.inc(1)

    def dve(self, fn, inc=True):
        return self._mk("dve", self.DVE, fn, inc)

    def gps(self, fn, inc=True):
        return self._mk("gps", self.GPS, fn, inc)

    def act(self, fn, inc=True):
        return self._mk("act", self.ACT, fn, inc)

    def pe(self, fn, inc=True):
        if inc:
            self.on("pe", lambda e, f=fn, s=self.PE: f(e).then_inc(s.h, 1))
            return self.PE.inc(1)
        self.on("pe", fn)
        return self.PE.n

    def tap(self, name, ap, dtype=None):
        if name not in self.taps:
            return
        shape = [ap.shape[0], int(np.prod(ap.shape[1:]))]
        dtype = dtype or ap.dtype
        o = self.nc.dram_tensor(f"tap_{name}", shape, dtype, kind="ExternalOutput").ap()
        self.tap_shapes[f"tap_{name}"] = (shape, dtype)
        self.barrier_into("act")
        self.dma("act", o[:, :], ap)

    def tapd(self, name, dram_ap, shape, dtype):
        if name not in self.taps:
            return
        o = self.nc.dram_tensor(f"tap_{name}", list(shape), dtype, kind="ExternalOutput").ap()
        self.tap_shapes[f"tap_{name}"] = (list(shape), dtype)
        self.barrier_into("act")
        self.dma("act", o[:, :], dram_ap)

    def x_tile(self, tt):
        return self.x_res[:, tt * D:(tt + 1) * D]

    def emit_setup(self):
        P, w, cst = self, self.w, self.cst
        P.dma("sp", self.cbf[:, 0:128], cst["ident_bf"][:, :])
        P.dma("sp", self.cbf[:, 128:256], cst["diag_bf"][:, :])
        P.dma("sp", self.cbf[:, 256:512], cst["ikron_bf"][:, :])
        P.dma("sp", self.e64[:, :], cst["e64_bf"][:, :])
        P.dma("sp", self.c32[:, 0:128], cst["ident_f32"][:, :])
        P.dma("gps", self.cosb_bf.rearrange("p (t j) -> p t j", j=32),
              cst["rope_cos"].rearrange("(t p) j -> p t j", p=128))
        P.dma("gps", self.sinb_bf.rearrange("p (t j) -> p t j", j=32),
              cst["rope_sin"].rearrange("(t p) j -> p t j", p=128))
        P.dma("sp", self.sm[:, 0:4], cst["kcols"][:, :])
        P.dma("sp", self.sm[:, 4:20], cst["lt16"][:, :])
        P.dma("sp", self.stats[:, 0:128].rearrange("p (c j) -> p c j", j=16),
              w["ffn_wr"].rearrange("(c p) j -> p c j", p=128))
        self.wr_sb = self.stats[:, 0:128]
        P.dma("sp", self.rows[0:1, 0:64], w["lq1"][None, :])
        P.dma("sp", self.rows[0:1, 64:128], w["lk1"][None, :])
        P.dma("sp", self.rows[0:1, 128:192], w["lq2"][None, :])
        P.dma("sp", self.rows[0:1, 192:256], w["lk2"][None, :])
        P.dma("sp", self.rows[0:1, 256:272], w["ffn_br"][None, :])
        self.wait_q("dve", "sp")
        for ap, v in [
            (self.cbf[:, 512:516], 1.0), (self.cbf[0:1, 516:644], 1.0),
            (self.c32[:, 128:129], 1.0 / 2048.0), (self.c32[:, 129:130], LN_EPS),
            (self.c32[:, 130:131], RMS_EPS), (self.c32[:, 131:132], 1.0),
            (self.c32[:, 132:133], -8.0), (self.c32[0:1, 140:268], 1.0),
            (self.c32[:, 134:135], math.log(ONE_MINUS_LI)), (self.c32[:, 135:136], 0.0),
        ]:
            self.dve(lambda e, a=ap, vv=v: e.memset(a, vv))
        R = self.rows
        self.dve(lambda e: e.tensor_mul(R[0:1, 280:344], R[0:1, 0:64], R[0:1, 64:128]), inc=False)
        self.dve(lambda e: e.tensor_mul(R[0:1, 344:408], R[0:1, 128:192], R[0:1, 192:256]), inc=False)
        self.dve(lambda e: e.reduce_sum(R[0:1, 408:409], R[0:1, 280:344], axis=AX.X), inc=False)
        dv = self.dve(lambda e: e.reduce_sum(R[0:1, 409:410], R[0:1, 344:408], axis=AX.X))
        self.wait("act", self.DVE, dv)
        av = self.act(lambda e: e.activation(out=R[0:1, 412:414], in_=R[0:1, 408:410], func=AF.Exp))
        self.wait("dve", self.ACT, av)
        self.dve(lambda e: e.tensor_sub(R[0:1, 414:415], R[0:1, 412:413], R[0:1, 413:414]), inc=False)
        dv = self.dve(lambda e: e.tensor_scalar(R[0:1, 415:416], R[0:1, 414:415], LAMBDA_INIT, -1.0, ALU.add, ALU.mult))
        self.wait("pe", self.DVE, dv)
        pv = self.pe(lambda e: e.matmul(self.pb[0][:, 0:1], self.ones_row_f32, R[0:1, 415:416], start=True, stop=True))
        self.wait("dve", self.PE, pv)
        dv = self.dve(lambda e: e.tensor_copy(self.c32[:, 133:134], self.pb[0][:, 0:1]))
        self.wait("pe", self.DVE, dv)
        self.tap("lam", self.c32[:, 133:134])

    def emit_load_x(self, b):
        xv = self.x_io.rearrange("(b t p) f -> b p t f", b=NB, p=128)
        return self.dma("sp", self.x_res.rearrange("p (t f) -> p t f", f=D), xv[b])

    def emit_store_y(self, b):
        yv = self.y_io.rearrange("(b t p) f -> b p t f", b=NB, p=128)
        self.barrier_into("sp")
        return self.dma("sp", yv[b], self.x_res.rearrange("p (t f) -> p t f", f=D))

    def _ln_tile(self, tt, with_router):
        P = self
        st = self.stats
        xt = self.x_tile(tt)
        P.dve(lambda e, xt=xt, tt=tt: e.reduce_sum(st[:, 192 + 2 * tt:193 + 2 * tt], xt, axis=AX.X))
        P.wait("act", self.DVE, self.DVE.n)
        P.act(lambda e, xt=xt, tt=tt: e.activation(
            out=self.scr2[:, 0:D], in_=xt, func=AF.Square,
            accum_out=st[:, 193 + 2 * tt:194 + 2 * tt]))

    def _ln_finish(self, with_router):
        P = self
        st = self.stats
        P.wait("dve", self.ACT, self.ACT.n)
        pairs = st[:, 192:224].rearrange("p (a b) -> p a b", b=2)
        mu = st[:, 224:240]
        m2 = st[:, 240:256]
        P.dve(lambda e: e.tensor_scalar(mu, pairs[:, :, 0:1], 1.0 / D, None, ALU.mult), inc=False)
        P.dve(lambda e: e.tensor_scalar(m2, pairs[:, :, 1:2], 1.0 / D, None, ALU.mult), inc=False)
        P.dve(lambda e: e.tensor_mul(st[:, 136:152], mu, mu), inc=False)
        dv = P.dve(lambda e: e.tensor_sub(m2, m2, st[:, 136:152]))
        P.wait("act", self.DVE, dv)
        P.act(lambda e: e.activation(out=st[:, 136:152], in_=m2, func=AF.Ln, bias=self.lneps_col))
        av = P.act(lambda e: e.activation(out=m2, in_=st[:, 136:152], func=AF.Exp,
                                          scale=-0.5, bias=self.lnzero_col))
        P.wait("dve", self.ACT, av)
        rstd = m2
        st_ms = [None, None]
        for tt in range(NT):
            xt = self.x_tile(tt)
            slot = (self.qk_bf if tt % 2 == 0 else self.pt)[:, 0:D]
            if st_ms[tt % 2] is not None:
                P.wait("dve", st_ms[tt % 2][0], st_ms[tt % 2][1])
            if with_router:
                P.dve(lambda e, xt=xt, tt=tt: e.tensor_scalar(
                    self.scr[:, 0:D], xt, mu[:, tt:tt + 1], rstd[:, tt:tt + 1],
                    ALU.subtract, ALU.mult), inc=False)
                dv = P.dve(lambda e, slot=slot: e.tensor_copy(slot, self.scr[:, 0:D]))
                P.wait("pe", self.DVE, dv)
                P.pe(lambda e, tt=tt: e.matmul(self.pb[6][0:1, 0:512], self.inv2048_col,
                                               self.scr[:, 0:512], start=(tt == 0), stop=(tt == NT - 1)), inc=False)
                P.pe(lambda e, tt=tt: e.matmul(self.pb[7][0:1, 0:512], self.inv2048_col,
                                               self.scr[:, 512:1024], start=(tt == 0), stop=(tt == NT - 1)))
                P.wait("dve", self.PE, self.PE.n)
                P.wait("sp", self.PE, self.PE.n)
            else:
                P.dve(lambda e, xt=xt, tt=tt, slot=slot: e.tensor_scalar(
                    slot, xt, mu[:, tt:tt + 1], rstd[:, tt:tt + 1],
                    ALU.subtract, ALU.mult))
            P.wait("act", self.DVE, self.DVE.n)
            st_ms[tt % 2] = P.dma("act", self.lnx_d.rearrange("(t p) f -> t p f", p=128)[tt], slot)
        self._ln_transposes()

    def _ln_transposes(self):
        P = self
        P.wait_q("act", "act")
        P.wait_q("act", "sp")
        P.wait("act", self.PE, self.PE.n)
        for c in range(NC):
            P.dmat("act", self.reg_a[:, c * S:(c + 1) * S], self.lnx_d[:, c * 128:(c + 1) * 128])

    def emit_ln(self, b, with_router, tag=""):
        for tt in range(NT):
            self._ln_tile(tt, with_router)
        self._ln_finish(with_router)
        return None

    def emit_router(self, b, ci):
        P = self
        R, SM = self.rows, self.sm
        xmrow = self.scr2[0:1, 0:1024]
        P.wait("act", self.PE, self.PE.n)
        P.act(lambda e: e.copy(xmrow[0:1, 0:512], self.pb[6][0:1, 0:512]), inc=False)
        av = P.act(lambda e: e.copy(xmrow[0:1, 512:1024], self.pb[7][0:1, 0:512]))
        self.tap(f"xmean_{b}_{ci}", xmrow)
        P.wait("pe", self.ACT, av)
        for c in range(NC):
            P.pe(lambda e, c=c: e.transpose(self.pb[5][:, c:c + 1],
                                            xmrow[0:1, c * 128:(c + 1) * 128],
                                            self.ones_row_f32[0:1, 0:1]),
                 inc=(c == NC - 1))
        P.wait("dve", self.PE, self.PE.n)
        dv = P.dve(lambda e: e.tensor_copy(self.stats[:, 136:144], self.pb[5][:, 0:8]))
        P.wait("pe", self.DVE, dv)
        for c in range(NC):
            P.pe(lambda e, c=c: e.matmul(self.pb[4][0:1, 0:16], self.stats[:, 136 + c:137 + c],
                                         self.wr_sb[:, c * 16:(c + 1) * 16],
                                         start=(c == 0), stop=False), inc=False)
        pv = P.pe(lambda e: e.matmul(self.pb[4][0:1, 0:16], self.ones_row_f32[0:1, 0:1],
                                     R[0:1, 256:272], start=False, stop=True))
        P.wait("dve", self.PE, pv)
        dv = P.dve(lambda e: e.tensor_copy(SM[0:1, 128:144], self.pb[4][0:1, 0:16]))
        logits = SM[0:1, 128:144]
        self.tap(f"logits_{b}_{ci}", logits)
        P.dve(lambda e: e.reduce_max(SM[0:1, 144:145], logits, axis=AX.X), inc=False)
        dv = P.dve(lambda e: e.tensor_scalar(SM[0:1, 145:146], SM[0:1, 144:145], -1.0, None, ALU.mult))
        P.wait("act", self.DVE, dv)
        av = P.act(lambda e: e.activation(out=SM[0:1, 148:164], in_=logits, func=AF.Exp,
                                          bias=SM[0:1, 145:146], accum_out=SM[0:1, 146:147]))
        P.wait("dve", self.ACT, av)
        P.dve(lambda e: e.reciprocal(SM[0:1, 147:148], SM[0:1, 146:147]), inc=False)
        dv = P.dve(lambda e: e.tensor_scalar(SM[0:1, 164:180], SM[0:1, 148:164], SM[0:1, 147:148], None, ALU.mult))
        dist = SM[0:1, 164:180]
        self.tap(f"dist_{b}_{ci}", dist)
        P.wait("pe", self.DVE, dv)
        P.pe(lambda e: e.matmul(self.pb[4][0:16, 16:32], self.ones_row_f32[0:1, 0:16], dist,
                                start=True, stop=True), inc=False)
        pv = P.pe(lambda e: e.transpose(self.pb[4][0:16, 32:33], dist, self.ones_row_f32[0:1, 0:1]))
        P.wait("dve", self.PE, pv)
        P.dve(lambda e: e.tensor_copy(SM[:, 20:36], self.pb[4][0:16, 16:32]), inc=False)
        dv = P.dve(lambda e: e.tensor_copy(SM[:, 36:37], self.pb[4][0:16, 32:33]))
        drow, dcol = SM[:, 20:36], SM[:, 36:37]
        dcolb = dcol.broadcast_to([16, 16])
        P.dve(lambda e: e.tensor_tensor(SM[:, 37:53], dcolb, drow, ALU.is_gt), inc=False)
        P.dve(lambda e: e.tensor_tensor(SM[:, 53:69], dcolb, drow, ALU.is_equal), inc=False)
        P.dve(lambda e: e.tensor_mul(SM[:, 53:69], SM[:, 53:69], SM[:, 4:20]), inc=False)
        dv = P.dve(lambda e: e.tensor_add(SM[:, 37:53], SM[:, 37:53], SM[:, 53:69]))
        P.wait("pe", self.DVE, dv)
        pv = P.pe(lambda e: e.matmul(self.pb[4][0:16, 34:35], SM[:, 37:53], self.one_col_f32[0:16, 0:1],
                                     start=True, stop=True))
        P.wait("dve", self.PE, pv)
        dv = P.dve(lambda e: e.tensor_copy(SM[:, 69:70], self.pb[4][0:16, 34:35]))
        rankcol = SM[:, 69:70]
        self.tap(f"rank_{b}_{ci}", rankcol)
        P.dve(lambda e: e.tensor_scalar(SM[:, 70:71], rankcol, float(FK), None, ALU.is_lt), inc=False)
        dv = P.dve(lambda e: e.tensor_mul(SM[:, 71:72], dcol, SM[:, 70:71]))
        P.wait("act", self.DVE, dv)
        av = P.act(lambda e: e.activation(out=SM[:, 72:73], in_=SM[:, 71:72], func=AF.Exp))
        P.wait("pe", self.ACT, av)
        pv = P.pe(lambda e: e.matmul(self.pb[4][0:1, 36:37], SM[:, 72:73], self.one_col_f32[0:16, 0:1],
                                     start=True, stop=True))
        P.wait("dve", self.PE, pv)
        P.dve(lambda e: e.tensor_copy(SM[0:1, 180:181], self.pb[4][0:1, 36:37]), inc=False)
        dv = P.dve(lambda e: e.reciprocal(SM[0:1, 181:182], SM[0:1, 180:181]))
        P.wait("pe", self.DVE, dv)
        pv = P.pe(lambda e: e.matmul(self.pb[4][0:16, 38:39], self.ones_row_f32[0:1, 0:16],
                                     SM[0:1, 181:182], start=True, stop=True))
        P.wait("dve", self.PE, pv)
        P.dve(lambda e: e.tensor_copy(SM[:, 73:74], self.pb[4][0:16, 38:39]), inc=False)
        dv = P.dve(lambda e: e.tensor_mul(SM[:, 74:75], SM[:, 72:73], SM[:, 73:74]))
        gatescol = SM[:, 74:75]
        self.tap(f"gates_{b}_{ci}", gatescol)
        P.dve(lambda e: e.tensor_tensor(SM[:, 75:79], rankcol.broadcast_to([16, 4]), SM[:, 0:4], ALU.is_equal), inc=False)
        P.dve(lambda e: e.tensor_scalar(SM[:, 75:79], SM[:, 75:79], gatescol, None, ALU.mult), inc=False)
        dv = P.dve(lambda e: e.tensor_copy(
            self.sm.bitcast(BF16)[0:16, 512:768].rearrange("p (a b) -> p a b", b=64),
            SM[:, 75:79][:, :, None].broadcast_to([16, 4, 64])))
        bcmp256 = self.sm.bitcast(BF16)[0:16, 512:768]
        P.wait("pe", self.DVE, dv)
        for c in range(NC):
            pv = P.pe(lambda e, c=c: e.matmul(self.pb[3][:, 0:256], self.e64[:, c * 128:(c + 1) * 128],
                                              bcmp256, start=True, stop=True))
            P.wait("dve", self.PE, pv)
            dv = P.dve(lambda e, c=c: e.tensor_tensor(self.gsel[:, c * 256:(c + 1) * 256],
                                                      self.pb[3][:, 0:256], self.ikron, ALU.mult))
            P.wait("pe", self.DVE, dv)
        self.tap(f"G_{b}_{ci}", self.gsel[:, 0:2048])

    def emit_wfold(self, wi, bias_name, bias_slot, wb=None, slot=0):
        P = self
        base = wi * NC * 256
        if wb is None:
            wb = self.big_b[:, slot * NC * D:(slot + 1) * NC * D]
        ev = [0, 0]
        for fs in range(NC):
            bank = self.pb[2 + fs % 2]
            P.wait("pe", self.DVE, ev[fs % 2])
            for mc in range(NC):
                P.pe(lambda e, fs=fs, mc=mc, bank=bank: e.matmul(
                    bank[:, 0:256],
                    wb[:, mc * D + fs * 128: mc * D + (fs + 1) * 128],
                    self.gsel[:, mc * 256:(mc + 1) * 256],
                    start=(mc == 0), stop=(mc == NC - 1)), inc=(mc == NC - 1))
            P.wait("dve", self.PE, self.PE.n)
            P.dve(lambda e, fs=fs, bank=bank: e.tensor_copy(
                self.wsel[:, base + fs * 256: base + (fs + 1) * 256], bank[:, 0:256]))
            ev[fs % 2] = self.DVE.n
        P.wait("pe", self.DVE, self.DVE.n)
        brow = self.scr2[0:1, 0:1024]
        P.wait("sp", self.DVE, self.DVE.n)
        P.wait("sp", self.PE, self.PE.n)
        P.dma("sp", brow, self.w[bias_name][None, :])
        P.wait_q("pe", "sp")
        for c in range(NC):
            P.pe(lambda e, c=c: e.transpose(self.pb[2][:, c:c + 1],
                                            brow[0:1, c * 128:(c + 1) * 128],
                                            self.ones_row_f32[0:1, 0:1]), inc=(c == NC - 1))
        P.wait("dve", self.PE, self.PE.n)
        P.dve(lambda e: e.tensor_copy(self.stats[:, 144:152], self.pb[2][:, 0:8]), inc=False)
        dv = P.dve(lambda e: e.tensor_copy(self.stats.bitcast(BF16)[:, 304:312], self.stats[:, 144:152]))
        bcol_bf = self.stats.bitcast(BF16)[:, 304:312]
        P.wait("pe", self.DVE, dv)
        for h in range(2):
            for mc in range(NC):
                P.pe(lambda e, h=h, mc=mc: e.matmul(
                    self.pb[2][:, 8 + h:9 + h],
                    self.gsel[:, mc * 256 + h * 128: mc * 256 + (h + 1) * 128],
                    bcol_bf[:, mc:mc + 1],
                    start=(mc == 0), stop=(mc == NC - 1)), inc=(h == 1 and mc == NC - 1))
        P.wait("dve", self.PE, self.PE.n)
        dv = P.dve(lambda e, sl=bias_slot: e.tensor_copy(self.rowbuf[:, sl:sl + 2], self.pb[2][:, 8:10]))
        P.wait("pe", self.DVE, dv)

    def emit_ffn_qkv(self, b, ci):
        P = self
        evac_ms = [0, 0]
        P.wait("pe", self.DVE, self.DVE.n)
        for wi, dst, bslot in ((0, self.qsel, 0), (1, self.ksel, 2)):
            base = wi * NC * 256
            for h in range(2):
                for qb in range(4):
                    bank = self.pb[qb % 2]
                    P.wait("pe", self.DVE, evac_ms[qb % 2])
                    for fc in range(NC):
                        P.pe(lambda e, h=h, qb=qb, fc=fc, bank=bank, base=base: e.matmul(
                            bank[:, 0:512],
                            self.wsel[:, base + fc * 256 + h * 128: base + fc * 256 + (h + 1) * 128],
                            self.reg_a[:, fc * S + qb * 512: fc * S + (qb + 1) * 512],
                            start=(fc == 0), stop=(fc == NC - 1)), inc=(fc == NC - 1))
                    P.wait("dve", self.PE, self.PE.n)
                    P.dve(lambda e, h=h, qb=qb, bank=bank, dst=dst, bslot=bslot: e.tensor_scalar(
                        dst[:, h * S + qb * 512:h * S + (qb + 1) * 512],
                        bank[:, 0:512], self.rowbuf[:, bslot + h:bslot + h + 1], None, ALU.add))
                    evac_ms[qb % 2] = self.DVE.n
        self.tap(f"qsel_{b}_{ci}", self.qsel[:, :])
        self.tap(f"ksel_{b}_{ci}", self.ksel[:, :])
        for h in range(2):
            P.pe(lambda e, h=h: e.transpose(self.pb[2][0:1, 16 + h * 128:16 + (h + 1) * 128],
                                            self.rowbuf[:, 4 + h:5 + h], self.ident_f32),
                 inc=(h == 1))
        P.wait("dve", self.PE, self.PE.n)
        dv = P.dve(lambda e: e.tensor_copy(self.rowsb[0:1, 0:256], self.pb[2][0:1, 16:272]))
        P.wait("pe", self.DVE, dv)
        dv = P.dve(lambda e: e.memset(self.vstrips[:, 0:4160], 1.0))
        P.wait("pe", self.DVE, dv)
        vevac_ms = [0, 0]
        for tt in range(NT):
            bank = self.pb[2 + tt % 2]
            P.wait("pe", self.DVE, vevac_ms[tt % 2])
            for fc in range(NC):
                P.pe(lambda e, tt=tt, fc=fc, bank=bank: e.matmul(
                    bank[:, 0:256],
                    self.reg_a[:, fc * S + tt * 128: fc * S + (tt + 1) * 128],
                    self.wsel[:, 2 * NC * 256 + fc * 256: 2 * NC * 256 + (fc + 1) * 256],
                    start=(fc == 0), stop=False), inc=False)
            P.pe(lambda e, bank=bank: e.matmul(bank[:, 0:256], self.ones_row_bf,
                                               self.rowsb[0:1, 0:256], start=False, stop=True))
            P.wait("dve", self.PE, self.PE.n)
            P.dve(lambda e, tt=tt, bank=bank: e.tensor_copy(
                self.vstrips.rearrange("p (h t c) -> p h t c", h=FK, t=NT)[:, :, tt, 0:64],
                bank[:, 0:256].rearrange("p (h c) -> p h c", h=FK)[:, :, :]))
            vevac_ms[tt % 2] = self.DVE.n
        self.tap(f"vffn_{b}_{ci}", self.vstrips[:, 0:4160])

    def emit_ffn_attn(self, b, ci):
        P = self
        fixup_ms = [0, 0]
        exp_ms = [0, 0]
        av_ms = {}
        for hp in range(2):
            for qh in range(8):
                o = qh % 2
                obs = (self.pb[2 * o], self.pb[2 * o + 1])
                oput = [0, 0]
                n_mm_out = 2 * 16

                def emit_scores(g):
                    pw = self.pbig[g % 2]
                    P.wait("pe", self.ACT, exp_ms[g % 2])
                    for ki, kt in enumerate((2 * g, 2 * g + 1)):
                        for hi in range(2):
                            P.pe(lambda e, hp=hp, qh=qh, kt=kt, ki=ki, hi=hi, pw=pw: e.matmul(
                                pw[:, hi * 512 + ki * 256:hi * 512 + (ki + 1) * 256],
                                self.ksel[hi * 64:(hi + 1) * 64, hp * S + kt * 128:hp * S + (kt + 1) * 128],
                                self.qsel[hi * 64:(hi + 1) * 64, hp * S + qh * 256:hp * S + (qh + 1) * 256],
                                start=(ki == 0), stop=(ki == 1),
                                tile_position=(hi * 64, 0)), inc=(ki == 1 and hi == 1))
                    return self.PE.n

                def emit_exp(g, pe_ms):
                    pw = self.pbig[g % 2]
                    P.wait("act", self.PE, pe_ms)
                    P.wait("act", self.PE, av_ms.get(g % 4, 0))
                    P.act(lambda e, pw=pw, g=g: e.activation(
                        out=self.big_b[:, (g % 4) * 1024:(g % 4) * 1024 + 1024],
                        in_=pw[:, 0:1024], func=AF.Exp, scale=0.125))
                    exp_ms[g % 2] = self.ACT.n
                    return self.ACT.n

                def emit_av(g, act_ms):
                    P.wait("pe", self.ACT, act_ms)
                    if oput[0] == 0:
                        P.wait("pe", self.DVE, fixup_ms[o])
                    for hi in range(2):
                        ob = obs[hi]
                        h = 2 * hp + hi
                        for ki in range(2):
                            kt = 2 * g + ki
                            for qi in range(2):
                                first = (oput[hi] == 0)
                                oput[hi] += 1
                                last = (oput[hi] == n_mm_out)
                                P.pe(lambda e, h=h, hi=hi, g=g, ki=ki, kt=kt, qi=qi, ob=ob,
                                     first=first, last=last: e.matmul(
                                    ob[:, qi * 65:(qi + 1) * 65],
                                    self.big_b[:, (g % 4) * 1024 + hi * 512 + ki * 256 + qi * 128:
                                               (g % 4) * 1024 + hi * 512 + ki * 256 + (qi + 1) * 128],
                                    self.vstrips[:, h * 1040 + kt * 65: h * 1040 + (kt + 1) * 65],
                                    start=first, stop=last), inc=last)
                    av_ms[g % 4] = self.PE.n
                    return self.PE.n

                sc_ms = {0: emit_scores(0), 1: emit_scores(1)}
                e_ms = {0: emit_exp(0, sc_ms[0])}
                pe_av = 0
                for g in range(8):
                    if g + 1 < 8:
                        e_ms[g + 1] = emit_exp(g + 1, sc_ms[g + 1])
                    pe_av = emit_av(g, e_ms[g])
                    if g + 2 < 8:
                        sc_ms[g + 2] = emit_scores(g + 2)
                P.wait("dve", self.PE, pe_av)
                for hi in range(2):
                    h = 2 * hp + hi
                    ob = obs[hi]
                    dn = self.stats[:, 160 + 2 * hi:162 + 2 * hi]
                    P.dve(lambda e, ob=ob, dn=dn: e.reciprocal(
                        dn, ob[:, 0:130].rearrange("p (a c) -> p a c", c=65)[:, :, 64:65]))
                    for qi in range(2):
                        qt = 2 * qh + qi
                        P.dve(lambda e, ob=ob, dn=dn, qi=qi, qt=qt, h=h: e.tensor_scalar(
                            self.zstrip[:, qt * 256 + h * 64: qt * 256 + (h + 1) * 64],
                            ob[:, qi * 65: qi * 65 + 64], dn[:, qi:qi + 1], None, ALU.mult))
                fixup_ms[o] = self.DVE.n
        self.tap(f"zffn_{b}_{ci}", self.zstrip[:, :])

    def emit_ffn_out(self, b, ci, fuse_ln=None):
        P = self
        P.wait("sp", self.DVE, self.DVE.n)
        P.dma("sp", self.zffn_d.rearrange("(t p) c -> p t c", p=128),
              self.zstrip.rearrange("p (t c) -> p t c", c=256))
        P.wait_q("sp", "sp")
        zt = self.qk_strm[:, 0:2 * S]
        P.wait("sp", self.PE, self.PE.n)
        for c in range(2):
            P.dmat("sp", zt[:, c * S:(c + 1) * S], self.zffn_d[:, c * 128:(c + 1) * 128])
        P.wait("gps", self.PE, self.PE.n)
        P.dma("gps", self.qk_strm[:, 4096:6144].rearrange("p (c f) -> p c f", f=D), self.w["ffn_wo"].rearrange("(c p) f -> p c f", p=128))
        P.dma("gps", self.rowsb[0:1, 0:1024], self.w["ffn_bo"][None, :])
        wo = self.qk_strm[:, 4096:6144]
        if getattr(self, "_prefetch_attn_w", False):
            P.wait("gps", self.PE, self.PE.n)
            P.dma("gps", self.big_b[:, 0:NC * D].rearrange("p (c f) -> p c f", f=D),
                  self.w["attn_wqT"].rearrange("(c p) f -> p c f", p=128))
            P.dma("gps", self.big_b[:, NC * D:2 * NC * D].rearrange("p (c f) -> p c f", f=D),
                  self.w["attn_wkT"].rearrange("(c p) f -> p c f", p=128))
        P.wait_q("pe", "sp")
        P.wait_q("pe", "gps")
        P.wait("pe", self.DVE, self.DVE.n)
        res_ms = [0, 0]
        for tt in range(NT):
            P.wait("pe", self.DVE, res_ms[tt % 2])
            for half in range(2):
                bank = self.pb[2 * (tt % 2) + half]
                for c in range(2):
                    P.pe(lambda e, tt=tt, half=half, c=c, bank=bank: e.matmul(
                        bank[:, 0:512],
                        zt[:, c * S + tt * 128:c * S + (tt + 1) * 128],
                        wo[:, c * 1024 + half * 512:c * 1024 + (half + 1) * 512],
                        start=(c == 0), stop=False), inc=False)
                P.pe(lambda e, half=half, bank=bank: e.matmul(
                    bank[:, 0:512], self.ones_row_bf,
                    self.rowsb[0:1, half * 512:(half + 1) * 512], start=False, stop=True),
                    inc=(half == 1))
            P.wait("dve", self.PE, self.PE.n)
            for half in range(2):
                bank = self.pb[2 * (tt % 2) + half]
                xsl = self.x_res[:, tt * D + half * 512: tt * D + (half + 1) * 512]
                P.dve(lambda e, bank=bank, xsl=xsl: e.scalar_tensor_tensor(
                    xsl, bank[:, 0:512], 0.5, xsl, ALU.mult, ALU.add))
            res_ms[tt % 2] = self.DVE.n
            if fuse_ln is not None:
                self._ln_tile(tt, with_router=(fuse_ln == "router"))
        if fuse_ln is not None:
            self._ln_finish(with_router=(fuse_ln == "router"))

    def emit_ffn(self, b, ci, skip_ln=False, next_ln=None, wq_prefetched=False):
        P = self
        if not skip_ln:
            self.barrier()
        if skip_ln:
            self._ln_transposes()
        else:
            self.emit_ln(b, with_router=True, tag=f"ffn{ci}_{b}")
        if wq_prefetched:
            P.dma("gps", self.big_b[:, 0:NC * D].rearrange("p (c f) -> p c f", f=D),
                  self.w["ffn_wkT"].rearrange("(c p) f -> p c f", p=128))
            P.dma("gps", self.big_b[:, NC * D:2 * NC * D].rearrange("p (c f) -> p c f", f=D),
                  self.w["ffn_wvT"].rearrange("(c p) f -> p c f", p=128))
            self.emit_router(b, ci)
            P.wait_q("pe", "gps")
            self.emit_wfold(0, "ffn_bq", 0, wb=self.qk_strm[:, 0:8192])
            self.emit_wfold(1, "ffn_bk", 2, slot=0)
            self.emit_wfold(2, "ffn_bv", 4, slot=1)
        else:
            P.dma("gps", self.big_b[:, 0:NC * D].rearrange("p (c f) -> p c f", f=D),
                  self.w["ffn_wqT"].rearrange("(c p) f -> p c f", p=128))
            P.dma("gps", self.big_b[:, NC * D:2 * NC * D].rearrange("p (c f) -> p c f", f=D),
                  self.w["ffn_wkT"].rearrange("(c p) f -> p c f", p=128))
            self.emit_router(b, ci)
            P.wait_q("pe", "gps")
            self.emit_wfold(0, "ffn_bq", 0, slot=0)
            P.wait("gps", self.PE, self.PE.n)
            P.dma("gps", self.big_b[:, 0:NC * D].rearrange("p (c f) -> p c f", f=D),
                  self.w["ffn_wvT"].rearrange("(c p) f -> p c f", p=128))
            self.emit_wfold(1, "ffn_bk", 2, slot=1)
            P.wait_q("pe", "gps")
            self.emit_wfold(2, "ffn_bv", 4, slot=0)
        self.tap(f"wsel_{b}_{ci}", self.wsel[:, :])
        P.wait("pe", self.DVE, self.DVE.n)
        P.wait_q("pe", "act")
        self.emit_ffn_qkv(b, ci)
        P.wait("pe", self.DVE, self.DVE.n)
        P.wait("act", self.DVE, self.DVE.n)
        self.emit_ffn_attn(b, ci)
        P.wait("pe", self.DVE, self.DVE.n)
        self.emit_ffn_out(b, ci, fuse_ln=next_ln)
        self.barrier()

    def emit_attn_qkv(self, b, w_prefetched=False):
        P = self
        self.krope_ms = [0, 0]
        if not w_prefetched:
            P.wait("gps", self.PE, self.PE.n)
            P.dma("gps", self.big_b[:, 0:NC * D].rearrange("p (c f) -> p c f", f=D),
                  self.w["attn_wqT"].rearrange("(c p) f -> p c f", p=128))
            P.dma("gps", self.big_b[:, NC * D:2 * NC * D].rearrange("p (c f) -> p c f", f=D),
                  self.w["attn_wkT"].rearrange("(c p) f -> p c f", p=128))
        P.wait_q("pe", "gps")
        P.wait("pe", self.DVE, self.DVE.n)
        wbq = self.big_b[:, 0:NC * D]
        wbk = self.big_b[:, NC * D:2 * NC * D]
        ev_q = [0, 0]
        ev_k = [0, 0]
        st_q = [None, None]
        st_k = [None, None]
        slots_q = (self.qk_bf[:, 0:D], self.pt[:, 0:D])
        slots_k = (self.pt[:, 1024:2048], self.vprod[:, 0:D])
        for tt in range(NT):
            for which, wb, pwi, ev_ms, st_ms, slots, ddst in (
                (0, wbq, tt % 2, ev_q, st_q, slots_q, self.qhat_d),
                (1, wbk, 2 + tt % 2, ev_k, st_k, slots_k, self.khat_d),
            ):
                pw = self.pwide[pwi]
                b0, b1 = pw[:, 0:512], pw[:, 512:1024]
                P.wait("pe", self.ACT, ev_ms[tt % 2])
                for fc in range(NC):
                    for half, bank in ((0, b0), (1, b1)):
                        P.pe(lambda e, tt=tt, fc=fc, half=half, bank=bank, wb=wb: e.matmul(
                            bank[:, 0:512],
                            self.reg_a[:, fc * S + tt * 128:fc * S + (tt + 1) * 128],
                            wb[:, fc * D + half * 512:fc * D + (half + 1) * 512],
                            start=(fc == 0), stop=(fc == NC - 1)),
                            inc=(fc == NC - 1 and half == 1))
                ev_ms[tt % 2] = self._emit_rms_rope(tt, pw, ddst, st_ms, which,
                                                    slots[tt % 2])
        P.wait("gps", self.PE, self.PE.n)
        P.dma("gps", self.big_b[:, 0:NC * D].rearrange("p (c f) -> p c f", f=D),
              self.w["attn_wvT"].rearrange("(c p) f -> p c f", p=128))
        P.wait_q("pe", "gps")
        P.wait("dve", self.GPS, self.GPS.n)
        P.wait_q("dve", "sp")
        P.wait("pe", self.ACT, self.ACT.n)
        wb = self.big_b[:, 0:NC * D]
        vev = [0, 0]
        vst = [None, None]
        for tt in range(NT):
            b0, b1 = self.pb[0 + 2 * (tt % 2)], self.pb[1 + 2 * (tt % 2)]
            P.wait("pe", self.DVE, vev[tt % 2])
            for fc in range(NC):
                for half, bank in ((0, b0), (1, b1)):
                    P.pe(lambda e, tt=tt, fc=fc, half=half, bank=bank, wb=wb: e.matmul(
                        bank[:, 0:512],
                        self.reg_a[:, fc * S + tt * 128:fc * S + (tt + 1) * 128],
                        wb[:, fc * D + half * 512:fc * D + (half + 1) * 512],
                        start=(fc == 0), stop=(fc == NC - 1)),
                        inc=(fc == NC - 1 and half == 1))
            P.wait("dve", self.PE, self.PE.n)
            vp = (self.vprod if tt % 2 == 0 else self.vstrips)[:, 0:1032]
            if vst[tt % 2] is not None:
                P.wait("dve", vst[tt % 2][0], vst[tt % 2][1])
            P.dve(lambda e, vp=vp: e.memset(
                vp.rearrange("p (h c) -> p h c", h=NH)[:, :, 128:129], 1.0))
            P.dve(lambda e, b0=b0, vp=vp: e.tensor_copy(
                vp.rearrange("p (h c) -> p h c", h=NH)[:, 0:4, 0:128],
                b0.rearrange("p (h c) -> p h c", h=4)[:, :, :]))
            dv = P.dve(lambda e, b1=b1, vp=vp: e.tensor_copy(
                vp.rearrange("p (h c) -> p h c", h=NH)[:, 4:8, 0:128],
                b1.rearrange("p (h c) -> p h c", h=4)[:, :, :]))
            vev[tt % 2] = dv
            P.wait("sp", self.DVE, dv)
            vst[tt % 2] = P.dma("sp", self.vhat_d.rearrange("(t p) c -> t p c", p=128)[tt], vp)
        self.tap(f"vtm_attn_{b}", self.vstrips[:, :])
        self.tapd(f"qhatd_{b}", self.qhat_d[:, :], [S, D], BF16)
        self.tapd(f"khatd_{b}", self.khat_d[:, :], [S, D], BF16)
        self.tapd(f"vhatd_{b}", self.vhat_d[:, :], [S, NH * 129], BF16)

    def _emit_rms_rope(self, tt, pw, ddst, st_ms, wi, qslot):
        P = self
        if wi == 0:
            EV, ES, ev = P.dve, self.DVE, "dve"
            qf = self.scr[:, 0:D]
            sq = self.scr2[:, 0:D]
            sb = self.scr2.bitcast(BF16)
            qbf = sb[:, 0:1024].rearrange("p (h d) -> p h d", d=64)
            t1 = sb[:, 1024:1536].rearrange("p (h d) -> p h d", d=32)
            t2 = sb[:, 1536:2048].rearrange("p (h d) -> p h d", d=32)
            ms = self.stats[:, 152:168]
            lno = self.stats[:, 136:152]
            rstd = self.stats[:, 168:184]
        else:
            EV, ES, ev = P.gps, self.GPS, "gps"
            zbuf = (self.zstrip if tt % 2 == 0 else self.vstrips)
            zf = zbuf.bitcast(F32)
            qf = zf[:, 0:D]
            sq = zf[:, 1024:2048]
            sb = zbuf
            qbf = sb[:, 2048:3072].rearrange("p (h d) -> p h d", d=64)
            t1 = sb[:, 3072:3584].rearrange("p (h d) -> p h d", d=32)
            t2 = sb[:, 3584:4096].rearrange("p (h d) -> p h d", d=32)
            ms = self.stats[:, 256:272]
            lno = self.stats[:, 272:288]
            rstd = self.stats[:, 288:304]
            P.wait("act", self.GPS, self.krope_ms[tt % 2])
        P.wait("act", self.PE, self.PE.n)
        if wi == 0:
            P.wait("act", self.DVE, self.DVE.n)
        P.act(lambda e, pw=pw, qf=qf: e.copy(qf, pw[:, 0:1024]))
        av = P.act(lambda e, pw=pw, sq=sq: e.activation(out=sq, in_=pw[:, 0:1024], func=AF.Square))
        evac_av = av
        P.wait("dve", self.ACT, av)
        P.dve(lambda e, sq=sq, ms=ms: e.reduce_sum(ms, sq.rearrange("p (h d) -> p h d", d=64), axis=AX.X), inc=False)
        dv = P.dve(lambda e, ms=ms: e.tensor_scalar(ms, ms, 1.0 / 64.0, None, ALU.mult))
        P.wait("act", self.DVE, dv)
        if wi == 1:
            P.wait("act", self.GPS, self.GPS.n)
        P.act(lambda e, ms=ms, lno=lno: e.activation(out=lno, in_=ms, func=AF.Ln,
                                                     bias=self.rmseps_col))
        av = P.act(lambda e, lno=lno, rstd=rstd: e.activation(out=rstd, in_=lno,
                                                              func=AF.Exp, scale=-0.5, bias=self.lnzero_col))
        P.wait(ev, self.ACT, av)
        qv = qf.rearrange("p (h d) -> p h d", d=64)
        EV(lambda e, qv=qv, rstd=rstd, qbf=qbf: e.tensor_tensor(
            qbf, qv, rstd[:, :, None].broadcast_to([128, 16, 64]), ALU.mult), inc=False)
        cosb = self.cosb_bf[:, tt * 32:(tt + 1) * 32][:, None, :].broadcast_to([128, 16, 32])
        sinb = self.sinb_bf[:, tt * 32:(tt + 1) * 32][:, None, :].broadcast_to([128, 16, 32])
        qb = qslot.rearrange("p (h d) -> p h d", d=64)
        q1, q2 = qbf[:, :, 0:32], qbf[:, :, 32:64]
        if st_ms[tt % 2] is not None:
            P.wait(ev, st_ms[tt % 2][0], st_ms[tt % 2][1])
        EV(lambda e: e.tensor_tensor(t1, q1, cosb, ALU.mult), inc=False)
        EV(lambda e: e.tensor_tensor(t2, q2, sinb, ALU.mult), inc=False)
        EV(lambda e: e.tensor_tensor(qb[:, :, 0:32], t1, t2, ALU.add), inc=False)
        EV(lambda e: e.tensor_tensor(t1, q2, cosb, ALU.mult), inc=False)
        EV(lambda e: e.tensor_tensor(t2, q1, sinb, ALU.mult), inc=False)
        dv = EV(lambda e: e.tensor_tensor(qb[:, :, 32:64], t1, t2, ALU.subtract))
        if wi == 1:
            self.krope_ms[tt % 2] = dv
        P.wait("sp", ES, dv)
        st_ms[tt % 2] = P.dma("sp", ddst.rearrange("(t p) f -> t p f", p=128)[tt], qslot)
        return evac_av

    def emit_attn_core(self, b):
        P = self
        def issue_pair_loads(p):
            s = p % 2
            a = P.dmat("sp", self.qk_strm[:, s * 4096: s * 4096 + 2048],
                       self.qhat_d[:, p * 128:(p + 1) * 128])
            b2 = P.dmat("sp", self.qk_strm[:, s * 4096 + 2048: s * 4096 + 4096],
                        self.khat_d[:, p * 128:(p + 1) * 128])
            c = P.dma("sp", self.vstrips[:, s * 2080: s * 2080 + 2064].rearrange(
                          "p (t c) -> p t c", c=129),
                      self.vhat_d.rearrange("(t p) c -> p t c", p=128)[:, :, p * 129:(p + 1) * 129])
            return [a, b2, c]
        ready = {0: issue_pair_loads(0)}
        pe_after_pair = {}
        fixup_ms = [0, 0]
        exp_ms = [0, 0]
        av_ms = {}
        store_ms = None
        for p in range(NH):
            s = p % 2
            if p + 1 < NH:
                P.wait("sp", self.PE, pe_after_pair.get(p - 1, 0))
                ready[p + 1] = issue_pair_loads(p + 1)
            for sem_v in ready[p]:
                P.wait("pe", sem_v[0], sem_v[1])
            qT = self.qk_strm[:, s * 4096: s * 4096 + 2048]
            kT = self.qk_strm[:, s * 4096 + 2048: s * 4096 + 4096]
            vst = self.vstrips[:, s * 2080: s * 2080 + 2064]
            first_fix_of_pair = True
            for qh in range(8):
                o = qh % 2
                obs = (self.pb[2 * o], self.pb[2 * o + 1])
                oput = [0, 0]
                nkt = 2 * qh + 2
                ngroups = nkt // 2
                n_out = [0, 0]
                for kt in range(nkt):
                    for qi in range(2):
                        if 2 * qh + qi >= kt:
                            n_out[0] += 1
                            n_out[1] += 1

                def emit_scores(g):
                    kts = (2 * g, 2 * g + 1)
                    pw = self.pbig[g % 2]
                    P.wait("pe", self.ACT, exp_ms[g % 2])
                    diag = [(ki, qi) for ki, kt in enumerate(kts)
                            for qi in range(2) if 2 * qh + qi == kt]
                    for hi in range(2):
                        last_of_all = (hi == 1)
                        for ki, kt in enumerate(kts):
                            is_last = (not diag) and (ki == 1)
                            P.pe(lambda e, kt=kt, ki=ki, hi=hi, pw=pw, qh=qh,
                                 qT=qT, kT=kT, is_last=is_last: e.matmul(
                                pw[:, hi * 512 + ki * 256:hi * 512 + (ki + 1) * 256],
                                kT[hi * 64:(hi + 1) * 64, kt * 128:(kt + 1) * 128],
                                qT[hi * 64:(hi + 1) * 64, qh * 256:(qh + 1) * 256],
                                start=(ki == 0), stop=is_last,
                                tile_position=(hi * 64, 0)),
                                inc=(is_last and last_of_all))
                        for di, (ki, qi) in enumerate(diag):
                            is_last = (di == len(diag) - 1)
                            P.pe(lambda e, pw=pw, hi=hi, ki=ki, qi=qi, is_last=is_last: e.matmul(
                                pw[:, hi * 512 + ki * 256 + qi * 128:hi * 512 + ki * 256 + (qi + 1) * 128],
                                self.ident_bf, self.diag_bf,
                                start=False, stop=is_last),
                                inc=(is_last and last_of_all))
                    return self.PE.n

                def emit_exp(g, pe_ms):
                    pw = self.pbig[g % 2]
                    P.wait("act", self.PE, pe_ms)
                    P.wait("act", self.PE, av_ms.get(("pt", g % 4), 0))
                    a0 = P.act(lambda e, pw=pw, g=g: e.activation(
                        out=self.big_b[:, (g % 4) * 1024:(g % 4) * 1024 + 512],
                        in_=pw[:, 0:512], func=AF.Exp, scale=0.125, bias=self.neg8_col))
                    a1 = P.act(lambda e, pw=pw, g=g: e.activation(
                        out=self.big_b[:, (g % 4) * 1024 + 512:(g % 4) * 1024 + 1024],
                        in_=pw[:, 512:1024], func=AF.Exp, scale=0.125, bias=self.neg8_col))
                    exp_ms[g % 2] = self.ACT.n
                    return (a0, a1)

                def emit_av(g, act_ms):
                    kts = (2 * g, 2 * g + 1)
                    if oput[0] == 0:
                        P.wait("pe", self.DVE, fixup_ms[o])
                    for hi in range(2):
                        P.wait("pe", self.ACT, act_ms[hi])
                        ob = obs[hi]
                        for ki, kt in enumerate(kts):
                            for qi in range(2):
                                if 2 * qh + qi < kt:
                                    continue
                                first = (oput[hi] == 0)
                                oput[hi] += 1
                                last = (oput[hi] == n_out[hi])
                                P.pe(lambda e, hi=hi, g=g, ki=ki, kt=kt, qi=qi, ob=ob,
                                     vst=vst, first=first, last=last: e.matmul(
                                    ob[:, qi * 129:(qi + 1) * 129],
                                    self.big_b[:, (g % 4) * 1024 + hi * 512 + ki * 256 + qi * 128:
                                               (g % 4) * 1024 + hi * 512 + ki * 256 + (qi + 1) * 128],
                                    vst[:, kt * 129:(kt + 1) * 129],
                                    start=first, stop=last), inc=last)
                    av_ms[("pt", g % 4)] = self.PE.n
                    return self.PE.n

                G = ngroups
                sc_ms = {0: emit_scores(0)}
                if G > 1:
                    sc_ms[1] = emit_scores(1)
                e_ms = {0: emit_exp(0, sc_ms[0])}
                pe_av = 0
                for g in range(G):
                    if g + 1 < G:
                        e_ms[g + 1] = emit_exp(g + 1, sc_ms[g + 1])
                    pe_av = emit_av(g, e_ms[g])
                    if g + 2 < G:
                        sc_ms[g + 2] = emit_scores(g + 2)
                P.wait("dve", self.PE, pe_av)
                if first_fix_of_pair and store_ms is not None:
                    P.wait("dve", store_ms[0], store_ms[1])
                    first_fix_of_pair = False
                ob1, ob2 = obs
                st = self.stats
                P.dve(lambda e, ob1=ob1: e.reciprocal(
                    st[:, 184:186], ob1[:, 0:258].rearrange("p (a c) -> p a c", c=129)[:, :, 128:129]))
                P.dve(lambda e, ob2=ob2: e.reciprocal(
                    st[:, 186:188], ob2[:, 0:258].rearrange("p (a c) -> p a c", c=129)[:, :, 128:129]))
                P.dve(lambda e: e.tensor_scalar(st[:, 188:190], st[:, 186:188],
                                                self.neglam_col, None, ALU.mult))
                for qi in range(2):
                    qt = 2 * qh + qi
                    zraw = self.zstrip[:, 2048 + qt * 128: 2048 + (qt + 1) * 128]
                    P.dve(lambda e, ob1=ob1, qi=qi: e.tensor_scalar(
                        self.scr2[:, qi * 128:(qi + 1) * 128],
                        ob1[:, qi * 129:qi * 129 + 128], st[:, 184 + qi:185 + qi], None, ALU.mult))
                    P.dve(lambda e, ob2=ob2, qi=qi, zraw=zraw: e.scalar_tensor_tensor(
                        zraw, ob2[:, qi * 129:qi * 129 + 128], st[:, 188 + qi:189 + qi],
                        self.scr2[:, qi * 128:(qi + 1) * 128], ALU.mult, ALU.add))
                    P.dve(lambda e, zraw=zraw: e.bn_stats(st[:, 136:142], zraw))
                    P.dve(lambda e, qt=qt: e.bn_aggr(st[:, 192 + 2 * qt:194 + 2 * qt], st[:, 136:142]))
                fixup_ms[o] = self.DVE.n
            st = self.stats
            for qt in range(NT):
                zraw = self.zstrip[:, 2048 + qt * 128: 2048 + (qt + 1) * 128]
                P.dve(lambda e, zraw=zraw: e.bn_stats(st[:, 136:142], zraw))
                P.dve(lambda e, qt=qt: e.bn_aggr(st[:, 192 + 2 * qt:194 + 2 * qt], st[:, 136:142]))
            dv = self.DVE.n
            P.wait("act", self.DVE, dv)
            varv = self.stats[:, 192:224].rearrange("p (a b) -> p a b", b=2)[:, :, 1:2]
            P.act(lambda e, varv=varv: e.activation(
                out=self.stats[:, 136:152], in_=varv, func=AF.Ln, bias=self.lneps_col))
            av2 = P.act(lambda e: e.activation(
                out=self.stats[:, 224:240], in_=self.stats[:, 136:152],
                func=AF.Exp, scale=-0.5, bias=self.lnomli_col))
            P.wait("dve", self.ACT, av2)
            for qt in range(NT):
                P.dve(lambda e, qt=qt: e.tensor_scalar(
                    self.zstrip[:, qt * 128:(qt + 1) * 128],
                    self.zstrip[:, 2048 + qt * 128: 2048 + (qt + 1) * 128],
                    self.stats[:, 192 + 2 * qt:193 + 2 * qt],
                    self.stats[:, 224 + qt:225 + qt], ALU.subtract, ALU.mult))
            pe_after_pair[p] = self.PE.n
            P.wait("sp", self.DVE, self.DVE.n)
            store_ms = P.dma("sp", self.zhat_d.rearrange("(h t p) d -> h p t d", h=NH, p=128)[p],
                             self.zstrip[:, 0:2048].rearrange("p (t d) -> p t d", d=128))

    def emit_attn_out(self, b, fuse_ln=None):
        P = self
        self.tapd(f"zhatd_{b}", self.zhat_d[:, :], [NH * S, 128], BF16)
        P.wait_q("sp", "sp")
        for c in range(NC):
            P.dmat("sp", self.big_b[:, c * S:(c + 1) * S],
                   self.zhat_d.rearrange("(h t p) d -> h (t p) d", h=NH, p=128)[c])
        P.dma("gps", self.wselg[:, 0:8192].rearrange("p (c f) -> p c f", f=D), self.w["attn_wpT"].rearrange("(c p) f -> p c f", p=128))
        if getattr(self, "_prefetch_ffn_w", False):
            P.wait("gps", self.PE, self.PE.n)
            P.dma("gps", self.qk_strm[:, 0:8192].rearrange("p (c f) -> p c f", f=D),
                  self.w["ffn_wqT"].rearrange("(c p) f -> p c f", p=128))
        P.wait_q("pe", "sp")
        P.wait_q("pe", "gps")
        wp = self.wselg[:, 0:8192]
        res_ms = [0, 0]
        for tt in range(NT):
            b0, b1 = self.pb[0 + 2 * (tt % 2)], self.pb[1 + 2 * (tt % 2)]
            P.wait("pe", self.DVE, res_ms[tt % 2])
            for fc in range(NC):
                for half, bank in ((0, b0), (1, b1)):
                    P.pe(lambda e, tt=tt, fc=fc, half=half, bank=bank: e.matmul(
                        bank[:, 0:512],
                        self.big_b[:, fc * S + tt * 128:fc * S + (tt + 1) * 128],
                        wp[:, fc * D + half * 512:fc * D + (half + 1) * 512],
                        start=(fc == 0), stop=(fc == NC - 1)),
                        inc=(fc == NC - 1 and half == 1))
            P.wait("dve", self.PE, self.PE.n)
            for half, bank in ((0, b0), (1, b1)):
                xsl = self.x_res[:, tt * D + half * 512: tt * D + (half + 1) * 512]
                P.dve(lambda e, bank=bank, xsl=xsl: e.tensor_tensor(
                    xsl, xsl, bank[:, 0:512], ALU.add))
            res_ms[tt % 2] = self.DVE.n
            if fuse_ln is not None:
                self._ln_tile(tt, with_router=(fuse_ln == "router"))
        if fuse_ln is not None:
            self._ln_finish(with_router=(fuse_ln == "router"))

    def emit_attn(self, b, skip_ln=False, next_ln=None, w_prefetched=False):
        if skip_ln:
            self._ln_transposes()
        else:
            self.barrier()
            self.emit_ln(b, with_router=False, tag=f"attn_{b}")
        P = self
        P.wait("pe", self.DVE, self.DVE.n)
        P.wait_q("pe", "act")
        self.emit_attn_qkv(b, w_prefetched=w_prefetched)
        P.wait_q("sp", "sp")
        P.wait("dve", self.GPS, self.GPS.n)
        self.emit_attn_core(b)
        P.wait_q("sp", "sp")
        P.wait("sp", self.PE, self.PE.n)
        P.wait("act", self.PE, self.PE.n)
        self.emit_attn_out(b, fuse_ln=next_ln)
        self.barrier()

    def emit_all(self, phases=None):
        import os
        if phases is None:
            phases = int(os.environ.get("MK_PHASES", "99"))
        self.emit_setup()
        for b in range(NB):
            self.barrier()
            self.emit_load_x(b)
            self.barrier()
            if phases >= 1:
                self.emit_ln(b, with_router=True, tag=f"ffn0_{b}")
                self.dma("gps", self.big_b[:, 0:NC * D].rearrange("p (c f) -> p c f", f=D),
                         self.w["ffn_wqT"].rearrange("(c p) f -> p c f", p=128))
            if phases >= 2:
                self.emit_router(b, 0)
            if phases >= 3:
                self.wait_q("pe", "gps")
                self.emit_wfold(0, "ffn_bq", 0)
                self.wait("gps", self.PE, self.PE.n)
                self.dma("gps", self.big_b[:, 0:NC * D].rearrange("p (c f) -> p c f", f=D),
                         self.w["ffn_wkT"].rearrange("(c p) f -> p c f", p=128))
                self.wait_q("pe", "gps")
                self.emit_wfold(1, "ffn_bk", 2)
                self.wait("gps", self.PE, self.PE.n)
                self.dma("gps", self.big_b[:, 0:NC * D].rearrange("p (c f) -> p c f", f=D),
                         self.w["ffn_wvT"].rearrange("(c p) f -> p c f", p=128))
                self.wait_q("pe", "gps")
                self.emit_wfold(2, "ffn_bv", 4)
                self.tap(f"wsel_{b}_0", self.wsel[:, :])
            if phases >= 4:
                self.barrier()
                self.emit_ffn_qkv(b, 0)
            if phases >= 5:
                self.barrier()
                self.emit_ffn_attn(b, 0)
            if phases >= 6:
                self.barrier()
                self._prefetch_attn_w = (phases >= 7)
                self.emit_ffn_out(b, 0, fuse_ln=("plain" if phases >= 7 else None))
                self._prefetch_attn_w = False
                self.barrier()
            if phases >= 7:
                self._prefetch_ffn_w = (phases >= 8)
                self.emit_attn(b, skip_ln=True,
                               next_ln=("router" if phases >= 8 else None),
                               w_prefetched=True)
                self._prefetch_ffn_w = False
            if phases >= 8:
                self.emit_ffn(b, 1, skip_ln=True, wq_prefetched=True)
            self.emit_store_y(b)
            if phases < 99:
                break
        self.barrier()

    def finalize(self):
        nc = self.nc
        with nc.Block() as block:
            @block.tensor
            def _(t):
                for fn in self.q["pe"]:
                    fn(t)

            @block.scalar
            def _(s):
                for fn in self.q["act"]:
                    fn(s)

            @block.vector
            def _(v):
                for fn in self.q["dve"]:
                    fn(v)

            @block.gpsimd
            def _(g):
                for fn in self.q["gps"]:
                    fn(g)

            @block.sync
            def _(sy):
                for fn in self.q["sp"]:
                    fn(sy)
        return nc


_PROG_CACHE = {}


def build(taps=()):
    key = tuple(sorted(taps))
    if key in _PROG_CACHE:
        return _PROG_CACHE[key]
    mk = MK(taps)
    mk.emit_all()
    nc = mk.finalize()
    _PROG_CACHE[key] = (nc, mk)
    return nc, mk


def make_inputs(inputs, core):
    x = np.ascontiguousarray(np.asarray(inputs["L0_attn_x"])[2 * core:2 * core + 2].reshape(NB * S, D))
    t = lambda a: np.ascontiguousarray(np.asarray(a).T)
    m = {
        "x": x,
        "attn_wqT": t(inputs["L0_attn_Wq"]),
        "attn_wkT": t(inputs["L0_attn_Wk"]),
        "attn_wvT": t(inputs["L0_attn_Wv"]),
        "attn_wpT": t(inputs["L0_attn_Wproj"]),
        "ffn_wqT": t(inputs["L0_ffn_Wq"]),
        "ffn_wkT": t(inputs["L0_ffn_Wk"]),
        "ffn_wvT": t(inputs["L0_ffn_Wv"]),
        "ffn_wr": np.asarray(inputs["L0_ffn_Wr"]),
        "ffn_br": np.asarray(inputs["L0_ffn_br"]),
        "ffn_bq": np.asarray(inputs["L0_ffn_bq"]),
        "ffn_bk": np.asarray(inputs["L0_ffn_bk"]),
        "ffn_bv": np.asarray(inputs["L0_ffn_bv"]),
        "ffn_wo": np.asarray(inputs["L0_ffn_Wo"]),
        "ffn_bo": np.asarray(inputs["L0_ffn_bo"]),
        "lq1": np.asarray(inputs["L0_attn_lq1"]),
        "lk1": np.asarray(inputs["L0_attn_lk1"]),
        "lq2": np.asarray(inputs["L0_attn_lq2"]),
        "lk2": np.asarray(inputs["L0_attn_lk2"]),
    }
    return {k: np.ascontiguousarray(np.asarray(v, dtype=np.float32)) for k, v in m.items()}


def kernel(**inputs):
    nc, mk = build()
    in_maps = [make_inputs(inputs, c) for c in range(8)]
    res = run_bass_kernel_spmd(nc, in_maps, core_ids=list(range(8)))
    out = np.stack([r["y"].reshape(NB, S, D) for r in res.results])
    return out.reshape(16, S, D).astype(np.float32)


# version 16
# speedup vs baseline: 1.1777x; 1.1777x over previous
import math
import numpy as np
import ml_dtypes
import concourse.bass as bass
import concourse.mybir as mybir
from concourse.bass_utils import run_bass_kernel_spmd

F32 = mybir.dt.float32
BF16 = mybir.dt.bfloat16
AF = mybir.ActivationFunctionType
ALU = mybir.AluOpType
AX = mybir.AxisListType

S = 2048
D = 1024
NT = 16
NC = 8
NB = 2
HD = 64
NH = 8
FK = 4
FDH = 64
LN_EPS = 1e-5
RMS_EPS = float(np.finfo(np.float32).eps)
LAMBDA_INIT = 0.8 - 0.6 * math.exp(-0.3 * 4)
ONE_MINUS_LI = 1.0 - LAMBDA_INIT
NEG_BIG = -30000.0


def _rope_tables():
    dim = HD
    inv_freq = 1.0 / 10000.0 ** (np.arange(0, dim, 2, dtype=np.float32) / dim)
    t = np.arange(S, dtype=np.float32)
    freqs = np.outer(t, inv_freq)
    return np.cos(freqs).astype(np.float32), np.sin(freqs).astype(np.float32)


def _consts():
    c = {}
    cos, sin = _rope_tables()
    c["rope_cos"] = cos
    c["rope_sin"] = sin
    c["ident_bf"] = np.eye(128, dtype=np.float32).astype(ml_dtypes.bfloat16)
    c["ident_f32"] = np.eye(128, dtype=np.float32)
    m = np.zeros((128, 128), dtype=np.float32)
    for k in range(128):
        m[k, :k] = NEG_BIG
    c["diag_bf"] = m.astype(ml_dtypes.bfloat16)
    e = np.zeros((16, NC * 128), dtype=np.float32)
    for cc in range(NC):
        for p in range(128):
            e[2 * cc + p // 64, cc * 128 + p] = 1.0
    c["e64_bf"] = e.astype(ml_dtypes.bfloat16)
    ik = np.zeros((128, FK * FDH), dtype=np.float32)
    for p in range(128):
        for k in range(FK):
            ik[p, k * FDH + (p % 64)] = 1.0
    c["ikron_bf"] = ik.astype(ml_dtypes.bfloat16)
    c["kcols"] = np.tile(np.arange(FK, dtype=np.float32)[None, :], (16, 1))
    lt = np.zeros((16, 16), dtype=np.float32)
    for p in range(16):
        lt[p, p + 1:] = 1.0
    c["lt16"] = lt
    return c


class Sem:
    def __init__(self, handle):
        self.h = handle
        self.n = 0

    def inc(self, k=1):
        self.n += k
        return self.n


class MK:
    ENGINES = ("pe", "act", "dve", "gps", "sp")

    def __init__(self, taps=()):
        self.taps = set(taps)
        nc = self.nc = bass.Bass("TRN2", target_bir_lowering=False)
        self.q = {e: [] for e in self.ENGINES}
        self.tap_shapes = {}
        C = _consts()

        self.x_io = nc.dram_tensor("x", [NB * S, D], F32, kind="ExternalInput").ap()
        self.y_io = nc.dram_tensor("y", [NB * S, D], F32, kind="ExternalOutput").ap()
        self.w = {}
        for name, shape in [
            ("attn_wqT", [D, D]), ("attn_wkT", [D, D]), ("attn_wvT", [D, D]),
            ("attn_wpT", [D, D]),
            ("ffn_wqT", [D, D]), ("ffn_wkT", [D, D]), ("ffn_wvT", [D, D]),
            ("ffn_wr", [D, 16]), ("ffn_br", [16]),
            ("ffn_bq", [D]), ("ffn_bk", [D]), ("ffn_bv", [D]),
            ("ffn_wo", [256, D]), ("ffn_bo", [D]),
            ("lq1", [HD]), ("lk1", [HD]), ("lq2", [HD]), ("lk2", [HD]),
        ]:
            self.w[name] = nc.dram_tensor(name, shape, F32, kind="ExternalInput").ap()
        self.cst = {k: nc.inline_tensor(np.asarray(v), f"c_{k}").ap() for k, v in C.items()}

        self.lnx_d = nc.dram_tensor("lnx_d", [S, D], BF16, kind="Internal").ap()
        self.qhat_d = nc.dram_tensor("qhat_d", [S, D], BF16, kind="Internal").ap()
        self.khat_d = nc.dram_tensor("khat_d", [S, D], BF16, kind="Internal").ap()
        self.vhat_d = nc.dram_tensor("vhat_d", [S, NH * 129], BF16, kind="Internal").ap()
        self.zhat_d = nc.dram_tensor("zhat_d", [NH * S, 128], BF16, kind="Internal").ap()
        self.zffn_d = nc.dram_tensor("zffn_d", [S, 256], BF16, kind="Internal").ap()

        sbuf = lambda n, s, d: nc.alloc_sbuf_tensor(n, list(s), d).ap()
        self.x_res = sbuf("x_res", [128, NT * D], F32)
        self.reg_a = sbuf("reg_a", [128, NC * S], BF16)
        self.big_b = sbuf("big_b", [128, NC * S], BF16)
        self.qk_strm = sbuf("qk_strm", [128, 8192], BF16)
        self.wselg = sbuf("wselg", [128, 8192], BF16)
        self.vstrips = sbuf("vstrips", [128, 4160], BF16)
        self.vprod = sbuf("vprod", [128, 1032], BF16)
        self.scr = sbuf("scr", [128, D], F32)
        self.scr2 = sbuf("scr2", [128, D], F32)
        self.qk_bf = sbuf("qk_bf", [128, D], BF16)
        self.pt = sbuf("pt", [128, 2048], BF16)
        self.zstrip = sbuf("zstrip", [128, NT * 256], BF16)
        self.stats = sbuf("stats", [128, 304], F32)
        self.rowbuf = sbuf("rowbuf", [128, 16], F32)
        self.cosb_bf = sbuf("cosb_bf", [128, NT * 32], BF16)
        self.sinb_bf = sbuf("sinb_bf", [128, NT * 32], BF16)
        self.cbf = sbuf("cbf", [128, 648], BF16)
        self.c32 = sbuf("c32", [128, 272], F32)
        self.e64 = sbuf("e64", [16, NC * 128], BF16)
        self.sm = sbuf("sm", [16, 384], F32)
        self.rows = sbuf("rows", [1, 420], F32)
        self.rowsb = sbuf("rowsb", [1, 1024], BF16)

        self.pwide = [nc.alloc_psum_tensor(f"pw{i}", [128, 1024], F32).ap() for i in range(4)]
        self.pb = [self.pwide[i // 2][:, (i % 2) * 512:(i % 2) * 512 + 512] for i in range(8)]
        self.pbig = [self.pwide[2], self.pwide[3]]

        self.qsem = {
            "sp": [Sem(nc.alloc_semaphore(f"d_sp{i}")) for i in range(8)],
            "gps": [Sem(nc.alloc_semaphore(f"d_gp{i}")) for i in range(4)],
            "act": [Sem(nc.alloc_semaphore(f"d_ac{i}")) for i in range(4)],
        }
        self.qrr = {"sp": 0, "gps": 0, "act": 0}
        self.PE = Sem(nc.alloc_semaphore("s_pe"))
        self.ACT = Sem(nc.alloc_semaphore("s_act"))
        self.DVE = Sem(nc.alloc_semaphore("s_dve"))
        self.GPS = Sem(nc.alloc_semaphore("s_gps"))
        self.ALLSEM = ([self.PE, self.ACT, self.DVE, self.GPS]
                       + self.qsem["sp"] + self.qsem["gps"] + self.qsem["act"])

        self.ident_bf = self.cbf[:, 0:128]
        self.diag_bf = self.cbf[:, 128:256]
        self.ikron = self.cbf[:, 256:512]
        self.ones_row_bf = self.cbf[0:1, 516:644]
        self.ident_f32 = self.c32[:, 0:128]
        self.inv2048_col = self.c32[:, 128:129]
        self.lneps_col = self.c32[:, 129:130]
        self.rmseps_col = self.c32[:, 130:131]
        self.one_col_f32 = self.c32[:, 131:132]
        self.neg8_col = self.c32[:, 132:133]
        self.neglam_col = self.c32[:, 133:134]
        self.lnomli_col = self.c32[:, 134:135]
        self.lnzero_col = self.c32[:, 135:136]
        self.ones_row_f32 = self.c32[0:1, 140:268]

        self.wsel = self.wselg[:, 0:6144]
        self.gsel = self.wselg[:, 6144:8192]
        self.qsel = self.qk_strm[:, 0:4096]
        self.ksel = self.qk_strm[:, 4096:8192]

    def on(self, eng, fn):
        self.q[eng].append(fn)

    def wait(self, eng, sem, val):
        if val > 0:
            self.q[eng].append(lambda e, s=sem, v=val: e.wait_ge(s.h, v))

    def barrier_into(self, eng):
        for s in self.ALLSEM:
            self.wait(eng, s, s.n)

    def barrier(self):
        for e in self.ENGINES:
            self.barrier_into(e)

    def _chan(self, queue):
        chans = self.qsem[queue]
        ch = chans[self.qrr[queue] % len(chans)]
        self.qrr[queue] += 1
        return ch

    def dma(self, queue, out, in_, **kw):
        ch = self._chan(queue)
        v0 = ch.n
        self.on(queue, lambda e, o=out, i=in_, s=ch, w=v0, kw=kw: (
            e.wait_ge(s.h, w), e.dma_start(out=o, in_=i, **kw).then_inc(s.h, 16)))
        return ch, ch.inc(16)

    def dmat(self, queue, out, in_):
        ch = self._chan(queue)
        v0 = ch.n
        self.on(queue, lambda e, o=out, i=in_, s=ch, w=v0: (
            e.wait_ge(s.h, w), e.dma_start_transpose(o, i).then_inc(s.h, 16)))
        return ch, ch.inc(16)

    def wait_q(self, eng, queue):
        for ch in self.qsem[queue]:
            self.wait(eng, ch, ch.n)

    def _mk(self, eng, sem, fn, inc):
        w = sem.n

        def step(e, f=fn, s=sem, w=w):
            e.wait_ge(s.h, w)
            f(e).then_inc(s.h, 1)
        self.on(eng, step)
        return sem.inc(1)

    def dve(self, fn, inc=True):
        return self._mk("dve", self.DVE, fn, inc)

    def gps(self, fn, inc=True):
        return self._mk("gps", self.GPS, fn, inc)

    def act(self, fn, inc=True):
        return self._mk("act", self.ACT, fn, inc)

    def pe(self, fn, inc=True):
        if inc:
            self.on("pe", lambda e, f=fn, s=self.PE: f(e).then_inc(s.h, 1))
            return self.PE.inc(1)
        self.on("pe", fn)
        return self.PE.n

    def tap(self, name, ap, dtype=None):
        if name not in self.taps:
            return
        shape = [ap.shape[0], int(np.prod(ap.shape[1:]))]
        dtype = dtype or ap.dtype
        o = self.nc.dram_tensor(f"tap_{name}", shape, dtype, kind="ExternalOutput").ap()
        self.tap_shapes[f"tap_{name}"] = (shape, dtype)
        self.barrier_into("act")
        self.dma("act", o[:, :], ap)

    def tapd(self, name, dram_ap, shape, dtype):
        if name not in self.taps:
            return
        o = self.nc.dram_tensor(f"tap_{name}", list(shape), dtype, kind="ExternalOutput").ap()
        self.tap_shapes[f"tap_{name}"] = (list(shape), dtype)
        self.barrier_into("act")
        self.dma("act", o[:, :], dram_ap)

    def x_tile(self, tt):
        return self.x_res[:, tt * D:(tt + 1) * D]

    def emit_setup(self):
        P, w, cst = self, self.w, self.cst
        P.dma("sp", self.cbf[:, 0:128], cst["ident_bf"][:, :])
        P.dma("sp", self.cbf[:, 128:256], cst["diag_bf"][:, :])
        P.dma("sp", self.cbf[:, 256:512], cst["ikron_bf"][:, :])
        P.dma("sp", self.e64[:, :], cst["e64_bf"][:, :])
        P.dma("sp", self.c32[:, 0:128], cst["ident_f32"][:, :])
        P.dma("gps", self.cosb_bf.rearrange("p (t j) -> p t j", j=32),
              cst["rope_cos"].rearrange("(t p) j -> p t j", p=128))
        P.dma("gps", self.sinb_bf.rearrange("p (t j) -> p t j", j=32),
              cst["rope_sin"].rearrange("(t p) j -> p t j", p=128))
        P.dma("sp", self.sm[:, 0:4], cst["kcols"][:, :])
        P.dma("sp", self.sm[:, 4:20], cst["lt16"][:, :])
        P.dma("sp", self.stats[:, 0:128].rearrange("p (c j) -> p c j", j=16),
              w["ffn_wr"].rearrange("(c p) j -> p c j", p=128))
        self.wr_sb = self.stats[:, 0:128]
        P.dma("sp", self.rows[0:1, 0:64], w["lq1"][None, :])
        P.dma("sp", self.rows[0:1, 64:128], w["lk1"][None, :])
        P.dma("sp", self.rows[0:1, 128:192], w["lq2"][None, :])
        P.dma("sp", self.rows[0:1, 192:256], w["lk2"][None, :])
        P.dma("sp", self.rows[0:1, 256:272], w["ffn_br"][None, :])
        self.wait_q("dve", "sp")
        for ap, v in [
            (self.cbf[:, 512:516], 1.0), (self.cbf[0:1, 516:644], 1.0),
            (self.c32[:, 128:129], 1.0 / 2048.0), (self.c32[:, 129:130], LN_EPS),
            (self.c32[:, 130:131], RMS_EPS), (self.c32[:, 131:132], 1.0),
            (self.c32[:, 132:133], -8.0), (self.c32[0:1, 140:268], 1.0),
            (self.c32[:, 134:135], math.log(ONE_MINUS_LI)), (self.c32[:, 135:136], 0.0),
        ]:
            self.dve(lambda e, a=ap, vv=v: e.memset(a, vv))
        R = self.rows
        self.dve(lambda e: e.tensor_mul(R[0:1, 280:344], R[0:1, 0:64], R[0:1, 64:128]), inc=False)
        self.dve(lambda e: e.tensor_mul(R[0:1, 344:408], R[0:1, 128:192], R[0:1, 192:256]), inc=False)
        self.dve(lambda e: e.reduce_sum(R[0:1, 408:409], R[0:1, 280:344], axis=AX.X), inc=False)
        dv = self.dve(lambda e: e.reduce_sum(R[0:1, 409:410], R[0:1, 344:408], axis=AX.X))
        self.wait("act", self.DVE, dv)
        av = self.act(lambda e: e.activation(out=R[0:1, 412:414], in_=R[0:1, 408:410], func=AF.Exp))
        self.wait("dve", self.ACT, av)
        self.dve(lambda e: e.tensor_sub(R[0:1, 414:415], R[0:1, 412:413], R[0:1, 413:414]), inc=False)
        dv = self.dve(lambda e: e.tensor_scalar(R[0:1, 415:416], R[0:1, 414:415], LAMBDA_INIT, -1.0, ALU.add, ALU.mult))
        self.wait("pe", self.DVE, dv)
        pv = self.pe(lambda e: e.matmul(self.pb[0][:, 0:1], self.ones_row_f32, R[0:1, 415:416], start=True, stop=True))
        self.wait("dve", self.PE, pv)
        dv = self.dve(lambda e: e.tensor_copy(self.c32[:, 133:134], self.pb[0][:, 0:1]))
        self.wait("pe", self.DVE, dv)
        self.tap("lam", self.c32[:, 133:134])

    def emit_load_x(self, b):
        xv = self.x_io.rearrange("(b t p) f -> b p t f", b=NB, p=128)
        return self.dma("sp", self.x_res.rearrange("p (t f) -> p t f", f=D), xv[b])

    def emit_store_y(self, b):
        yv = self.y_io.rearrange("(b t p) f -> b p t f", b=NB, p=128)
        self.barrier_into("sp")
        return self.dma("sp", yv[b], self.x_res.rearrange("p (t f) -> p t f", f=D))

    def _ln_tile(self, tt, with_router):
        P = self
        st = self.stats
        xt = self.x_tile(tt)
        P.dve(lambda e, xt=xt, tt=tt: e.reduce_sum(st[:, 192 + 2 * tt:193 + 2 * tt], xt, axis=AX.X))
        P.wait("act", self.DVE, self.DVE.n)
        P.act(lambda e, xt=xt, tt=tt: e.activation(
            out=self.scr2[:, 0:D], in_=xt, func=AF.Square,
            accum_out=st[:, 193 + 2 * tt:194 + 2 * tt]))

    def _ln_finish(self, with_router):
        P = self
        st = self.stats
        P.wait("dve", self.ACT, self.ACT.n)
        pairs = st[:, 192:224].rearrange("p (a b) -> p a b", b=2)
        mu = st[:, 224:240]
        m2 = st[:, 240:256]
        P.dve(lambda e: e.tensor_scalar(mu, pairs[:, :, 0:1], 1.0 / D, None, ALU.mult), inc=False)
        P.dve(lambda e: e.tensor_scalar(m2, pairs[:, :, 1:2], 1.0 / D, None, ALU.mult), inc=False)
        P.dve(lambda e: e.tensor_mul(st[:, 136:152], mu, mu), inc=False)
        dv = P.dve(lambda e: e.tensor_sub(m2, m2, st[:, 136:152]))
        P.wait("act", self.DVE, dv)
        P.act(lambda e: e.activation(out=st[:, 136:152], in_=m2, func=AF.Ln, bias=self.lneps_col))
        av = P.act(lambda e: e.activation(out=m2, in_=st[:, 136:152], func=AF.Exp,
                                          scale=-0.5, bias=self.lnzero_col))
        P.wait("dve", self.ACT, av)
        rstd = m2
        st_ms = [None, None]
        for tt in range(NT):
            xt = self.x_tile(tt)
            slot = (self.qk_bf if tt % 2 == 0 else self.pt)[:, 0:D]
            if st_ms[tt % 2] is not None:
                P.wait("dve", st_ms[tt % 2][0], st_ms[tt % 2][1])
            if with_router:
                P.dve(lambda e, xt=xt, tt=tt: e.tensor_scalar(
                    self.scr[:, 0:D], xt, mu[:, tt:tt + 1], rstd[:, tt:tt + 1],
                    ALU.subtract, ALU.mult), inc=False)
                dv = P.dve(lambda e, slot=slot: e.tensor_copy(slot, self.scr[:, 0:D]))
                P.wait("pe", self.DVE, dv)
                P.pe(lambda e, tt=tt: e.matmul(self.pb[6][0:1, 0:512], self.inv2048_col,
                                               self.scr[:, 0:512], start=(tt == 0), stop=(tt == NT - 1)), inc=False)
                P.pe(lambda e, tt=tt: e.matmul(self.pb[7][0:1, 0:512], self.inv2048_col,
                                               self.scr[:, 512:1024], start=(tt == 0), stop=(tt == NT - 1)))
                P.wait("dve", self.PE, self.PE.n)
                P.wait("sp", self.PE, self.PE.n)
            else:
                P.dve(lambda e, xt=xt, tt=tt, slot=slot: e.tensor_scalar(
                    slot, xt, mu[:, tt:tt + 1], rstd[:, tt:tt + 1],
                    ALU.subtract, ALU.mult))
            P.wait("act", self.DVE, self.DVE.n)
            st_ms[tt % 2] = P.dma("act", self.lnx_d.rearrange("(t p) f -> t p f", p=128)[tt], slot)
        self._ln_transposes()

    def _ln_transposes(self):
        P = self
        P.wait_q("act", "act")
        P.wait_q("act", "sp")
        P.wait("act", self.PE, self.PE.n)
        for c in range(NC):
            P.dmat("act", self.reg_a[:, c * S:(c + 1) * S], self.lnx_d[:, c * 128:(c + 1) * 128])

    def emit_ln(self, b, with_router, tag=""):
        for tt in range(NT):
            self._ln_tile(tt, with_router)
        self._ln_finish(with_router)
        return None

    def emit_router(self, b, ci):
        P = self
        R, SM = self.rows, self.sm
        xmrow = self.scr2[0:1, 0:1024]
        P.wait("act", self.PE, self.PE.n)
        P.act(lambda e: e.copy(xmrow[0:1, 0:512], self.pb[6][0:1, 0:512]), inc=False)
        av = P.act(lambda e: e.copy(xmrow[0:1, 512:1024], self.pb[7][0:1, 0:512]))
        self.tap(f"xmean_{b}_{ci}", xmrow)
        P.wait("pe", self.ACT, av)
        for c in range(NC):
            P.pe(lambda e, c=c: e.transpose(self.pb[5][:, c:c + 1],
                                            xmrow[0:1, c * 128:(c + 1) * 128],
                                            self.ones_row_f32[0:1, 0:1]),
                 inc=(c == NC - 1))
        P.wait("dve", self.PE, self.PE.n)
        dv = P.dve(lambda e: e.tensor_copy(self.stats[:, 136:144], self.pb[5][:, 0:8]))
        P.wait("pe", self.DVE, dv)
        for c in range(NC):
            P.pe(lambda e, c=c: e.matmul(self.pb[4][0:1, 0:16], self.stats[:, 136 + c:137 + c],
                                         self.wr_sb[:, c * 16:(c + 1) * 16],
                                         start=(c == 0), stop=False), inc=False)
        pv = P.pe(lambda e: e.matmul(self.pb[4][0:1, 0:16], self.ones_row_f32[0:1, 0:1],
                                     R[0:1, 256:272], start=False, stop=True))
        P.wait("dve", self.PE, pv)
        dv = P.dve(lambda e: e.tensor_copy(SM[0:1, 128:144], self.pb[4][0:1, 0:16]))
        logits = SM[0:1, 128:144]
        self.tap(f"logits_{b}_{ci}", logits)
        P.dve(lambda e: e.reduce_max(SM[0:1, 144:145], logits, axis=AX.X), inc=False)
        dv = P.dve(lambda e: e.tensor_scalar(SM[0:1, 145:146], SM[0:1, 144:145], -1.0, None, ALU.mult))
        P.wait("act", self.DVE, dv)
        av = P.act(lambda e: e.activation(out=SM[0:1, 148:164], in_=logits, func=AF.Exp,
                                          bias=SM[0:1, 145:146], accum_out=SM[0:1, 146:147]))
        P.wait("dve", self.ACT, av)
        P.dve(lambda e: e.reciprocal(SM[0:1, 147:148], SM[0:1, 146:147]), inc=False)
        dv = P.dve(lambda e: e.tensor_scalar(SM[0:1, 164:180], SM[0:1, 148:164], SM[0:1, 147:148], None, ALU.mult))
        dist = SM[0:1, 164:180]
        self.tap(f"dist_{b}_{ci}", dist)
        P.wait("pe", self.DVE, dv)
        P.pe(lambda e: e.matmul(self.pb[4][0:16, 16:32], self.ones_row_f32[0:1, 0:16], dist,
                                start=True, stop=True), inc=False)
        pv = P.pe(lambda e: e.transpose(self.pb[4][0:16, 32:33], dist, self.ones_row_f32[0:1, 0:1]))
        P.wait("dve", self.PE, pv)
        P.dve(lambda e: e.tensor_copy(SM[:, 20:36], self.pb[4][0:16, 16:32]), inc=False)
        dv = P.dve(lambda e: e.tensor_copy(SM[:, 36:37], self.pb[4][0:16, 32:33]))
        drow, dcol = SM[:, 20:36], SM[:, 36:37]
        dcolb = dcol.broadcast_to([16, 16])
        P.dve(lambda e: e.tensor_tensor(SM[:, 37:53], dcolb, drow, ALU.is_gt), inc=False)
        P.dve(lambda e: e.tensor_tensor(SM[:, 53:69], dcolb, drow, ALU.is_equal), inc=False)
        P.dve(lambda e: e.tensor_mul(SM[:, 53:69], SM[:, 53:69], SM[:, 4:20]), inc=False)
        dv = P.dve(lambda e: e.tensor_add(SM[:, 37:53], SM[:, 37:53], SM[:, 53:69]))
        P.wait("pe", self.DVE, dv)
        pv = P.pe(lambda e: e.matmul(self.pb[4][0:16, 34:35], SM[:, 37:53], self.one_col_f32[0:16, 0:1],
                                     start=True, stop=True))
        P.wait("dve", self.PE, pv)
        dv = P.dve(lambda e: e.tensor_copy(SM[:, 69:70], self.pb[4][0:16, 34:35]))
        rankcol = SM[:, 69:70]
        self.tap(f"rank_{b}_{ci}", rankcol)
        P.dve(lambda e: e.tensor_scalar(SM[:, 70:71], rankcol, float(FK), None, ALU.is_lt), inc=False)
        dv = P.dve(lambda e: e.tensor_mul(SM[:, 71:72], dcol, SM[:, 70:71]))
        P.wait("act", self.DVE, dv)
        av = P.act(lambda e: e.activation(out=SM[:, 72:73], in_=SM[:, 71:72], func=AF.Exp))
        P.wait("pe", self.ACT, av)
        pv = P.pe(lambda e: e.matmul(self.pb[4][0:1, 36:37], SM[:, 72:73], self.one_col_f32[0:16, 0:1],
                                     start=True, stop=True))
        P.wait("dve", self.PE, pv)
        P.dve(lambda e: e.tensor_copy(SM[0:1, 180:181], self.pb[4][0:1, 36:37]), inc=False)
        dv = P.dve(lambda e: e.reciprocal(SM[0:1, 181:182], SM[0:1, 180:181]))
        P.wait("pe", self.DVE, dv)
        pv = P.pe(lambda e: e.matmul(self.pb[4][0:16, 38:39], self.ones_row_f32[0:1, 0:16],
                                     SM[0:1, 181:182], start=True, stop=True))
        P.wait("dve", self.PE, pv)
        P.dve(lambda e: e.tensor_copy(SM[:, 73:74], self.pb[4][0:16, 38:39]), inc=False)
        dv = P.dve(lambda e: e.tensor_mul(SM[:, 74:75], SM[:, 72:73], SM[:, 73:74]))
        gatescol = SM[:, 74:75]
        self.tap(f"gates_{b}_{ci}", gatescol)
        P.dve(lambda e: e.tensor_tensor(SM[:, 75:79], rankcol.broadcast_to([16, 4]), SM[:, 0:4], ALU.is_equal), inc=False)
        P.dve(lambda e: e.tensor_scalar(SM[:, 75:79], SM[:, 75:79], gatescol, None, ALU.mult), inc=False)
        dv = P.dve(lambda e: e.tensor_copy(
            self.sm.bitcast(BF16)[0:16, 512:768].rearrange("p (a b) -> p a b", b=64),
            SM[:, 75:79][:, :, None].broadcast_to([16, 4, 64])))
        bcmp256 = self.sm.bitcast(BF16)[0:16, 512:768]
        P.wait("pe", self.DVE, dv)
        for c in range(NC):
            pv = P.pe(lambda e, c=c: e.matmul(self.pb[3][:, 0:256], self.e64[:, c * 128:(c + 1) * 128],
                                              bcmp256, start=True, stop=True))
            P.wait("dve", self.PE, pv)
            dv = P.dve(lambda e, c=c: e.tensor_tensor(self.gsel[:, c * 256:(c + 1) * 256],
                                                      self.pb[3][:, 0:256], self.ikron, ALU.mult))
            P.wait("pe", self.DVE, dv)
        self.tap(f"G_{b}_{ci}", self.gsel[:, 0:2048])

    def emit_wfold(self, wi, bias_name, bias_slot, wb=None, slot=0):
        P = self
        base = wi * NC * 256
        if wb is None:
            wb = self.big_b[:, slot * NC * D:(slot + 1) * NC * D]
        ev = [0, 0]
        for fs in range(NC):
            bank = self.pb[2 + fs % 2]
            P.wait("pe", self.DVE, ev[fs % 2])
            for mc in range(NC):
                P.pe(lambda e, fs=fs, mc=mc, bank=bank: e.matmul(
                    bank[:, 0:256],
                    wb[:, mc * D + fs * 128: mc * D + (fs + 1) * 128],
                    self.gsel[:, mc * 256:(mc + 1) * 256],
                    start=(mc == 0), stop=(mc == NC - 1)), inc=(mc == NC - 1))
            P.wait("dve", self.PE, self.PE.n)
            P.dve(lambda e, fs=fs, bank=bank: e.tensor_copy(
                self.wsel[:, base + fs * 256: base + (fs + 1) * 256], bank[:, 0:256]))
            ev[fs % 2] = self.DVE.n
        P.wait("pe", self.DVE, self.DVE.n)
        brow = self.scr2[0:1, 0:1024]
        P.wait("sp", self.DVE, self.DVE.n)
        P.wait("sp", self.PE, self.PE.n)
        P.dma("sp", brow, self.w[bias_name][None, :])
        P.wait_q("pe", "sp")
        for c in range(NC):
            P.pe(lambda e, c=c: e.transpose(self.pb[2][:, c:c + 1],
                                            brow[0:1, c * 128:(c + 1) * 128],
                                            self.ones_row_f32[0:1, 0:1]), inc=(c == NC - 1))
        P.wait("dve", self.PE, self.PE.n)
        P.dve(lambda e: e.tensor_copy(self.stats[:, 144:152], self.pb[2][:, 0:8]), inc=False)
        dv = P.dve(lambda e: e.tensor_copy(self.stats.bitcast(BF16)[:, 304:312], self.stats[:, 144:152]))
        bcol_bf = self.stats.bitcast(BF16)[:, 304:312]
        P.wait("pe", self.DVE, dv)
        for h in range(2):
            for mc in range(NC):
                P.pe(lambda e, h=h, mc=mc: e.matmul(
                    self.pb[2][:, 8 + h:9 + h],
                    self.gsel[:, mc * 256 + h * 128: mc * 256 + (h + 1) * 128],
                    bcol_bf[:, mc:mc + 1],
                    start=(mc == 0), stop=(mc == NC - 1)), inc=(h == 1 and mc == NC - 1))
        P.wait("dve", self.PE, self.PE.n)
        dv = P.dve(lambda e, sl=bias_slot: e.tensor_copy(self.rowbuf[:, sl:sl + 2], self.pb[2][:, 8:10]))
        P.wait("pe", self.DVE, dv)

    def emit_ffn_qkv(self, b, ci):
        P = self
        evac_ms = [0, 0]
        P.wait("pe", self.DVE, self.DVE.n)
        for wi, dst, bslot in ((0, self.qsel, 0), (1, self.ksel, 2)):
            base = wi * NC * 256
            for h in range(2):
                for qb in range(4):
                    bank = self.pb[qb % 2]
                    P.wait("pe", self.DVE, evac_ms[qb % 2])
                    for fc in range(NC):
                        P.pe(lambda e, h=h, qb=qb, fc=fc, bank=bank, base=base: e.matmul(
                            bank[:, 0:512],
                            self.wsel[:, base + fc * 256 + h * 128: base + fc * 256 + (h + 1) * 128],
                            self.reg_a[:, fc * S + qb * 512: fc * S + (qb + 1) * 512],
                            start=(fc == 0), stop=(fc == NC - 1)), inc=(fc == NC - 1))
                    P.wait("dve", self.PE, self.PE.n)
                    P.dve(lambda e, h=h, qb=qb, bank=bank, dst=dst, bslot=bslot: e.tensor_scalar(
                        dst[:, h * S + qb * 512:h * S + (qb + 1) * 512],
                        bank[:, 0:512], self.rowbuf[:, bslot + h:bslot + h + 1], None, ALU.add))
                    evac_ms[qb % 2] = self.DVE.n
        self.tap(f"qsel_{b}_{ci}", self.qsel[:, :])
        self.tap(f"ksel_{b}_{ci}", self.ksel[:, :])
        for h in range(2):
            P.pe(lambda e, h=h: e.transpose(self.pb[2][0:1, 16 + h * 128:16 + (h + 1) * 128],
                                            self.rowbuf[:, 4 + h:5 + h], self.ident_f32),
                 inc=(h == 1))
        P.wait("dve", self.PE, self.PE.n)
        dv = P.dve(lambda e: e.tensor_copy(self.rowsb[0:1, 0:256], self.pb[2][0:1, 16:272]))
        P.wait("pe", self.DVE, dv)
        dv = P.dve(lambda e: e.memset(self.vstrips[:, 0:4160], 1.0))
        P.wait("pe", self.DVE, dv)
        vevac_ms = [0, 0]
        for tt in range(NT):
            bank = self.pb[2 + tt % 2]
            P.wait("pe", self.DVE, vevac_ms[tt % 2])
            for fc in range(NC):
                P.pe(lambda e, tt=tt, fc=fc, bank=bank: e.matmul(
                    bank[:, 0:256],
                    self.reg_a[:, fc * S + tt * 128: fc * S + (tt + 1) * 128],
                    self.wsel[:, 2 * NC * 256 + fc * 256: 2 * NC * 256 + (fc + 1) * 256],
                    start=(fc == 0), stop=False), inc=False)
            P.pe(lambda e, bank=bank: e.matmul(bank[:, 0:256], self.ones_row_bf,
                                               self.rowsb[0:1, 0:256], start=False, stop=True))
            P.wait("dve", self.PE, self.PE.n)
            P.dve(lambda e, tt=tt, bank=bank: e.tensor_copy(
                self.vstrips.rearrange("p (h t c) -> p h t c", h=FK, t=NT)[:, :, tt, 0:64],
                bank[:, 0:256].rearrange("p (h c) -> p h c", h=FK)[:, :, :]))
            vevac_ms[tt % 2] = self.DVE.n
        self.tap(f"vffn_{b}_{ci}", self.vstrips[:, 0:4160])

    def emit_ffn_attn(self, b, ci):
        P = self
        fixup_ms = [0, 0]
        exp_ms = [0, 0]
        av_ms = {}
        for hp in range(2):
            for qh in range(8):
                o = qh % 2
                obs = (self.pb[2 * o], self.pb[2 * o + 1])
                oput = [0, 0]
                n_mm_out = 2 * 16

                def emit_scores(g):
                    pw = self.pbig[g % 2]
                    P.wait("pe", self.ACT, exp_ms[g % 2])
                    for ki, kt in enumerate((2 * g, 2 * g + 1)):
                        for hi in range(2):
                            P.pe(lambda e, hp=hp, qh=qh, kt=kt, ki=ki, hi=hi, pw=pw: e.matmul(
                                pw[:, hi * 512 + ki * 256:hi * 512 + (ki + 1) * 256],
                                self.ksel[hi * 64:(hi + 1) * 64, hp * S + kt * 128:hp * S + (kt + 1) * 128],
                                self.qsel[hi * 64:(hi + 1) * 64, hp * S + qh * 256:hp * S + (qh + 1) * 256],
                                start=(ki == 0), stop=(ki == 1),
                                tile_position=(hi * 64, 0)), inc=(ki == 1 and hi == 1))
                    return self.PE.n

                def emit_exp(g, pe_ms):
                    pw = self.pbig[g % 2]
                    P.wait("act", self.PE, pe_ms)
                    P.wait("act", self.PE, av_ms.get(g % 4, 0))
                    P.act(lambda e, pw=pw, g=g: e.activation(
                        out=self.big_b[:, (g % 4) * 1024:(g % 4) * 1024 + 1024],
                        in_=pw[:, 0:1024], func=AF.Exp, scale=0.125))
                    exp_ms[g % 2] = self.ACT.n
                    return self.ACT.n

                def emit_av(g, act_ms):
                    P.wait("pe", self.ACT, act_ms)
                    if oput[0] == 0:
                        P.wait("pe", self.DVE, fixup_ms[o])
                    for hi in range(2):
                        ob = obs[hi]
                        h = 2 * hp + hi
                        for ki in range(2):
                            kt = 2 * g + ki
                            for qi in range(2):
                                first = (oput[hi] == 0)
                                oput[hi] += 1
                                last = (oput[hi] == n_mm_out)
                                P.pe(lambda e, h=h, hi=hi, g=g, ki=ki, kt=kt, qi=qi, ob=ob,
                                     first=first, last=last: e.matmul(
                                    ob[:, qi * 65:(qi + 1) * 65],
                                    self.big_b[:, (g % 4) * 1024 + hi * 512 + ki * 256 + qi * 128:
                                               (g % 4) * 1024 + hi * 512 + ki * 256 + (qi + 1) * 128],
                                    self.vstrips[:, h * 1040 + kt * 65: h * 1040 + (kt + 1) * 65],
                                    start=first, stop=last), inc=last)
                    av_ms[g % 4] = self.PE.n
                    return self.PE.n

                sc_ms = {0: emit_scores(0), 1: emit_scores(1)}
                e_ms = {0: emit_exp(0, sc_ms[0])}
                pe_av = 0
                for g in range(8):
                    if g + 1 < 8:
                        e_ms[g + 1] = emit_exp(g + 1, sc_ms[g + 1])
                    pe_av = emit_av(g, e_ms[g])
                    if g + 2 < 8:
                        sc_ms[g + 2] = emit_scores(g + 2)
                P.wait("dve", self.PE, pe_av)
                for hi in range(2):
                    h = 2 * hp + hi
                    ob = obs[hi]
                    dn = self.stats[:, 160 + 2 * hi:162 + 2 * hi]
                    P.dve(lambda e, ob=ob, dn=dn: e.reciprocal(
                        dn, ob[:, 0:130].rearrange("p (a c) -> p a c", c=65)[:, :, 64:65]))
                    for qi in range(2):
                        qt = 2 * qh + qi
                        P.dve(lambda e, ob=ob, dn=dn, qi=qi, qt=qt, h=h: e.tensor_scalar(
                            self.zstrip[:, qt * 256 + h * 64: qt * 256 + (h + 1) * 64],
                            ob[:, qi * 65: qi * 65 + 64], dn[:, qi:qi + 1], None, ALU.mult))
                fixup_ms[o] = self.DVE.n
        self.tap(f"zffn_{b}_{ci}", self.zstrip[:, :])

    def emit_ffn_out(self, b, ci, fuse_ln=None):
        P = self
        P.wait("sp", self.DVE, self.DVE.n)
        P.dma("sp", self.zffn_d.rearrange("(t p) c -> p t c", p=128),
              self.zstrip.rearrange("p (t c) -> p t c", c=256))
        P.wait_q("sp", "sp")
        zt = self.qk_strm[:, 0:2 * S]
        P.wait("sp", self.PE, self.PE.n)
        for c in range(2):
            P.dmat("sp", zt[:, c * S:(c + 1) * S], self.zffn_d[:, c * 128:(c + 1) * 128])
        P.wait("gps", self.PE, self.PE.n)
        P.dma("gps", self.qk_strm[:, 4096:6144].rearrange("p (c f) -> p c f", f=D), self.w["ffn_wo"].rearrange("(c p) f -> p c f", p=128))
        P.dma("gps", self.rowsb[0:1, 0:1024], self.w["ffn_bo"][None, :])
        wo = self.qk_strm[:, 4096:6144]
        if getattr(self, "_prefetch_attn_w", False):
            P.wait("gps", self.PE, self.PE.n)
            P.dma("gps", self.big_b[:, 0:NC * D].rearrange("p (c f) -> p c f", f=D),
                  self.w["attn_wqT"].rearrange("(c p) f -> p c f", p=128))
            P.dma("gps", self.big_b[:, NC * D:2 * NC * D].rearrange("p (c f) -> p c f", f=D),
                  self.w["attn_wkT"].rearrange("(c p) f -> p c f", p=128))
        P.wait_q("pe", "sp")
        P.wait_q("pe", "gps")
        P.wait("pe", self.DVE, self.DVE.n)
        res_ms = [0, 0]
        for tt in range(NT):
            P.wait("pe", self.DVE, res_ms[tt % 2])
            for half in range(2):
                bank = self.pb[2 * (tt % 2) + half]
                for c in range(2):
                    P.pe(lambda e, tt=tt, half=half, c=c, bank=bank: e.matmul(
                        bank[:, 0:512],
                        zt[:, c * S + tt * 128:c * S + (tt + 1) * 128],
                        wo[:, c * 1024 + half * 512:c * 1024 + (half + 1) * 512],
                        start=(c == 0), stop=False), inc=False)
                P.pe(lambda e, half=half, bank=bank: e.matmul(
                    bank[:, 0:512], self.ones_row_bf,
                    self.rowsb[0:1, half * 512:(half + 1) * 512], start=False, stop=True),
                    inc=(half == 1))
            P.wait("dve", self.PE, self.PE.n)
            for half in range(2):
                bank = self.pb[2 * (tt % 2) + half]
                xsl = self.x_res[:, tt * D + half * 512: tt * D + (half + 1) * 512]
                P.dve(lambda e, bank=bank, xsl=xsl: e.scalar_tensor_tensor(
                    xsl, bank[:, 0:512], 0.5, xsl, ALU.mult, ALU.add))
            res_ms[tt % 2] = self.DVE.n
            if fuse_ln is not None:
                self._ln_tile(tt, with_router=(fuse_ln == "router"))
        if fuse_ln is not None:
            self._ln_finish(with_router=(fuse_ln == "router"))

    def emit_ffn(self, b, ci, skip_ln=False, next_ln=None, wq_prefetched=False):
        P = self
        if not skip_ln:
            self.barrier()
        if skip_ln:
            self._ln_transposes()
        else:
            self.emit_ln(b, with_router=True, tag=f"ffn{ci}_{b}")
        if wq_prefetched:
            P.dma("gps", self.big_b[:, 0:NC * D].rearrange("p (c f) -> p c f", f=D),
                  self.w["ffn_wkT"].rearrange("(c p) f -> p c f", p=128))
            P.dma("gps", self.big_b[:, NC * D:2 * NC * D].rearrange("p (c f) -> p c f", f=D),
                  self.w["ffn_wvT"].rearrange("(c p) f -> p c f", p=128))
            self.emit_router(b, ci)
            P.wait_q("pe", "gps")
            self.emit_wfold(0, "ffn_bq", 0, wb=self.qk_strm[:, 0:8192])
            self.emit_wfold(1, "ffn_bk", 2, slot=0)
            self.emit_wfold(2, "ffn_bv", 4, slot=1)
        else:
            P.dma("gps", self.big_b[:, 0:NC * D].rearrange("p (c f) -> p c f", f=D),
                  self.w["ffn_wqT"].rearrange("(c p) f -> p c f", p=128))
            P.dma("gps", self.big_b[:, NC * D:2 * NC * D].rearrange("p (c f) -> p c f", f=D),
                  self.w["ffn_wkT"].rearrange("(c p) f -> p c f", p=128))
            self.emit_router(b, ci)
            P.wait_q("pe", "gps")
            self.emit_wfold(0, "ffn_bq", 0, slot=0)
            P.wait("gps", self.PE, self.PE.n)
            P.dma("gps", self.big_b[:, 0:NC * D].rearrange("p (c f) -> p c f", f=D),
                  self.w["ffn_wvT"].rearrange("(c p) f -> p c f", p=128))
            self.emit_wfold(1, "ffn_bk", 2, slot=1)
            P.wait_q("pe", "gps")
            self.emit_wfold(2, "ffn_bv", 4, slot=0)
        self.tap(f"wsel_{b}_{ci}", self.wsel[:, :])
        P.wait("pe", self.DVE, self.DVE.n)
        P.wait_q("pe", "act")
        self.emit_ffn_qkv(b, ci)
        P.wait("pe", self.DVE, self.DVE.n)
        P.wait("act", self.DVE, self.DVE.n)
        self.emit_ffn_attn(b, ci)
        P.wait("pe", self.DVE, self.DVE.n)
        self.emit_ffn_out(b, ci, fuse_ln=next_ln)
        self.barrier()

    def emit_attn_qkv(self, b, w_prefetched=False):
        P = self
        self.krope_ms = [0, 0]
        if not w_prefetched:
            P.wait("gps", self.PE, self.PE.n)
            P.dma("gps", self.big_b[:, 0:NC * D].rearrange("p (c f) -> p c f", f=D),
                  self.w["attn_wqT"].rearrange("(c p) f -> p c f", p=128))
            P.dma("gps", self.big_b[:, NC * D:2 * NC * D].rearrange("p (c f) -> p c f", f=D),
                  self.w["attn_wkT"].rearrange("(c p) f -> p c f", p=128))
        P.wait_q("pe", "gps")
        P.wait("pe", self.DVE, self.DVE.n)
        wbq = self.big_b[:, 0:NC * D]
        wbk = self.big_b[:, NC * D:2 * NC * D]
        ev_q = [0, 0]
        ev_k = [0, 0]
        st_q = [None, None]
        st_k = [None, None]
        slots_q = (self.qk_bf[:, 0:D], self.pt[:, 0:D])
        slots_k = (self.pt[:, 1024:2048], self.vprod[:, 0:D])
        for tt in range(NT):
            for which, wb, pwi, ev_ms, st_ms, slots, ddst in (
                (0, wbq, tt % 2, ev_q, st_q, slots_q, self.qhat_d),
                (1, wbk, 2 + tt % 2, ev_k, st_k, slots_k, self.khat_d),
            ):
                pw = self.pwide[pwi]
                b0, b1 = pw[:, 0:512], pw[:, 512:1024]
                P.wait("pe", self.ACT, ev_ms[tt % 2])
                for fc in range(NC):
                    for half, bank in ((0, b0), (1, b1)):
                        P.pe(lambda e, tt=tt, fc=fc, half=half, bank=bank, wb=wb: e.matmul(
                            bank[:, 0:512],
                            self.reg_a[:, fc * S + tt * 128:fc * S + (tt + 1) * 128],
                            wb[:, fc * D + half * 512:fc * D + (half + 1) * 512],
                            start=(fc == 0), stop=(fc == NC - 1)),
                            inc=(fc == NC - 1 and half == 1))
                ev_ms[tt % 2] = self._emit_rms_rope(tt, pw, ddst, st_ms, which,
                                                    slots[tt % 2])
        P.wait("gps", self.PE, self.PE.n)
        P.dma("gps", self.big_b[:, 0:NC * D].rearrange("p (c f) -> p c f", f=D),
              self.w["attn_wvT"].rearrange("(c p) f -> p c f", p=128))
        P.wait_q("pe", "gps")
        P.wait("dve", self.GPS, self.GPS.n)
        P.wait_q("dve", "sp")
        P.wait("pe", self.ACT, self.ACT.n)
        wb = self.big_b[:, 0:NC * D]
        vev = [0, 0]
        vst = [None, None]
        for tt in range(NT):
            b0, b1 = self.pb[0 + 2 * (tt % 2)], self.pb[1 + 2 * (tt % 2)]
            P.wait("pe", self.DVE, vev[tt % 2])
            for fc in range(NC):
                for half, bank in ((0, b0), (1, b1)):
                    P.pe(lambda e, tt=tt, fc=fc, half=half, bank=bank, wb=wb: e.matmul(
                        bank[:, 0:512],
                        self.reg_a[:, fc * S + tt * 128:fc * S + (tt + 1) * 128],
                        wb[:, fc * D + half * 512:fc * D + (half + 1) * 512],
                        start=(fc == 0), stop=(fc == NC - 1)),
                        inc=(fc == NC - 1 and half == 1))
            P.wait("dve", self.PE, self.PE.n)
            vp = (self.vprod if tt % 2 == 0 else self.vstrips)[:, 0:1032]
            if vst[tt % 2] is not None:
                P.wait("dve", vst[tt % 2][0], vst[tt % 2][1])
            P.dve(lambda e, vp=vp: e.memset(
                vp.rearrange("p (h c) -> p h c", h=NH)[:, :, 128:129], 1.0))
            P.dve(lambda e, b0=b0, vp=vp: e.tensor_copy(
                vp.rearrange("p (h c) -> p h c", h=NH)[:, 0:4, 0:128],
                b0.rearrange("p (h c) -> p h c", h=4)[:, :, :]))
            dv = P.dve(lambda e, b1=b1, vp=vp: e.tensor_copy(
                vp.rearrange("p (h c) -> p h c", h=NH)[:, 4:8, 0:128],
                b1.rearrange("p (h c) -> p h c", h=4)[:, :, :]))
            vev[tt % 2] = dv
            P.wait("sp", self.DVE, dv)
            vst[tt % 2] = P.dma("sp", self.vhat_d.rearrange("(t p) c -> t p c", p=128)[tt], vp)
        self.tap(f"vtm_attn_{b}", self.vstrips[:, :])
        self.tapd(f"qhatd_{b}", self.qhat_d[:, :], [S, D], BF16)
        self.tapd(f"khatd_{b}", self.khat_d[:, :], [S, D], BF16)
        self.tapd(f"vhatd_{b}", self.vhat_d[:, :], [S, NH * 129], BF16)

    def _emit_rms_rope(self, tt, pw, ddst, st_ms, wi, qslot):
        P = self
        if wi == 0:
            EV, ES, ev = P.dve, self.DVE, "dve"
            qf = self.scr[:, 0:D]
            sq = self.scr2[:, 0:D]
            sb = self.scr2.bitcast(BF16)
            qbf = sb[:, 0:1024].rearrange("p (h d) -> p h d", d=64)
            t1 = sb[:, 1024:1536].rearrange("p (h d) -> p h d", d=32)
            t2 = sb[:, 1536:2048].rearrange("p (h d) -> p h d", d=32)
            ms = self.stats[:, 152:168]
            lno = self.stats[:, 136:152]
            rstd = self.stats[:, 168:184]
        else:
            EV, ES, ev = P.gps, self.GPS, "gps"
            zbuf = (self.zstrip if tt % 2 == 0 else self.vstrips)
            zf = zbuf.bitcast(F32)
            qf = zf[:, 0:D]
            sq = zf[:, 1024:2048]
            sb = zbuf
            qbf = sb[:, 2048:3072].rearrange("p (h d) -> p h d", d=64)
            t1 = sb[:, 3072:3584].rearrange("p (h d) -> p h d", d=32)
            t2 = sb[:, 3584:4096].rearrange("p (h d) -> p h d", d=32)
            ms = self.stats[:, 256:272]
            lno = self.stats[:, 272:288]
            rstd = self.stats[:, 288:304]
            P.wait("act", self.GPS, self.krope_ms[tt % 2])
        P.wait("act", self.PE, self.PE.n)
        if wi == 0:
            P.wait("act", self.DVE, self.DVE.n)
        P.act(lambda e, pw=pw, qf=qf: e.copy(qf, pw[:, 0:1024]))
        av = P.act(lambda e, pw=pw, sq=sq: e.activation(out=sq, in_=pw[:, 0:1024], func=AF.Square))
        evac_av = av
        P.wait("dve", self.ACT, av)
        P.dve(lambda e, sq=sq, ms=ms: e.reduce_sum(ms, sq.rearrange("p (h d) -> p h d", d=64), axis=AX.X), inc=False)
        dv = P.dve(lambda e, ms=ms: e.tensor_scalar(ms, ms, 1.0 / 64.0, None, ALU.mult))
        P.wait("act", self.DVE, dv)
        if wi == 1:
            P.wait("act", self.GPS, self.GPS.n)
        P.act(lambda e, ms=ms, lno=lno: e.activation(out=lno, in_=ms, func=AF.Ln,
                                                     bias=self.rmseps_col))
        av = P.act(lambda e, lno=lno, rstd=rstd: e.activation(out=rstd, in_=lno,
                                                              func=AF.Exp, scale=-0.5, bias=self.lnzero_col))
        P.wait(ev, self.ACT, av)
        qv = qf.rearrange("p (h d) -> p h d", d=64)
        EV(lambda e, qv=qv, rstd=rstd, qbf=qbf: e.tensor_tensor(
            qbf, qv, rstd[:, :, None].broadcast_to([128, 16, 64]), ALU.mult), inc=False)
        cosb = self.cosb_bf[:, tt * 32:(tt + 1) * 32][:, None, :].broadcast_to([128, 16, 32])
        sinb = self.sinb_bf[:, tt * 32:(tt + 1) * 32][:, None, :].broadcast_to([128, 16, 32])
        qb = qslot.rearrange("p (h d) -> p h d", d=64)
        q1, q2 = qbf[:, :, 0:32], qbf[:, :, 32:64]
        if st_ms[tt % 2] is not None:
            P.wait(ev, st_ms[tt % 2][0], st_ms[tt % 2][1])
        EV(lambda e: e.tensor_tensor(t1, q1, cosb, ALU.mult), inc=False)
        EV(lambda e: e.tensor_tensor(t2, q2, sinb, ALU.mult), inc=False)
        EV(lambda e: e.tensor_tensor(qb[:, :, 0:32], t1, t2, ALU.add), inc=False)
        EV(lambda e: e.tensor_tensor(t1, q2, cosb, ALU.mult), inc=False)
        EV(lambda e: e.tensor_tensor(t2, q1, sinb, ALU.mult), inc=False)
        dv = EV(lambda e: e.tensor_tensor(qb[:, :, 32:64], t1, t2, ALU.subtract))
        if wi == 1:
            self.krope_ms[tt % 2] = dv
        P.wait("sp", ES, dv)
        st_ms[tt % 2] = P.dma("sp", ddst.rearrange("(t p) f -> t p f", p=128)[tt], qslot)
        return evac_av

    def emit_attn_core(self, b):
        P = self
        def issue_pair_loads(p):
            s = p % 2
            a = P.dmat("sp", self.qk_strm[:, s * 4096: s * 4096 + 2048],
                       self.qhat_d[:, p * 128:(p + 1) * 128])
            b2 = P.dmat("sp", self.qk_strm[:, s * 4096 + 2048: s * 4096 + 4096],
                        self.khat_d[:, p * 128:(p + 1) * 128])
            c = P.dma("sp", self.vstrips[:, s * 2080: s * 2080 + 2064].rearrange(
                          "p (t c) -> p t c", c=129),
                      self.vhat_d.rearrange("(t p) c -> p t c", p=128)[:, :, p * 129:(p + 1) * 129])
            return [a, b2, c]
        ready = {0: issue_pair_loads(0)}
        pe_after_pair = {}
        fixup_ms = [0, 0]
        exp_ms = [0, 0]
        av_ms = {}
        store_ms = None
        for p in range(NH):
            s = p % 2
            if p + 1 < NH:
                P.wait("sp", self.PE, pe_after_pair.get(p - 1, 0))
                ready[p + 1] = issue_pair_loads(p + 1)
            for sem_v in ready[p]:
                P.wait("pe", sem_v[0], sem_v[1])
            qT = self.qk_strm[:, s * 4096: s * 4096 + 2048]
            kT = self.qk_strm[:, s * 4096 + 2048: s * 4096 + 4096]
            vst = self.vstrips[:, s * 2080: s * 2080 + 2064]
            first_fix_of_pair = True
            for qh in range(8):
                o = qh % 2
                obs = (self.pb[2 * o], self.pb[2 * o + 1])
                oput = [0, 0]
                nkt = 2 * qh + 2
                ngroups = nkt // 2
                n_out = [0, 0]
                for kt in range(nkt):
                    for qi in range(2):
                        if 2 * qh + qi >= kt:
                            n_out[0] += 1
                            n_out[1] += 1

                def emit_scores(g):
                    kts = (2 * g, 2 * g + 1)
                    pw = self.pbig[g % 2]
                    P.wait("pe", self.ACT, exp_ms[g % 2])
                    diag = [(ki, qi) for ki, kt in enumerate(kts)
                            for qi in range(2) if 2 * qh + qi == kt]
                    for hi in range(2):
                        last_of_all = (hi == 1)
                        for ki, kt in enumerate(kts):
                            is_last = (not diag) and (ki == 1)
                            P.pe(lambda e, kt=kt, ki=ki, hi=hi, pw=pw, qh=qh,
                                 qT=qT, kT=kT, is_last=is_last: e.matmul(
                                pw[:, hi * 512 + ki * 256:hi * 512 + (ki + 1) * 256],
                                kT[hi * 64:(hi + 1) * 64, kt * 128:(kt + 1) * 128],
                                qT[hi * 64:(hi + 1) * 64, qh * 256:(qh + 1) * 256],
                                start=(ki == 0), stop=is_last,
                                tile_position=(hi * 64, 0)),
                                inc=(is_last and last_of_all))
                        for di, (ki, qi) in enumerate(diag):
                            is_last = (di == len(diag) - 1)
                            P.pe(lambda e, pw=pw, hi=hi, ki=ki, qi=qi, is_last=is_last: e.matmul(
                                pw[:, hi * 512 + ki * 256 + qi * 128:hi * 512 + ki * 256 + (qi + 1) * 128],
                                self.ident_bf, self.diag_bf,
                                start=False, stop=is_last),
                                inc=(is_last and last_of_all))
                    return self.PE.n

                def emit_exp(g, pe_ms):
                    pw = self.pbig[g % 2]
                    P.wait("act", self.PE, pe_ms)
                    P.wait("act", self.PE, av_ms.get(("pt", g % 4), 0))
                    av = P.act(lambda e, pw=pw, g=g: e.activation(
                        out=self.big_b[:, (g % 4) * 1024:(g % 4) * 1024 + 1024],
                        in_=pw[:, 0:1024], func=AF.Exp, scale=0.125, bias=self.neg8_col))
                    exp_ms[g % 2] = self.ACT.n
                    return (av, av)

                def emit_av(g, act_ms):
                    kts = (2 * g, 2 * g + 1)
                    if oput[0] == 0:
                        P.wait("pe", self.DVE, fixup_ms[o])
                    for hi in range(2):
                        P.wait("pe", self.ACT, act_ms[hi])
                        ob = obs[hi]
                        for ki, kt in enumerate(kts):
                            for qi in range(2):
                                if 2 * qh + qi < kt:
                                    continue
                                first = (oput[hi] == 0)
                                oput[hi] += 1
                                last = (oput[hi] == n_out[hi])
                                P.pe(lambda e, hi=hi, g=g, ki=ki, kt=kt, qi=qi, ob=ob,
                                     vst=vst, first=first, last=last: e.matmul(
                                    ob[:, qi * 129:(qi + 1) * 129],
                                    self.big_b[:, (g % 4) * 1024 + hi * 512 + ki * 256 + qi * 128:
                                               (g % 4) * 1024 + hi * 512 + ki * 256 + (qi + 1) * 128],
                                    vst[:, kt * 129:(kt + 1) * 129],
                                    start=first, stop=last), inc=last)
                    av_ms[("pt", g % 4)] = self.PE.n
                    return self.PE.n

                G = ngroups
                sc_ms = {0: emit_scores(0)}
                if G > 1:
                    sc_ms[1] = emit_scores(1)
                e_ms = {0: emit_exp(0, sc_ms[0])}
                pe_av = 0
                for g in range(G):
                    if g + 1 < G:
                        e_ms[g + 1] = emit_exp(g + 1, sc_ms[g + 1])
                    pe_av = emit_av(g, e_ms[g])
                    if g + 2 < G:
                        sc_ms[g + 2] = emit_scores(g + 2)
                P.wait("dve", self.PE, pe_av)
                if first_fix_of_pair and store_ms is not None:
                    P.wait("dve", store_ms[0], store_ms[1])
                    first_fix_of_pair = False
                ob1, ob2 = obs
                st = self.stats
                P.dve(lambda e, ob1=ob1: e.reciprocal(
                    st[:, 184:186], ob1[:, 0:258].rearrange("p (a c) -> p a c", c=129)[:, :, 128:129]))
                P.dve(lambda e, ob2=ob2: e.reciprocal(
                    st[:, 186:188], ob2[:, 0:258].rearrange("p (a c) -> p a c", c=129)[:, :, 128:129]))
                P.dve(lambda e: e.tensor_scalar(st[:, 188:190], st[:, 186:188],
                                                self.neglam_col, None, ALU.mult))
                for qi in range(2):
                    qt = 2 * qh + qi
                    zraw = self.zstrip[:, 2048 + qt * 128: 2048 + (qt + 1) * 128]
                    P.dve(lambda e, ob1=ob1, qi=qi: e.tensor_scalar(
                        self.scr2[:, qi * 128:(qi + 1) * 128],
                        ob1[:, qi * 129:qi * 129 + 128], st[:, 184 + qi:185 + qi], None, ALU.mult))
                    P.dve(lambda e, ob2=ob2, qi=qi, zraw=zraw: e.scalar_tensor_tensor(
                        zraw, ob2[:, qi * 129:qi * 129 + 128], st[:, 188 + qi:189 + qi],
                        self.scr2[:, qi * 128:(qi + 1) * 128], ALU.mult, ALU.add))
                    P.dve(lambda e, zraw=zraw: e.bn_stats(st[:, 136:142], zraw))
                    P.dve(lambda e, qt=qt: e.bn_aggr(st[:, 192 + 2 * qt:194 + 2 * qt], st[:, 136:142]))
                fixup_ms[o] = self.DVE.n
            st = self.stats
            for qt in range(NT):
                zraw = self.zstrip[:, 2048 + qt * 128: 2048 + (qt + 1) * 128]
                P.dve(lambda e, zraw=zraw: e.bn_stats(st[:, 136:142], zraw))
                P.dve(lambda e, qt=qt: e.bn_aggr(st[:, 192 + 2 * qt:194 + 2 * qt], st[:, 136:142]))
            dv = self.DVE.n
            P.wait("act", self.DVE, dv)
            varv = self.stats[:, 192:224].rearrange("p (a b) -> p a b", b=2)[:, :, 1:2]
            P.act(lambda e, varv=varv: e.activation(
                out=self.stats[:, 136:152], in_=varv, func=AF.Ln, bias=self.lneps_col))
            av2 = P.act(lambda e: e.activation(
                out=self.stats[:, 224:240], in_=self.stats[:, 136:152],
                func=AF.Exp, scale=-0.5, bias=self.lnomli_col))
            P.wait("dve", self.ACT, av2)
            for qt in range(NT):
                P.dve(lambda e, qt=qt: e.tensor_scalar(
                    self.zstrip[:, qt * 128:(qt + 1) * 128],
                    self.zstrip[:, 2048 + qt * 128: 2048 + (qt + 1) * 128],
                    self.stats[:, 192 + 2 * qt:193 + 2 * qt],
                    self.stats[:, 224 + qt:225 + qt], ALU.subtract, ALU.mult))
            pe_after_pair[p] = self.PE.n
            P.wait("sp", self.DVE, self.DVE.n)
            store_ms = P.dma("sp", self.zhat_d.rearrange("(h t p) d -> h p t d", h=NH, p=128)[p],
                             self.zstrip[:, 0:2048].rearrange("p (t d) -> p t d", d=128))

    def emit_attn_out(self, b, fuse_ln=None):
        P = self
        self.tapd(f"zhatd_{b}", self.zhat_d[:, :], [NH * S, 128], BF16)
        P.wait_q("sp", "sp")
        for c in range(NC):
            P.dmat("sp", self.big_b[:, c * S:(c + 1) * S],
                   self.zhat_d.rearrange("(h t p) d -> h (t p) d", h=NH, p=128)[c])
        P.dma("gps", self.wselg[:, 0:8192].rearrange("p (c f) -> p c f", f=D), self.w["attn_wpT"].rearrange("(c p) f -> p c f", p=128))
        if getattr(self, "_prefetch_ffn_w", False):
            P.wait("gps", self.PE, self.PE.n)
            P.dma("gps", self.qk_strm[:, 0:8192].rearrange("p (c f) -> p c f", f=D),
                  self.w["ffn_wqT"].rearrange("(c p) f -> p c f", p=128))
        P.wait_q("pe", "sp")
        P.wait_q("pe", "gps")
        wp = self.wselg[:, 0:8192]
        res_ms = [0, 0]
        for tt in range(NT):
            b0, b1 = self.pb[0 + 2 * (tt % 2)], self.pb[1 + 2 * (tt % 2)]
            P.wait("pe", self.DVE, res_ms[tt % 2])
            for fc in range(NC):
                for half, bank in ((0, b0), (1, b1)):
                    P.pe(lambda e, tt=tt, fc=fc, half=half, bank=bank: e.matmul(
                        bank[:, 0:512],
                        self.big_b[:, fc * S + tt * 128:fc * S + (tt + 1) * 128],
                        wp[:, fc * D + half * 512:fc * D + (half + 1) * 512],
                        start=(fc == 0), stop=(fc == NC - 1)),
                        inc=(fc == NC - 1 and half == 1))
            P.wait("dve", self.PE, self.PE.n)
            for half, bank in ((0, b0), (1, b1)):
                xsl = self.x_res[:, tt * D + half * 512: tt * D + (half + 1) * 512]
                P.dve(lambda e, bank=bank, xsl=xsl: e.tensor_tensor(
                    xsl, xsl, bank[:, 0:512], ALU.add))
            res_ms[tt % 2] = self.DVE.n
            if fuse_ln is not None:
                self._ln_tile(tt, with_router=(fuse_ln == "router"))
        if fuse_ln is not None:
            self._ln_finish(with_router=(fuse_ln == "router"))

    def emit_attn(self, b, skip_ln=False, next_ln=None, w_prefetched=False):
        if skip_ln:
            self._ln_transposes()
        else:
            self.barrier()
            self.emit_ln(b, with_router=False, tag=f"attn_{b}")
        P = self
        P.wait("pe", self.DVE, self.DVE.n)
        P.wait_q("pe", "act")
        self.emit_attn_qkv(b, w_prefetched=w_prefetched)
        P.wait_q("sp", "sp")
        P.wait("dve", self.GPS, self.GPS.n)
        self.emit_attn_core(b)
        P.wait_q("sp", "sp")
        P.wait("sp", self.PE, self.PE.n)
        P.wait("act", self.PE, self.PE.n)
        self.emit_attn_out(b, fuse_ln=next_ln)
        self.barrier()

    def emit_all(self, phases=None):
        import os
        if phases is None:
            phases = int(os.environ.get("MK_PHASES", "99"))
        self.emit_setup()
        for b in range(NB):
            self.barrier()
            self.emit_load_x(b)
            self.barrier()
            if phases >= 1:
                self.emit_ln(b, with_router=True, tag=f"ffn0_{b}")
                self.dma("gps", self.big_b[:, 0:NC * D].rearrange("p (c f) -> p c f", f=D),
                         self.w["ffn_wqT"].rearrange("(c p) f -> p c f", p=128))
            if phases >= 2:
                self.emit_router(b, 0)
            if phases >= 3:
                self.wait_q("pe", "gps")
                self.emit_wfold(0, "ffn_bq", 0)
                self.wait("gps", self.PE, self.PE.n)
                self.dma("gps", self.big_b[:, 0:NC * D].rearrange("p (c f) -> p c f", f=D),
                         self.w["ffn_wkT"].rearrange("(c p) f -> p c f", p=128))
                self.wait_q("pe", "gps")
                self.emit_wfold(1, "ffn_bk", 2)
                self.wait("gps", self.PE, self.PE.n)
                self.dma("gps", self.big_b[:, 0:NC * D].rearrange("p (c f) -> p c f", f=D),
                         self.w["ffn_wvT"].rearrange("(c p) f -> p c f", p=128))
                self.wait_q("pe", "gps")
                self.emit_wfold(2, "ffn_bv", 4)
                self.tap(f"wsel_{b}_0", self.wsel[:, :])
            if phases >= 4:
                self.barrier()
                self.emit_ffn_qkv(b, 0)
            if phases >= 5:
                self.barrier()
                self.emit_ffn_attn(b, 0)
            if phases >= 6:
                self.barrier()
                self._prefetch_attn_w = (phases >= 7)
                self.emit_ffn_out(b, 0, fuse_ln=("plain" if phases >= 7 else None))
                self._prefetch_attn_w = False
                self.barrier()
            if phases >= 7:
                self._prefetch_ffn_w = (phases >= 8)
                self.emit_attn(b, skip_ln=True,
                               next_ln=("router" if phases >= 8 else None),
                               w_prefetched=True)
                self._prefetch_ffn_w = False
            if phases >= 8:
                self.emit_ffn(b, 1, skip_ln=True, wq_prefetched=True)
            self.emit_store_y(b)
            if phases < 99:
                break
        self.barrier()

    def finalize(self):
        nc = self.nc
        with nc.Block() as block:
            @block.tensor
            def _(t):
                for fn in self.q["pe"]:
                    fn(t)

            @block.scalar
            def _(s):
                for fn in self.q["act"]:
                    fn(s)

            @block.vector
            def _(v):
                for fn in self.q["dve"]:
                    fn(v)

            @block.gpsimd
            def _(g):
                for fn in self.q["gps"]:
                    fn(g)

            @block.sync
            def _(sy):
                for fn in self.q["sp"]:
                    fn(sy)
        return nc


_PROG_CACHE = {}


def build(taps=()):
    key = tuple(sorted(taps))
    if key in _PROG_CACHE:
        return _PROG_CACHE[key]
    mk = MK(taps)
    mk.emit_all()
    nc = mk.finalize()
    _PROG_CACHE[key] = (nc, mk)
    return nc, mk


def make_inputs(inputs, core):
    x = np.ascontiguousarray(np.asarray(inputs["L0_attn_x"])[2 * core:2 * core + 2].reshape(NB * S, D))
    t = lambda a: np.ascontiguousarray(np.asarray(a).T)
    m = {
        "x": x,
        "attn_wqT": t(inputs["L0_attn_Wq"]),
        "attn_wkT": t(inputs["L0_attn_Wk"]),
        "attn_wvT": t(inputs["L0_attn_Wv"]),
        "attn_wpT": t(inputs["L0_attn_Wproj"]),
        "ffn_wqT": t(inputs["L0_ffn_Wq"]),
        "ffn_wkT": t(inputs["L0_ffn_Wk"]),
        "ffn_wvT": t(inputs["L0_ffn_Wv"]),
        "ffn_wr": np.asarray(inputs["L0_ffn_Wr"]),
        "ffn_br": np.asarray(inputs["L0_ffn_br"]),
        "ffn_bq": np.asarray(inputs["L0_ffn_bq"]),
        "ffn_bk": np.asarray(inputs["L0_ffn_bk"]),
        "ffn_bv": np.asarray(inputs["L0_ffn_bv"]),
        "ffn_wo": np.asarray(inputs["L0_ffn_Wo"]),
        "ffn_bo": np.asarray(inputs["L0_ffn_bo"]),
        "lq1": np.asarray(inputs["L0_attn_lq1"]),
        "lk1": np.asarray(inputs["L0_attn_lk1"]),
        "lq2": np.asarray(inputs["L0_attn_lq2"]),
        "lk2": np.asarray(inputs["L0_attn_lk2"]),
    }
    return {k: np.ascontiguousarray(np.asarray(v, dtype=np.float32)) for k, v in m.items()}


def kernel(**inputs):
    nc, mk = build()
    in_maps = [make_inputs(inputs, c) for c in range(8)]
    res = run_bass_kernel_spmd(nc, in_maps, core_ids=list(range(8)))
    out = np.stack([r["y"].reshape(NB, S, D) for r in res.results])
    return out.reshape(16, S, D).astype(np.float32)


# version 17
# speedup vs baseline: 1.2878x; 1.0935x over previous
import math
import numpy as np
import ml_dtypes
import concourse.bass as bass
import concourse.mybir as mybir
from concourse.bass_utils import run_bass_kernel_spmd

F32 = mybir.dt.float32
BF16 = mybir.dt.bfloat16
AF = mybir.ActivationFunctionType
ALU = mybir.AluOpType
AX = mybir.AxisListType

S = 2048
D = 1024
NT = 16
NC = 8
NB = 2
HD = 64
NH = 8
FK = 4
FDH = 64
LN_EPS = 1e-5
RMS_EPS = float(np.finfo(np.float32).eps)
LAMBDA_INIT = 0.8 - 0.6 * math.exp(-0.3 * 4)
ONE_MINUS_LI = 1.0 - LAMBDA_INIT
NEG_BIG = -30000.0


def _rope_tables():
    dim = HD
    inv_freq = 1.0 / 10000.0 ** (np.arange(0, dim, 2, dtype=np.float32) / dim)
    t = np.arange(S, dtype=np.float32)
    freqs = np.outer(t, inv_freq)
    return np.cos(freqs).astype(np.float32), np.sin(freqs).astype(np.float32)


def _consts():
    c = {}
    cos, sin = _rope_tables()
    c["rope_cos"] = cos
    c["rope_sin"] = sin
    c["ident_bf"] = np.eye(128, dtype=np.float32).astype(ml_dtypes.bfloat16)
    c["ident_f32"] = np.eye(128, dtype=np.float32)
    m = np.zeros((128, 128), dtype=np.float32)
    for k in range(128):
        m[k, :k] = NEG_BIG
    c["diag_bf"] = m.astype(ml_dtypes.bfloat16)
    e = np.zeros((16, NC * 128), dtype=np.float32)
    for cc in range(NC):
        for p in range(128):
            e[2 * cc + p // 64, cc * 128 + p] = 1.0
    c["e64_bf"] = e.astype(ml_dtypes.bfloat16)
    ik = np.zeros((128, FK * FDH), dtype=np.float32)
    for p in range(128):
        for k in range(FK):
            ik[p, k * FDH + (p % 64)] = 1.0
    c["ikron_bf"] = ik.astype(ml_dtypes.bfloat16)
    c["kcols"] = np.tile(np.arange(FK, dtype=np.float32)[None, :], (16, 1))
    lt = np.zeros((16, 16), dtype=np.float32)
    for p in range(16):
        lt[p, p + 1:] = 1.0
    c["lt16"] = lt
    return c


class Sem:
    def __init__(self, handle):
        self.h = handle
        self.n = 0

    def inc(self, k=1):
        self.n += k
        return self.n


class MK:
    ENGINES = ("pe", "act", "dve", "gps", "sp")

    def __init__(self, taps=()):
        self.taps = set(taps)
        nc = self.nc = bass.Bass("TRN2", target_bir_lowering=False)
        self.q = {e: [] for e in self.ENGINES}
        self.tap_shapes = {}
        C = _consts()

        self.x_io = nc.dram_tensor("x", [NB * S, D], F32, kind="ExternalInput").ap()
        self.y_io = nc.dram_tensor("y", [NB * S, D], F32, kind="ExternalOutput").ap()
        self.w = {}
        for name, shape in [
            ("attn_wqT", [D, D]), ("attn_wkT", [D, D]), ("attn_wvT", [D, D]),
            ("attn_wpT", [D, D]),
            ("ffn_wqT", [D, D]), ("ffn_wkT", [D, D]), ("ffn_wvT", [D, D]),
            ("ffn_wr", [D, 16]), ("ffn_br", [16]),
            ("ffn_bq", [D]), ("ffn_bk", [D]), ("ffn_bv", [D]),
            ("ffn_wo", [256, D]), ("ffn_bo", [D]),
            ("lq1", [HD]), ("lk1", [HD]), ("lq2", [HD]), ("lk2", [HD]),
        ]:
            self.w[name] = nc.dram_tensor(name, shape, F32, kind="ExternalInput").ap()
        self.cst = {k: nc.inline_tensor(np.asarray(v), f"c_{k}").ap() for k, v in C.items()}

        self.lnx_d = nc.dram_tensor("lnx_d", [S, D], BF16, kind="Internal").ap()
        self.qhat_d = nc.dram_tensor("qhat_d", [S, D], BF16, kind="Internal").ap()
        self.khat_d = nc.dram_tensor("khat_d", [S, D], BF16, kind="Internal").ap()
        self.vhat_d = nc.dram_tensor("vhat_d", [S, NH * 129], BF16, kind="Internal").ap()
        self.zhat_d = nc.dram_tensor("zhat_d", [NH * S, 128], BF16, kind="Internal").ap()
        self.zffn_d = nc.dram_tensor("zffn_d", [S, 256], BF16, kind="Internal").ap()

        sbuf = lambda n, s, d: nc.alloc_sbuf_tensor(n, list(s), d).ap()
        self.x_res = sbuf("x_res", [128, NT * D], F32)
        self.reg_a = sbuf("reg_a", [128, NC * S], BF16)
        self.big_b = sbuf("big_b", [128, NC * S], BF16)
        self.qk_strm = sbuf("qk_strm", [128, 8192], BF16)
        self.wselg = sbuf("wselg", [128, 8192], BF16)
        self.vstrips = sbuf("vstrips", [128, 4160], BF16)
        self.vprod = sbuf("vprod", [128, 1032], BF16)
        self.scr = sbuf("scr", [128, D], F32)
        self.scr2 = sbuf("scr2", [128, D], F32)
        self.qk_bf = sbuf("qk_bf", [128, D], BF16)
        self.pt = sbuf("pt", [128, 2048], BF16)
        self.zstrip = sbuf("zstrip", [128, NT * 256], BF16)
        self.stats = sbuf("stats", [128, 304], F32)
        self.rowbuf = sbuf("rowbuf", [128, 16], F32)
        self.cosb_bf = sbuf("cosb_bf", [128, NT * 32], BF16)
        self.sinb_bf = sbuf("sinb_bf", [128, NT * 32], BF16)
        self.cbf = sbuf("cbf", [128, 648], BF16)
        self.c32 = sbuf("c32", [128, 272], F32)
        self.e64 = sbuf("e64", [16, NC * 128], BF16)
        self.sm = sbuf("sm", [16, 384], F32)
        self.rows = sbuf("rows", [1, 420], F32)
        self.rowsb = sbuf("rowsb", [1, 1024], BF16)

        self.pwide = [nc.alloc_psum_tensor(f"pw{i}", [128, 1024], F32).ap() for i in range(4)]
        self.pb = [self.pwide[i // 2][:, (i % 2) * 512:(i % 2) * 512 + 512] for i in range(8)]
        self.pbig = [self.pwide[2], self.pwide[3]]

        self.qsem = {
            "sp": [Sem(nc.alloc_semaphore(f"d_sp{i}")) for i in range(8)],
            "gps": [Sem(nc.alloc_semaphore(f"d_gp{i}")) for i in range(4)],
            "act": [Sem(nc.alloc_semaphore(f"d_ac{i}")) for i in range(4)],
        }
        self.qrr = {"sp": 0, "gps": 0, "act": 0}
        self.PE = Sem(nc.alloc_semaphore("s_pe"))
        self.ACT = Sem(nc.alloc_semaphore("s_act"))
        self.DVE = Sem(nc.alloc_semaphore("s_dve"))
        self.GPS = Sem(nc.alloc_semaphore("s_gps"))
        self.ALLSEM = ([self.PE, self.ACT, self.DVE, self.GPS]
                       + self.qsem["sp"] + self.qsem["gps"] + self.qsem["act"])

        self.ident_bf = self.cbf[:, 0:128]
        self.diag_bf = self.cbf[:, 128:256]
        self.ikron = self.cbf[:, 256:512]
        self.ones_row_bf = self.cbf[0:1, 516:644]
        self.ident_f32 = self.c32[:, 0:128]
        self.inv2048_col = self.c32[:, 128:129]
        self.lneps_col = self.c32[:, 129:130]
        self.rmseps_col = self.c32[:, 130:131]
        self.one_col_f32 = self.c32[:, 131:132]
        self.neg8_col = self.c32[:, 132:133]
        self.neglam_col = self.c32[:, 133:134]
        self.lnomli_col = self.c32[:, 134:135]
        self.lnzero_col = self.c32[:, 135:136]
        self.ones_row_f32 = self.c32[0:1, 140:268]

        self.wsel = self.wselg[:, 0:6144]
        self.gsel = self.wselg[:, 6144:8192]
        self.qsel = self.qk_strm[:, 0:4096]
        self.ksel = self.qk_strm[:, 4096:8192]

    def on(self, eng, fn):
        self.q[eng].append(fn)

    def wait(self, eng, sem, val):
        if val > 0:
            self.q[eng].append(lambda e, s=sem, v=val: e.wait_ge(s.h, v))

    def barrier_into(self, eng):
        for s in self.ALLSEM:
            self.wait(eng, s, s.n)

    def barrier(self):
        for e in self.ENGINES:
            self.barrier_into(e)

    def _chan(self, queue):
        chans = self.qsem[queue]
        ch = chans[self.qrr[queue] % len(chans)]
        self.qrr[queue] += 1
        return ch

    def dma(self, queue, out, in_, **kw):
        ch = self._chan(queue)
        v0 = ch.n
        self.on(queue, lambda e, o=out, i=in_, s=ch, w=v0, kw=kw: (
            e.wait_ge(s.h, w), e.dma_start(out=o, in_=i, **kw).then_inc(s.h, 16)))
        return ch, ch.inc(16)

    def dmat(self, queue, out, in_):
        ch = self._chan(queue)
        v0 = ch.n
        self.on(queue, lambda e, o=out, i=in_, s=ch, w=v0: (
            e.wait_ge(s.h, w), e.dma_start_transpose(o, i).then_inc(s.h, 16)))
        return ch, ch.inc(16)

    def wait_q(self, eng, queue):
        for ch in self.qsem[queue]:
            self.wait(eng, ch, ch.n)

    def _mk(self, eng, sem, fn, inc):
        w = sem.n

        def step(e, f=fn, s=sem, w=w):
            e.wait_ge(s.h, w)
            f(e).then_inc(s.h, 1)
        self.on(eng, step)
        return sem.inc(1)

    def dve(self, fn, inc=True):
        return self._mk("dve", self.DVE, fn, inc)

    def gps(self, fn, inc=True):
        return self._mk("gps", self.GPS, fn, inc)

    def act(self, fn, inc=True):
        return self._mk("act", self.ACT, fn, inc)

    def pe(self, fn, inc=True):
        if inc:
            self.on("pe", lambda e, f=fn, s=self.PE: f(e).then_inc(s.h, 1))
            return self.PE.inc(1)
        self.on("pe", fn)
        return self.PE.n

    def tap(self, name, ap, dtype=None):
        if name not in self.taps:
            return
        shape = [ap.shape[0], int(np.prod(ap.shape[1:]))]
        dtype = dtype or ap.dtype
        o = self.nc.dram_tensor(f"tap_{name}", shape, dtype, kind="ExternalOutput").ap()
        self.tap_shapes[f"tap_{name}"] = (shape, dtype)
        self.barrier_into("act")
        self.dma("act", o[:, :], ap)

    def tapd(self, name, dram_ap, shape, dtype):
        if name not in self.taps:
            return
        o = self.nc.dram_tensor(f"tap_{name}", list(shape), dtype, kind="ExternalOutput").ap()
        self.tap_shapes[f"tap_{name}"] = (list(shape), dtype)
        self.barrier_into("act")
        self.dma("act", o[:, :], dram_ap)

    def x_tile(self, tt):
        return self.x_res[:, tt * D:(tt + 1) * D]

    def emit_setup(self):
        P, w, cst = self, self.w, self.cst
        P.dma("sp", self.cbf[:, 0:128], cst["ident_bf"][:, :])
        P.dma("sp", self.cbf[:, 128:256], cst["diag_bf"][:, :])
        P.dma("sp", self.cbf[:, 256:512], cst["ikron_bf"][:, :])
        P.dma("sp", self.e64[:, :], cst["e64_bf"][:, :])
        P.dma("sp", self.c32[:, 0:128], cst["ident_f32"][:, :])
        P.dma("gps", self.cosb_bf.rearrange("p (t j) -> p t j", j=32),
              cst["rope_cos"].rearrange("(t p) j -> p t j", p=128))
        P.dma("gps", self.sinb_bf.rearrange("p (t j) -> p t j", j=32),
              cst["rope_sin"].rearrange("(t p) j -> p t j", p=128))
        P.dma("sp", self.sm[:, 0:4], cst["kcols"][:, :])
        P.dma("sp", self.sm[:, 4:20], cst["lt16"][:, :])
        P.dma("sp", self.stats[:, 0:128].rearrange("p (c j) -> p c j", j=16),
              w["ffn_wr"].rearrange("(c p) j -> p c j", p=128))
        self.wr_sb = self.stats[:, 0:128]
        P.dma("sp", self.rows[0:1, 0:64], w["lq1"][None, :])
        P.dma("sp", self.rows[0:1, 64:128], w["lk1"][None, :])
        P.dma("sp", self.rows[0:1, 128:192], w["lq2"][None, :])
        P.dma("sp", self.rows[0:1, 192:256], w["lk2"][None, :])
        P.dma("sp", self.rows[0:1, 256:272], w["ffn_br"][None, :])
        self.wait_q("dve", "sp")
        for ap, v in [
            (self.cbf[:, 512:516], 1.0), (self.cbf[0:1, 516:644], 1.0),
            (self.c32[:, 128:129], 1.0 / 2048.0), (self.c32[:, 129:130], LN_EPS),
            (self.c32[:, 130:131], RMS_EPS), (self.c32[:, 131:132], 1.0),
            (self.c32[:, 132:133], -8.0), (self.c32[0:1, 140:268], 1.0),
            (self.c32[:, 134:135], math.log(ONE_MINUS_LI)), (self.c32[:, 135:136], 0.0),
        ]:
            self.dve(lambda e, a=ap, vv=v: e.memset(a, vv))
        R = self.rows
        self.dve(lambda e: e.tensor_mul(R[0:1, 280:344], R[0:1, 0:64], R[0:1, 64:128]), inc=False)
        self.dve(lambda e: e.tensor_mul(R[0:1, 344:408], R[0:1, 128:192], R[0:1, 192:256]), inc=False)
        self.dve(lambda e: e.reduce_sum(R[0:1, 408:409], R[0:1, 280:344], axis=AX.X), inc=False)
        dv = self.dve(lambda e: e.reduce_sum(R[0:1, 409:410], R[0:1, 344:408], axis=AX.X))
        self.wait("act", self.DVE, dv)
        av = self.act(lambda e: e.activation(out=R[0:1, 412:414], in_=R[0:1, 408:410], func=AF.Exp))
        self.wait("dve", self.ACT, av)
        self.dve(lambda e: e.tensor_sub(R[0:1, 414:415], R[0:1, 412:413], R[0:1, 413:414]), inc=False)
        dv = self.dve(lambda e: e.tensor_scalar(R[0:1, 415:416], R[0:1, 414:415], LAMBDA_INIT, -1.0, ALU.add, ALU.mult))
        self.wait("pe", self.DVE, dv)
        pv = self.pe(lambda e: e.matmul(self.pb[0][:, 0:1], self.ones_row_f32, R[0:1, 415:416], start=True, stop=True))
        self.wait("dve", self.PE, pv)
        dv = self.dve(lambda e: e.tensor_copy(self.c32[:, 133:134], self.pb[0][:, 0:1]))
        self.wait("pe", self.DVE, dv)
        self.tap("lam", self.c32[:, 133:134])

    def emit_load_x(self, b):
        xv = self.x_io.rearrange("(b t p) f -> b p t f", b=NB, p=128)
        return self.dma("sp", self.x_res.rearrange("p (t f) -> p t f", f=D), xv[b])

    def emit_load_ln(self, b, prev_stores=None):
        P = self
        xv = self.x_io.rearrange("(b t p) f -> b t p f", b=NB, p=128)
        for tt in range(NT):
            if prev_stores is not None and prev_stores.get(tt) is not None:
                P.wait("sp", prev_stores[tt][0], prev_stores[tt][1])
            ld = P.dma("sp", self.x_tile(tt), xv[b, tt])
            P.wait("dve", ld[0], ld[1])
            P.wait("act", ld[0], ld[1])
            self._ln_tile(tt, with_router=True)
        self._ln_finish(with_router=True)

    def emit_store_y(self, b):
        yv = self.y_io.rearrange("(b t p) f -> b p t f", b=NB, p=128)
        self.barrier_into("sp")
        return self.dma("sp", yv[b], self.x_res.rearrange("p (t f) -> p t f", f=D))

    def _ln_tile(self, tt, with_router):
        P = self
        st = self.stats
        xt = self.x_tile(tt)
        P.dve(lambda e, xt=xt, tt=tt: e.reduce_sum(st[:, 192 + 2 * tt:193 + 2 * tt], xt, axis=AX.X))
        P.wait("act", self.DVE, self.DVE.n)
        P.act(lambda e, xt=xt, tt=tt: e.activation(
            out=self.scr2[:, 0:D], in_=xt, func=AF.Square,
            accum_out=st[:, 193 + 2 * tt:194 + 2 * tt]))

    def _ln_finish(self, with_router):
        P = self
        st = self.stats
        P.wait("dve", self.ACT, self.ACT.n)
        pairs = st[:, 192:224].rearrange("p (a b) -> p a b", b=2)
        mu = st[:, 224:240]
        m2 = st[:, 240:256]
        P.dve(lambda e: e.tensor_scalar(mu, pairs[:, :, 0:1], 1.0 / D, None, ALU.mult), inc=False)
        P.dve(lambda e: e.tensor_scalar(m2, pairs[:, :, 1:2], 1.0 / D, None, ALU.mult), inc=False)
        P.dve(lambda e: e.tensor_mul(st[:, 136:152], mu, mu), inc=False)
        dv = P.dve(lambda e: e.tensor_sub(m2, m2, st[:, 136:152]))
        P.wait("act", self.DVE, dv)
        P.act(lambda e: e.activation(out=st[:, 136:152], in_=m2, func=AF.Ln, bias=self.lneps_col))
        av = P.act(lambda e: e.activation(out=m2, in_=st[:, 136:152], func=AF.Exp,
                                          scale=-0.5, bias=self.lnzero_col))
        P.wait("dve", self.ACT, av)
        rstd = m2
        st_ms = [None, None, None, None]
        slots4 = (self.qk_bf[:, 0:D], self.pt[:, 0:D], self.pt[:, 1024:2048],
                  self.vprod[:, 0:D])
        for tt in range(NT):
            xt = self.x_tile(tt)
            slot = slots4[tt % 4]
            if st_ms[tt % 4] is not None:
                P.wait("dve", st_ms[tt % 4][0], st_ms[tt % 4][1])
            if with_router:
                P.dve(lambda e, xt=xt, tt=tt: e.tensor_scalar(
                    self.scr[:, 0:D], xt, mu[:, tt:tt + 1], rstd[:, tt:tt + 1],
                    ALU.subtract, ALU.mult), inc=False)
                dv = P.dve(lambda e, slot=slot: e.tensor_copy(slot, self.scr[:, 0:D]))
                P.wait("pe", self.DVE, dv)
                P.pe(lambda e, tt=tt: e.matmul(self.pb[6][0:1, 0:512], self.inv2048_col,
                                               self.scr[:, 0:512], start=(tt == 0), stop=(tt == NT - 1)), inc=False)
                P.pe(lambda e, tt=tt: e.matmul(self.pb[7][0:1, 0:512], self.inv2048_col,
                                               self.scr[:, 512:1024], start=(tt == 0), stop=(tt == NT - 1)))
                P.wait("dve", self.PE, self.PE.n)
                P.wait("sp", self.PE, self.PE.n)
            else:
                P.dve(lambda e, xt=xt, tt=tt, slot=slot: e.tensor_scalar(
                    slot, xt, mu[:, tt:tt + 1], rstd[:, tt:tt + 1],
                    ALU.subtract, ALU.mult))
            P.wait("act", self.DVE, self.DVE.n)
            st_ms[tt % 4] = P.dma("act", self.lnx_d.rearrange("(t p) f -> t p f", p=128)[tt], slot)
        self._ln_transposes()

    def _ln_transposes(self):
        P = self
        P.wait_q("act", "act")
        P.wait_q("act", "sp")
        P.wait("act", self.PE, self.PE.n)
        for c in range(NC):
            P.dmat("act", self.reg_a[:, c * S:(c + 1) * S], self.lnx_d[:, c * 128:(c + 1) * 128])

    def emit_ln(self, b, with_router, tag=""):
        for tt in range(NT):
            self._ln_tile(tt, with_router)
        self._ln_finish(with_router)
        return None

    def emit_router(self, b, ci):
        P = self
        R, SM = self.rows, self.sm
        xmrow = self.scr2[0:1, 0:1024]
        P.wait("act", self.PE, self.PE.n)
        P.act(lambda e: e.copy(xmrow[0:1, 0:512], self.pb[6][0:1, 0:512]), inc=False)
        av = P.act(lambda e: e.copy(xmrow[0:1, 512:1024], self.pb[7][0:1, 0:512]))
        self.tap(f"xmean_{b}_{ci}", xmrow)
        P.wait("pe", self.ACT, av)
        for c in range(NC):
            P.pe(lambda e, c=c: e.transpose(self.pb[5][:, c:c + 1],
                                            xmrow[0:1, c * 128:(c + 1) * 128],
                                            self.ones_row_f32[0:1, 0:1]),
                 inc=(c == NC - 1))
        P.wait("dve", self.PE, self.PE.n)
        dv = P.dve(lambda e: e.tensor_copy(self.stats[:, 136:144], self.pb[5][:, 0:8]))
        P.wait("pe", self.DVE, dv)
        for c in range(NC):
            P.pe(lambda e, c=c: e.matmul(self.pb[4][0:1, 0:16], self.stats[:, 136 + c:137 + c],
                                         self.wr_sb[:, c * 16:(c + 1) * 16],
                                         start=(c == 0), stop=False), inc=False)
        pv = P.pe(lambda e: e.matmul(self.pb[4][0:1, 0:16], self.ones_row_f32[0:1, 0:1],
                                     R[0:1, 256:272], start=False, stop=True))
        P.wait("dve", self.PE, pv)
        dv = P.dve(lambda e: e.tensor_copy(SM[0:1, 128:144], self.pb[4][0:1, 0:16]))
        logits = SM[0:1, 128:144]
        self.tap(f"logits_{b}_{ci}", logits)
        P.dve(lambda e: e.reduce_max(SM[0:1, 144:145], logits, axis=AX.X), inc=False)
        dv = P.dve(lambda e: e.tensor_scalar(SM[0:1, 145:146], SM[0:1, 144:145], -1.0, None, ALU.mult))
        P.wait("act", self.DVE, dv)
        av = P.act(lambda e: e.activation(out=SM[0:1, 148:164], in_=logits, func=AF.Exp,
                                          bias=SM[0:1, 145:146], accum_out=SM[0:1, 146:147]))
        P.wait("dve", self.ACT, av)
        P.dve(lambda e: e.reciprocal(SM[0:1, 147:148], SM[0:1, 146:147]), inc=False)
        dv = P.dve(lambda e: e.tensor_scalar(SM[0:1, 164:180], SM[0:1, 148:164], SM[0:1, 147:148], None, ALU.mult))
        dist = SM[0:1, 164:180]
        self.tap(f"dist_{b}_{ci}", dist)
        P.wait("pe", self.DVE, dv)
        P.pe(lambda e: e.matmul(self.pb[4][0:16, 16:32], self.ones_row_f32[0:1, 0:16], dist,
                                start=True, stop=True), inc=False)
        pv = P.pe(lambda e: e.transpose(self.pb[4][0:16, 32:33], dist, self.ones_row_f32[0:1, 0:1]))
        P.wait("dve", self.PE, pv)
        P.dve(lambda e: e.tensor_copy(SM[:, 20:36], self.pb[4][0:16, 16:32]), inc=False)
        dv = P.dve(lambda e: e.tensor_copy(SM[:, 36:37], self.pb[4][0:16, 32:33]))
        drow, dcol = SM[:, 20:36], SM[:, 36:37]
        dcolb = dcol.broadcast_to([16, 16])
        P.dve(lambda e: e.tensor_tensor(SM[:, 37:53], dcolb, drow, ALU.is_gt), inc=False)
        P.dve(lambda e: e.tensor_tensor(SM[:, 53:69], dcolb, drow, ALU.is_equal), inc=False)
        P.dve(lambda e: e.tensor_mul(SM[:, 53:69], SM[:, 53:69], SM[:, 4:20]), inc=False)
        dv = P.dve(lambda e: e.tensor_add(SM[:, 37:53], SM[:, 37:53], SM[:, 53:69]))
        P.wait("pe", self.DVE, dv)
        pv = P.pe(lambda e: e.matmul(self.pb[4][0:16, 34:35], SM[:, 37:53], self.one_col_f32[0:16, 0:1],
                                     start=True, stop=True))
        P.wait("dve", self.PE, pv)
        dv = P.dve(lambda e: e.tensor_copy(SM[:, 69:70], self.pb[4][0:16, 34:35]))
        rankcol = SM[:, 69:70]
        self.tap(f"rank_{b}_{ci}", rankcol)
        P.dve(lambda e: e.tensor_scalar(SM[:, 70:71], rankcol, float(FK), None, ALU.is_lt), inc=False)
        dv = P.dve(lambda e: e.tensor_mul(SM[:, 71:72], dcol, SM[:, 70:71]))
        P.wait("act", self.DVE, dv)
        av = P.act(lambda e: e.activation(out=SM[:, 72:73], in_=SM[:, 71:72], func=AF.Exp))
        P.wait("pe", self.ACT, av)
        pv = P.pe(lambda e: e.matmul(self.pb[4][0:1, 36:37], SM[:, 72:73], self.one_col_f32[0:16, 0:1],
                                     start=True, stop=True))
        P.wait("dve", self.PE, pv)
        P.dve(lambda e: e.tensor_copy(SM[0:1, 180:181], self.pb[4][0:1, 36:37]), inc=False)
        dv = P.dve(lambda e: e.reciprocal(SM[0:1, 181:182], SM[0:1, 180:181]))
        P.wait("pe", self.DVE, dv)
        pv = P.pe(lambda e: e.matmul(self.pb[4][0:16, 38:39], self.ones_row_f32[0:1, 0:16],
                                     SM[0:1, 181:182], start=True, stop=True))
        P.wait("dve", self.PE, pv)
        P.dve(lambda e: e.tensor_copy(SM[:, 73:74], self.pb[4][0:16, 38:39]), inc=False)
        dv = P.dve(lambda e: e.tensor_mul(SM[:, 74:75], SM[:, 72:73], SM[:, 73:74]))
        gatescol = SM[:, 74:75]
        self.tap(f"gates_{b}_{ci}", gatescol)
        P.dve(lambda e: e.tensor_tensor(SM[:, 75:79], rankcol.broadcast_to([16, 4]), SM[:, 0:4], ALU.is_equal), inc=False)
        P.dve(lambda e: e.tensor_scalar(SM[:, 75:79], SM[:, 75:79], gatescol, None, ALU.mult), inc=False)
        dv = P.dve(lambda e: e.tensor_copy(
            self.sm.bitcast(BF16)[0:16, 512:768].rearrange("p (a b) -> p a b", b=64),
            SM[:, 75:79][:, :, None].broadcast_to([16, 4, 64])))
        bcmp256 = self.sm.bitcast(BF16)[0:16, 512:768]
        P.wait("pe", self.DVE, dv)
        for c in range(NC):
            pv = P.pe(lambda e, c=c: e.matmul(self.pb[3][:, 0:256], self.e64[:, c * 128:(c + 1) * 128],
                                              bcmp256, start=True, stop=True))
            P.wait("dve", self.PE, pv)
            dv = P.dve(lambda e, c=c: e.tensor_tensor(self.gsel[:, c * 256:(c + 1) * 256],
                                                      self.pb[3][:, 0:256], self.ikron, ALU.mult))
            P.wait("pe", self.DVE, dv)
        self.tap(f"G_{b}_{ci}", self.gsel[:, 0:2048])

    def emit_wfold(self, wi, bias_name, bias_slot, wb=None, slot=0):
        P = self
        base = wi * NC * 256
        if wb is None:
            wb = self.big_b[:, slot * NC * D:(slot + 1) * NC * D]
        ev = [0, 0]
        for fs in range(NC):
            bank = self.pb[2 + fs % 2]
            P.wait("pe", self.DVE, ev[fs % 2])
            for mc in range(NC):
                P.pe(lambda e, fs=fs, mc=mc, bank=bank: e.matmul(
                    bank[:, 0:256],
                    wb[:, mc * D + fs * 128: mc * D + (fs + 1) * 128],
                    self.gsel[:, mc * 256:(mc + 1) * 256],
                    start=(mc == 0), stop=(mc == NC - 1)), inc=(mc == NC - 1))
            P.wait("dve", self.PE, self.PE.n)
            P.dve(lambda e, fs=fs, bank=bank: e.tensor_copy(
                self.wsel[:, base + fs * 256: base + (fs + 1) * 256], bank[:, 0:256]))
            ev[fs % 2] = self.DVE.n
        P.wait("pe", self.DVE, self.DVE.n)
        brow = self.scr2[0:1, 0:1024]
        P.wait("sp", self.DVE, self.DVE.n)
        P.wait("sp", self.PE, self.PE.n)
        P.dma("sp", brow, self.w[bias_name][None, :])
        P.wait_q("pe", "sp")
        for c in range(NC):
            P.pe(lambda e, c=c: e.transpose(self.pb[2][:, c:c + 1],
                                            brow[0:1, c * 128:(c + 1) * 128],
                                            self.ones_row_f32[0:1, 0:1]), inc=(c == NC - 1))
        P.wait("dve", self.PE, self.PE.n)
        P.dve(lambda e: e.tensor_copy(self.stats[:, 144:152], self.pb[2][:, 0:8]), inc=False)
        dv = P.dve(lambda e: e.tensor_copy(self.stats.bitcast(BF16)[:, 304:312], self.stats[:, 144:152]))
        bcol_bf = self.stats.bitcast(BF16)[:, 304:312]
        P.wait("pe", self.DVE, dv)
        for h in range(2):
            for mc in range(NC):
                P.pe(lambda e, h=h, mc=mc: e.matmul(
                    self.pb[2][:, 8 + h:9 + h],
                    self.gsel[:, mc * 256 + h * 128: mc * 256 + (h + 1) * 128],
                    bcol_bf[:, mc:mc + 1],
                    start=(mc == 0), stop=(mc == NC - 1)), inc=(h == 1 and mc == NC - 1))
        P.wait("dve", self.PE, self.PE.n)
        dv = P.dve(lambda e, sl=bias_slot: e.tensor_copy(self.rowbuf[:, sl:sl + 2], self.pb[2][:, 8:10]))
        P.wait("pe", self.DVE, dv)

    def emit_ffn_qkv(self, b, ci):
        P = self
        evac_ms = [0, 0]
        P.wait("pe", self.DVE, self.DVE.n)
        for wi, dst, bslot in ((0, self.qsel, 0), (1, self.ksel, 2)):
            base = wi * NC * 256
            for h in range(2):
                for qb in range(4):
                    bank = self.pb[qb % 2]
                    P.wait("pe", self.DVE, evac_ms[qb % 2])
                    for fc in range(NC):
                        P.pe(lambda e, h=h, qb=qb, fc=fc, bank=bank, base=base: e.matmul(
                            bank[:, 0:512],
                            self.wsel[:, base + fc * 256 + h * 128: base + fc * 256 + (h + 1) * 128],
                            self.reg_a[:, fc * S + qb * 512: fc * S + (qb + 1) * 512],
                            start=(fc == 0), stop=(fc == NC - 1)), inc=(fc == NC - 1))
                    P.wait("dve", self.PE, self.PE.n)
                    P.dve(lambda e, h=h, qb=qb, bank=bank, dst=dst, bslot=bslot: e.tensor_scalar(
                        dst[:, h * S + qb * 512:h * S + (qb + 1) * 512],
                        bank[:, 0:512], self.rowbuf[:, bslot + h:bslot + h + 1], None, ALU.add))
                    evac_ms[qb % 2] = self.DVE.n
        self.tap(f"qsel_{b}_{ci}", self.qsel[:, :])
        self.tap(f"ksel_{b}_{ci}", self.ksel[:, :])
        for h in range(2):
            P.pe(lambda e, h=h: e.transpose(self.pb[2][0:1, 16 + h * 128:16 + (h + 1) * 128],
                                            self.rowbuf[:, 4 + h:5 + h], self.ident_f32),
                 inc=(h == 1))
        P.wait("dve", self.PE, self.PE.n)
        dv = P.dve(lambda e: e.tensor_copy(self.rowsb[0:1, 0:256], self.pb[2][0:1, 16:272]))
        P.wait("pe", self.DVE, dv)
        dv = P.dve(lambda e: e.memset(self.vstrips[:, 0:4160], 1.0))
        P.wait("pe", self.DVE, dv)
        vevac_ms = [0, 0]
        for tt in range(NT):
            bank = self.pb[2 + tt % 2]
            P.wait("pe", self.DVE, vevac_ms[tt % 2])
            for fc in range(NC):
                P.pe(lambda e, tt=tt, fc=fc, bank=bank: e.matmul(
                    bank[:, 0:256],
                    self.reg_a[:, fc * S + tt * 128: fc * S + (tt + 1) * 128],
                    self.wsel[:, 2 * NC * 256 + fc * 256: 2 * NC * 256 + (fc + 1) * 256],
                    start=(fc == 0), stop=False), inc=False)
            P.pe(lambda e, bank=bank: e.matmul(bank[:, 0:256], self.ones_row_bf,
                                               self.rowsb[0:1, 0:256], start=False, stop=True))
            P.wait("dve", self.PE, self.PE.n)
            P.dve(lambda e, tt=tt, bank=bank: e.tensor_copy(
                self.vstrips.rearrange("p (h t c) -> p h t c", h=FK, t=NT)[:, :, tt, 0:64],
                bank[:, 0:256].rearrange("p (h c) -> p h c", h=FK)[:, :, :]))
            vevac_ms[tt % 2] = self.DVE.n
        self.tap(f"vffn_{b}_{ci}", self.vstrips[:, 0:4160])

    def emit_ffn_attn(self, b, ci):
        P = self
        fixup_ms = [0, 0]
        exp_ms = [0, 0]
        av_ms = {}
        for hp in range(2):
            for qh in range(8):
                o = qh % 2
                obs = (self.pb[2 * o], self.pb[2 * o + 1])
                oput = [0, 0]
                n_mm_out = 2 * 16

                def emit_scores(g):
                    pw = self.pbig[g % 2]
                    P.wait("pe", self.ACT, exp_ms[g % 2])
                    for ki, kt in enumerate((2 * g, 2 * g + 1)):
                        for hi in range(2):
                            P.pe(lambda e, hp=hp, qh=qh, kt=kt, ki=ki, hi=hi, pw=pw: e.matmul(
                                pw[:, hi * 512 + ki * 256:hi * 512 + (ki + 1) * 256],
                                self.ksel[hi * 64:(hi + 1) * 64, hp * S + kt * 128:hp * S + (kt + 1) * 128],
                                self.qsel[hi * 64:(hi + 1) * 64, hp * S + qh * 256:hp * S + (qh + 1) * 256],
                                start=(ki == 0), stop=(ki == 1),
                                tile_position=(hi * 64, 0)), inc=(ki == 1 and hi == 1))
                    return self.PE.n

                def emit_exp(g, pe_ms):
                    pw = self.pbig[g % 2]
                    P.wait("act", self.PE, pe_ms)
                    P.wait("act", self.PE, av_ms.get(g % 4, 0))
                    P.act(lambda e, pw=pw, g=g: e.activation(
                        out=self.big_b[:, (g % 4) * 1024:(g % 4) * 1024 + 1024],
                        in_=pw[:, 0:1024], func=AF.Exp, scale=0.125))
                    exp_ms[g % 2] = self.ACT.n
                    return self.ACT.n

                def emit_av(g, act_ms):
                    P.wait("pe", self.ACT, act_ms)
                    if oput[0] == 0:
                        P.wait("pe", self.DVE, fixup_ms[o])
                    for hi in range(2):
                        ob = obs[hi]
                        h = 2 * hp + hi
                        for ki in range(2):
                            kt = 2 * g + ki
                            for qi in range(2):
                                first = (oput[hi] == 0)
                                oput[hi] += 1
                                last = (oput[hi] == n_mm_out)
                                P.pe(lambda e, h=h, hi=hi, g=g, ki=ki, kt=kt, qi=qi, ob=ob,
                                     first=first, last=last: e.matmul(
                                    ob[:, qi * 65:(qi + 1) * 65],
                                    self.big_b[:, (g % 4) * 1024 + hi * 512 + ki * 256 + qi * 128:
                                               (g % 4) * 1024 + hi * 512 + ki * 256 + (qi + 1) * 128],
                                    self.vstrips[:, h * 1040 + kt * 65: h * 1040 + (kt + 1) * 65],
                                    start=first, stop=last), inc=last)
                    av_ms[g % 4] = self.PE.n
                    return self.PE.n

                sc_ms = {0: emit_scores(0), 1: emit_scores(1)}
                e_ms = {0: emit_exp(0, sc_ms[0])}
                pe_av = 0
                for g in range(8):
                    if g + 1 < 8:
                        e_ms[g + 1] = emit_exp(g + 1, sc_ms[g + 1])
                    pe_av = emit_av(g, e_ms[g])
                    if g + 2 < 8:
                        sc_ms[g + 2] = emit_scores(g + 2)
                P.wait("dve", self.PE, pe_av)
                for hi in range(2):
                    h = 2 * hp + hi
                    ob = obs[hi]
                    dn = self.stats[:, 160 + 2 * hi:162 + 2 * hi]
                    P.dve(lambda e, ob=ob, dn=dn: e.reciprocal(
                        dn, ob[:, 0:130].rearrange("p (a c) -> p a c", c=65)[:, :, 64:65]))
                    for qi in range(2):
                        qt = 2 * qh + qi
                        P.dve(lambda e, ob=ob, dn=dn, qi=qi, qt=qt, h=h: e.tensor_scalar(
                            self.zstrip[:, qt * 256 + h * 64: qt * 256 + (h + 1) * 64],
                            ob[:, qi * 65: qi * 65 + 64], dn[:, qi:qi + 1], None, ALU.mult))
                fixup_ms[o] = self.DVE.n
        self.tap(f"zffn_{b}_{ci}", self.zstrip[:, :])

    def emit_ffn_out(self, b, ci, fuse_ln=None, fuse_store=False):
        P = self
        P.wait("sp", self.DVE, self.DVE.n)
        P.dma("sp", self.zffn_d.rearrange("(t p) c -> p t c", p=128),
              self.zstrip.rearrange("p (t c) -> p t c", c=256))
        P.wait_q("sp", "sp")
        zt = self.qk_strm[:, 0:2 * S]
        P.wait("sp", self.PE, self.PE.n)
        for c in range(2):
            P.dmat("sp", zt[:, c * S:(c + 1) * S], self.zffn_d[:, c * 128:(c + 1) * 128])
        P.wait("gps", self.PE, self.PE.n)
        P.dma("gps", self.qk_strm[:, 4096:6144].rearrange("p (c f) -> p c f", f=D), self.w["ffn_wo"].rearrange("(c p) f -> p c f", p=128))
        P.dma("gps", self.rowsb[0:1, 0:1024], self.w["ffn_bo"][None, :])
        wo = self.qk_strm[:, 4096:6144]
        if getattr(self, "_prefetch_attn_w", False):
            P.wait("gps", self.PE, self.PE.n)
            P.dma("gps", self.big_b[:, 0:NC * D].rearrange("p (c f) -> p c f", f=D),
                  self.w["attn_wqT"].rearrange("(c p) f -> p c f", p=128))
            P.dma("gps", self.big_b[:, NC * D:2 * NC * D].rearrange("p (c f) -> p c f", f=D),
                  self.w["attn_wkT"].rearrange("(c p) f -> p c f", p=128))
        P.wait_q("pe", "sp")
        P.wait_q("pe", "gps")
        P.wait("pe", self.DVE, self.DVE.n)
        res_ms = [0, 0]
        for tt in range(NT):
            P.wait("pe", self.DVE, res_ms[tt % 2])
            for half in range(2):
                bank = self.pb[2 * (tt % 2) + half]
                for c in range(2):
                    P.pe(lambda e, tt=tt, half=half, c=c, bank=bank: e.matmul(
                        bank[:, 0:512],
                        zt[:, c * S + tt * 128:c * S + (tt + 1) * 128],
                        wo[:, c * 1024 + half * 512:c * 1024 + (half + 1) * 512],
                        start=(c == 0), stop=False), inc=False)
                P.pe(lambda e, half=half, bank=bank: e.matmul(
                    bank[:, 0:512], self.ones_row_bf,
                    self.rowsb[0:1, half * 512:(half + 1) * 512], start=False, stop=True),
                    inc=(half == 1))
            P.wait("dve", self.PE, self.PE.n)
            for half in range(2):
                bank = self.pb[2 * (tt % 2) + half]
                xsl = self.x_res[:, tt * D + half * 512: tt * D + (half + 1) * 512]
                P.dve(lambda e, bank=bank, xsl=xsl: e.scalar_tensor_tensor(
                    xsl, bank[:, 0:512], 0.5, xsl, ALU.mult, ALU.add))
            res_ms[tt % 2] = self.DVE.n
            if fuse_ln is not None:
                self._ln_tile(tt, with_router=(fuse_ln == "router"))
            if fuse_store:
                P.wait("sp", self.DVE, self.DVE.n)
                yv = self.y_io.rearrange("(b t p) f -> b t p f", b=NB, p=128)
                self.ystore_ms[tt] = P.dma("sp", yv[b, tt], self.x_tile(tt))
        if fuse_ln is not None:
            self._ln_finish(with_router=(fuse_ln == "router"))

    def emit_ffn(self, b, ci, skip_ln=False, next_ln=None, wq_prefetched=False,
                 fuse_store=False):
        P = self
        if not skip_ln:
            self.barrier()
            self.emit_ln(b, with_router=True, tag=f"ffn{ci}_{b}")
        if wq_prefetched:
            P.wait("gps", self.PE, self.PE.n)
            P.dma("gps", self.big_b[:, 0:NC * D].rearrange("p (c f) -> p c f", f=D),
                  self.w["ffn_wkT"].rearrange("(c p) f -> p c f", p=128))
            P.dma("gps", self.big_b[:, NC * D:2 * NC * D].rearrange("p (c f) -> p c f", f=D),
                  self.w["ffn_wvT"].rearrange("(c p) f -> p c f", p=128))
            self.emit_router(b, ci)
            P.wait_q("pe", "gps")
            self.emit_wfold(0, "ffn_bq", 0, wb=self.qk_strm[:, 0:8192])
            self.emit_wfold(1, "ffn_bk", 2, slot=0)
            self.emit_wfold(2, "ffn_bv", 4, slot=1)
        else:
            P.wait("gps", self.PE, self.PE.n)
            P.dma("gps", self.big_b[:, 0:NC * D].rearrange("p (c f) -> p c f", f=D),
                  self.w["ffn_wqT"].rearrange("(c p) f -> p c f", p=128))
            P.dma("gps", self.big_b[:, NC * D:2 * NC * D].rearrange("p (c f) -> p c f", f=D),
                  self.w["ffn_wkT"].rearrange("(c p) f -> p c f", p=128))
            self.emit_router(b, ci)
            P.wait_q("pe", "gps")
            self.emit_wfold(0, "ffn_bq", 0, slot=0)
            P.wait("gps", self.PE, self.PE.n)
            P.dma("gps", self.big_b[:, 0:NC * D].rearrange("p (c f) -> p c f", f=D),
                  self.w["ffn_wvT"].rearrange("(c p) f -> p c f", p=128))
            self.emit_wfold(1, "ffn_bk", 2, slot=1)
            P.wait_q("pe", "gps")
            self.emit_wfold(2, "ffn_bv", 4, slot=0)
        self.tap(f"wsel_{b}_{ci}", self.wsel[:, :])
        P.wait("pe", self.DVE, self.DVE.n)
        P.wait_q("pe", "act")
        self.emit_ffn_qkv(b, ci)
        P.wait("pe", self.DVE, self.DVE.n)
        P.wait("act", self.DVE, self.DVE.n)
        self.emit_ffn_attn(b, ci)
        P.wait("pe", self.DVE, self.DVE.n)
        self.emit_ffn_out(b, ci, fuse_ln=next_ln, fuse_store=fuse_store)
        self.barrier()

    def emit_attn_qkv(self, b, w_prefetched=False):
        P = self
        self.krope_ms = [0, 0]
        if not w_prefetched:
            P.wait("gps", self.PE, self.PE.n)
            P.dma("gps", self.big_b[:, 0:NC * D].rearrange("p (c f) -> p c f", f=D),
                  self.w["attn_wqT"].rearrange("(c p) f -> p c f", p=128))
            P.dma("gps", self.big_b[:, NC * D:2 * NC * D].rearrange("p (c f) -> p c f", f=D),
                  self.w["attn_wkT"].rearrange("(c p) f -> p c f", p=128))
        P.wait_q("pe", "gps")
        P.wait("pe", self.DVE, self.DVE.n)
        wbq = self.big_b[:, 0:NC * D]
        wbk = self.big_b[:, NC * D:2 * NC * D]
        ev_q = [0, 0]
        ev_k = [0, 0]
        st_q = [None, None]
        st_k = [None, None]
        slots_q = (self.qk_bf[:, 0:D], self.pt[:, 0:D])
        slots_k = (self.pt[:, 1024:2048], self.vprod[:, 0:D])
        for tt in range(NT):
            for which, wb, pwi, ev_ms, st_ms, slots, ddst in (
                (0, wbq, tt % 2, ev_q, st_q, slots_q, self.qhat_d),
                (1, wbk, 2 + tt % 2, ev_k, st_k, slots_k, self.khat_d),
            ):
                pw = self.pwide[pwi]
                b0, b1 = pw[:, 0:512], pw[:, 512:1024]
                P.wait("pe", self.ACT, ev_ms[tt % 2])
                for fc in range(NC):
                    for half, bank in ((0, b0), (1, b1)):
                        P.pe(lambda e, tt=tt, fc=fc, half=half, bank=bank, wb=wb: e.matmul(
                            bank[:, 0:512],
                            self.reg_a[:, fc * S + tt * 128:fc * S + (tt + 1) * 128],
                            wb[:, fc * D + half * 512:fc * D + (half + 1) * 512],
                            start=(fc == 0), stop=(fc == NC - 1)),
                            inc=(fc == NC - 1 and half == 1))
                ev_ms[tt % 2] = self._emit_rms_rope(tt, pw, ddst, st_ms, which,
                                                    slots[tt % 2])
        P.wait("gps", self.PE, self.PE.n)
        P.dma("gps", self.big_b[:, 0:NC * D].rearrange("p (c f) -> p c f", f=D),
              self.w["attn_wvT"].rearrange("(c p) f -> p c f", p=128))
        P.wait_q("pe", "gps")
        P.wait("dve", self.GPS, self.GPS.n)
        P.wait_q("dve", "sp")
        P.wait("pe", self.ACT, self.ACT.n)
        wb = self.big_b[:, 0:NC * D]
        vev = [0, 0]
        vst = [None, None]
        for tt in range(NT):
            b0, b1 = self.pb[0 + 2 * (tt % 2)], self.pb[1 + 2 * (tt % 2)]
            P.wait("pe", self.DVE, vev[tt % 2])
            for fc in range(NC):
                for half, bank in ((0, b0), (1, b1)):
                    P.pe(lambda e, tt=tt, fc=fc, half=half, bank=bank, wb=wb: e.matmul(
                        bank[:, 0:512],
                        self.reg_a[:, fc * S + tt * 128:fc * S + (tt + 1) * 128],
                        wb[:, fc * D + half * 512:fc * D + (half + 1) * 512],
                        start=(fc == 0), stop=(fc == NC - 1)),
                        inc=(fc == NC - 1 and half == 1))
            P.wait("dve", self.PE, self.PE.n)
            vp = (self.vprod if tt % 2 == 0 else self.vstrips)[:, 0:1032]
            if vst[tt % 2] is not None:
                P.wait("dve", vst[tt % 2][0], vst[tt % 2][1])
            P.dve(lambda e, vp=vp: e.memset(
                vp.rearrange("p (h c) -> p h c", h=NH)[:, :, 128:129], 1.0))
            P.dve(lambda e, b0=b0, vp=vp: e.tensor_copy(
                vp.rearrange("p (h c) -> p h c", h=NH)[:, 0:4, 0:128],
                b0.rearrange("p (h c) -> p h c", h=4)[:, :, :]))
            dv = P.dve(lambda e, b1=b1, vp=vp: e.tensor_copy(
                vp.rearrange("p (h c) -> p h c", h=NH)[:, 4:8, 0:128],
                b1.rearrange("p (h c) -> p h c", h=4)[:, :, :]))
            vev[tt % 2] = dv
            P.wait("sp", self.DVE, dv)
            vst[tt % 2] = P.dma("sp", self.vhat_d.rearrange("(t p) c -> t p c", p=128)[tt], vp)
        self.tap(f"vtm_attn_{b}", self.vstrips[:, :])
        self.tapd(f"qhatd_{b}", self.qhat_d[:, :], [S, D], BF16)
        self.tapd(f"khatd_{b}", self.khat_d[:, :], [S, D], BF16)
        self.tapd(f"vhatd_{b}", self.vhat_d[:, :], [S, NH * 129], BF16)

    def _emit_rms_rope(self, tt, pw, ddst, st_ms, wi, qslot):
        P = self
        if wi == 0:
            EV, ES, ev = P.dve, self.DVE, "dve"
            qf = self.scr[:, 0:D]
            sq = self.scr2[:, 0:D]
            sb = self.scr2.bitcast(BF16)
            qbf = sb[:, 0:1024].rearrange("p (h d) -> p h d", d=64)
            t1 = sb[:, 1024:1536].rearrange("p (h d) -> p h d", d=32)
            t2 = sb[:, 1536:2048].rearrange("p (h d) -> p h d", d=32)
            ms = self.stats[:, 152:168]
            lno = self.stats[:, 136:152]
            rstd = self.stats[:, 168:184]
        else:
            EV, ES, ev = P.gps, self.GPS, "gps"
            zbuf = (self.zstrip if tt % 2 == 0 else self.vstrips)
            zf = zbuf.bitcast(F32)
            qf = zf[:, 0:D]
            sq = zf[:, 1024:2048]
            sb = zbuf
            qbf = sb[:, 2048:3072].rearrange("p (h d) -> p h d", d=64)
            t1 = sb[:, 3072:3584].rearrange("p (h d) -> p h d", d=32)
            t2 = sb[:, 3584:4096].rearrange("p (h d) -> p h d", d=32)
            ms = self.stats[:, 256:272]
            lno = self.stats[:, 272:288]
            rstd = self.stats[:, 288:304]
            P.wait("act", self.GPS, self.krope_ms[tt % 2])
        P.wait("act", self.PE, self.PE.n)
        if wi == 0:
            P.wait("act", self.DVE, self.DVE.n)
        P.act(lambda e, pw=pw, qf=qf: e.copy(qf, pw[:, 0:1024]))
        av = P.act(lambda e, pw=pw, sq=sq: e.activation(out=sq, in_=pw[:, 0:1024], func=AF.Square))
        evac_av = av
        P.wait("dve", self.ACT, av)
        P.dve(lambda e, sq=sq, ms=ms: e.reduce_sum(ms, sq.rearrange("p (h d) -> p h d", d=64), axis=AX.X), inc=False)
        dv = P.dve(lambda e, ms=ms: e.tensor_scalar(ms, ms, 1.0 / 64.0, None, ALU.mult))
        P.wait("act", self.DVE, dv)
        if wi == 1:
            P.wait("act", self.GPS, self.GPS.n)
        P.act(lambda e, ms=ms, lno=lno: e.activation(out=lno, in_=ms, func=AF.Ln,
                                                     bias=self.rmseps_col))
        av = P.act(lambda e, lno=lno, rstd=rstd: e.activation(out=rstd, in_=lno,
                                                              func=AF.Exp, scale=-0.5, bias=self.lnzero_col))
        P.wait(ev, self.ACT, av)
        qv = qf.rearrange("p (h d) -> p h d", d=64)
        EV(lambda e, qv=qv, rstd=rstd, qbf=qbf: e.tensor_tensor(
            qbf, qv, rstd[:, :, None].broadcast_to([128, 16, 64]), ALU.mult), inc=False)
        cosb = self.cosb_bf[:, tt * 32:(tt + 1) * 32][:, None, :].broadcast_to([128, 16, 32])
        sinb = self.sinb_bf[:, tt * 32:(tt + 1) * 32][:, None, :].broadcast_to([128, 16, 32])
        qb = qslot.rearrange("p (h d) -> p h d", d=64)
        q1, q2 = qbf[:, :, 0:32], qbf[:, :, 32:64]
        if st_ms[tt % 2] is not None:
            P.wait(ev, st_ms[tt % 2][0], st_ms[tt % 2][1])
        EV(lambda e: e.tensor_tensor(t1, q1, cosb, ALU.mult), inc=False)
        EV(lambda e: e.tensor_tensor(t2, q2, sinb, ALU.mult), inc=False)
        EV(lambda e: e.tensor_tensor(qb[:, :, 0:32], t1, t2, ALU.add), inc=False)
        EV(lambda e: e.tensor_tensor(t1, q2, cosb, ALU.mult), inc=False)
        EV(lambda e: e.tensor_tensor(t2, q1, sinb, ALU.mult), inc=False)
        dv = EV(lambda e: e.tensor_tensor(qb[:, :, 32:64], t1, t2, ALU.subtract))
        if wi == 1:
            self.krope_ms[tt % 2] = dv
        P.wait("sp", ES, dv)
        st_ms[tt % 2] = P.dma("sp", ddst.rearrange("(t p) f -> t p f", p=128)[tt], qslot)
        return evac_av

    def emit_attn_core(self, b):
        P = self
        def issue_pair_loads(p):
            s = p % 2
            a = P.dmat("sp", self.qk_strm[:, s * 4096: s * 4096 + 2048],
                       self.qhat_d[:, p * 128:(p + 1) * 128])
            b2 = P.dmat("sp", self.qk_strm[:, s * 4096 + 2048: s * 4096 + 4096],
                        self.khat_d[:, p * 128:(p + 1) * 128])
            c = P.dma("sp", self.vstrips[:, s * 2080: s * 2080 + 2064].rearrange(
                          "p (t c) -> p t c", c=129),
                      self.vhat_d.rearrange("(t p) c -> p t c", p=128)[:, :, p * 129:(p + 1) * 129])
            return [a, b2, c]
        ready = {0: issue_pair_loads(0)}
        pe_after_pair = {}
        fixup_ms = [0, 0]
        exp_ms = [0, 0]
        av_ms = {}
        store_ms = None
        for p in range(NH):
            s = p % 2
            if p + 1 < NH:
                P.wait("sp", self.PE, pe_after_pair.get(p - 1, 0))
                ready[p + 1] = issue_pair_loads(p + 1)
            for sem_v in ready[p]:
                P.wait("pe", sem_v[0], sem_v[1])
            qT = self.qk_strm[:, s * 4096: s * 4096 + 2048]
            kT = self.qk_strm[:, s * 4096 + 2048: s * 4096 + 4096]
            vst = self.vstrips[:, s * 2080: s * 2080 + 2064]
            first_fix_of_pair = True
            for qh in range(8):
                o = qh % 2
                obs = (self.pb[2 * o], self.pb[2 * o + 1])
                oput = [0, 0]
                nkt = 2 * qh + 2
                ngroups = nkt // 2
                n_out = [0, 0]
                for kt in range(nkt):
                    for qi in range(2):
                        if 2 * qh + qi >= kt:
                            n_out[0] += 1
                            n_out[1] += 1

                def emit_scores(g):
                    kts = (2 * g, 2 * g + 1)
                    pw = self.pbig[g % 2]
                    P.wait("pe", self.ACT, exp_ms[g % 2])
                    diag = [(ki, qi) for ki, kt in enumerate(kts)
                            for qi in range(2) if 2 * qh + qi == kt]
                    for hi in range(2):
                        last_of_all = (hi == 1)
                        for ki, kt in enumerate(kts):
                            is_last = (not diag) and (ki == 1)
                            P.pe(lambda e, kt=kt, ki=ki, hi=hi, pw=pw, qh=qh,
                                 qT=qT, kT=kT, is_last=is_last: e.matmul(
                                pw[:, hi * 512 + ki * 256:hi * 512 + (ki + 1) * 256],
                                kT[hi * 64:(hi + 1) * 64, kt * 128:(kt + 1) * 128],
                                qT[hi * 64:(hi + 1) * 64, qh * 256:(qh + 1) * 256],
                                start=(ki == 0), stop=is_last,
                                tile_position=(hi * 64, 0)),
                                inc=(is_last and last_of_all))
                        for di, (ki, qi) in enumerate(diag):
                            is_last = (di == len(diag) - 1)
                            P.pe(lambda e, pw=pw, hi=hi, ki=ki, qi=qi, is_last=is_last: e.matmul(
                                pw[:, hi * 512 + ki * 256 + qi * 128:hi * 512 + ki * 256 + (qi + 1) * 128],
                                self.ident_bf, self.diag_bf,
                                start=False, stop=is_last),
                                inc=(is_last and last_of_all))
                    return self.PE.n

                def emit_exp(g, pe_ms):
                    pw = self.pbig[g % 2]
                    P.wait("act", self.PE, pe_ms)
                    P.wait("act", self.PE, av_ms.get(("pt", g % 4), 0))
                    P.act(lambda e, pw=pw, g=g: e.activation(
                        out=self.big_b[:, (g % 4) * 1024:(g % 4) * 1024 + 1024],
                        in_=pw[:, 0:1024], func=AF.Exp, scale=0.125, bias=self.neg8_col))
                    exp_ms[g % 2] = self.ACT.n
                    return self.ACT.n

                def emit_av(g, act_ms):
                    kts = (2 * g, 2 * g + 1)
                    P.wait("pe", self.ACT, act_ms)
                    if oput[0] == 0:
                        P.wait("pe", self.DVE, fixup_ms[o])
                    for hi in range(2):
                        ob = obs[hi]
                        for ki, kt in enumerate(kts):
                            for qi in range(2):
                                if 2 * qh + qi < kt:
                                    continue
                                first = (oput[hi] == 0)
                                oput[hi] += 1
                                last = (oput[hi] == n_out[hi])
                                P.pe(lambda e, hi=hi, g=g, ki=ki, kt=kt, qi=qi, ob=ob,
                                     vst=vst, first=first, last=last: e.matmul(
                                    ob[:, qi * 129:(qi + 1) * 129],
                                    self.big_b[:, (g % 4) * 1024 + hi * 512 + ki * 256 + qi * 128:
                                               (g % 4) * 1024 + hi * 512 + ki * 256 + (qi + 1) * 128],
                                    vst[:, kt * 129:(kt + 1) * 129],
                                    start=first, stop=last), inc=last)
                    av_ms[("pt", g % 4)] = self.PE.n
                    return self.PE.n

                G = ngroups
                sc_ms = {0: emit_scores(0)}
                if G > 1:
                    sc_ms[1] = emit_scores(1)
                e_ms = {0: emit_exp(0, sc_ms[0])}
                pe_av = 0
                for g in range(G):
                    if g + 1 < G:
                        e_ms[g + 1] = emit_exp(g + 1, sc_ms[g + 1])
                    pe_av = emit_av(g, e_ms[g])
                    if g + 2 < G:
                        sc_ms[g + 2] = emit_scores(g + 2)
                P.wait("dve", self.PE, pe_av)
                if first_fix_of_pair and store_ms is not None:
                    P.wait("dve", store_ms[0], store_ms[1])
                    first_fix_of_pair = False
                ob1, ob2 = obs
                st = self.stats
                P.dve(lambda e, ob1=ob1: e.reciprocal(
                    st[:, 184:186], ob1[:, 0:258].rearrange("p (a c) -> p a c", c=129)[:, :, 128:129]))
                P.dve(lambda e, ob2=ob2: e.reciprocal(
                    st[:, 186:188], ob2[:, 0:258].rearrange("p (a c) -> p a c", c=129)[:, :, 128:129]))
                P.dve(lambda e: e.tensor_scalar(st[:, 188:190], st[:, 186:188],
                                                self.neglam_col, None, ALU.mult))
                for qi in range(2):
                    qt = 2 * qh + qi
                    zraw = self.zstrip[:, 2048 + qt * 128: 2048 + (qt + 1) * 128]
                    P.dve(lambda e, ob1=ob1, qi=qi: e.tensor_scalar(
                        self.scr2[:, qi * 128:(qi + 1) * 128],
                        ob1[:, qi * 129:qi * 129 + 128], st[:, 184 + qi:185 + qi], None, ALU.mult))
                    P.dve(lambda e, ob2=ob2, qi=qi, zraw=zraw: e.scalar_tensor_tensor(
                        zraw, ob2[:, qi * 129:qi * 129 + 128], st[:, 188 + qi:189 + qi],
                        self.scr2[:, qi * 128:(qi + 1) * 128], ALU.mult, ALU.add))
                    P.dve(lambda e, zraw=zraw: e.bn_stats(st[:, 136:142], zraw))
                    P.dve(lambda e, qt=qt: e.bn_aggr(st[:, 192 + 2 * qt:194 + 2 * qt], st[:, 136:142]))
                fixup_ms[o] = self.DVE.n
            dv = self.DVE.n
            P.wait("act", self.DVE, dv)
            varv = self.stats[:, 192:224].rearrange("p (a b) -> p a b", b=2)[:, :, 1:2]
            P.act(lambda e, varv=varv: e.activation(
                out=self.stats[:, 136:152], in_=varv, func=AF.Ln, bias=self.lneps_col))
            av2 = P.act(lambda e: e.activation(
                out=self.stats[:, 224:240], in_=self.stats[:, 136:152],
                func=AF.Exp, scale=-0.5, bias=self.lnomli_col))
            P.wait("dve", self.ACT, av2)
            for qt in range(NT):
                P.dve(lambda e, qt=qt: e.tensor_scalar(
                    self.zstrip[:, qt * 128:(qt + 1) * 128],
                    self.zstrip[:, 2048 + qt * 128: 2048 + (qt + 1) * 128],
                    self.stats[:, 192 + 2 * qt:193 + 2 * qt],
                    self.stats[:, 224 + qt:225 + qt], ALU.subtract, ALU.mult))
            pe_after_pair[p] = self.PE.n
            P.wait("sp", self.DVE, self.DVE.n)
            store_ms = P.dma("sp", self.zhat_d.rearrange("(h t p) d -> h p t d", h=NH, p=128)[p],
                             self.zstrip[:, 0:2048].rearrange("p (t d) -> p t d", d=128))

    def emit_attn_out(self, b, fuse_ln=None):
        P = self
        self.tapd(f"zhatd_{b}", self.zhat_d[:, :], [NH * S, 128], BF16)
        P.wait_q("sp", "sp")
        for c in range(NC):
            P.dmat("sp", self.big_b[:, c * S:(c + 1) * S],
                   self.zhat_d.rearrange("(h t p) d -> h (t p) d", h=NH, p=128)[c])
        P.dma("gps", self.wselg[:, 0:8192].rearrange("p (c f) -> p c f", f=D), self.w["attn_wpT"].rearrange("(c p) f -> p c f", p=128))
        if getattr(self, "_prefetch_ffn_w", False):
            P.wait("gps", self.PE, self.PE.n)
            P.dma("gps", self.qk_strm[:, 0:8192].rearrange("p (c f) -> p c f", f=D),
                  self.w["ffn_wqT"].rearrange("(c p) f -> p c f", p=128))
        P.wait_q("pe", "sp")
        P.wait_q("pe", "gps")
        wp = self.wselg[:, 0:8192]
        res_ms = [0, 0]
        for tt in range(NT):
            b0, b1 = self.pb[0 + 2 * (tt % 2)], self.pb[1 + 2 * (tt % 2)]
            P.wait("pe", self.DVE, res_ms[tt % 2])
            for fc in range(NC):
                for half, bank in ((0, b0), (1, b1)):
                    P.pe(lambda e, tt=tt, fc=fc, half=half, bank=bank: e.matmul(
                        bank[:, 0:512],
                        self.big_b[:, fc * S + tt * 128:fc * S + (tt + 1) * 128],
                        wp[:, fc * D + half * 512:fc * D + (half + 1) * 512],
                        start=(fc == 0), stop=(fc == NC - 1)),
                        inc=(fc == NC - 1 and half == 1))
            P.wait("dve", self.PE, self.PE.n)
            for half, bank in ((0, b0), (1, b1)):
                xsl = self.x_res[:, tt * D + half * 512: tt * D + (half + 1) * 512]
                P.dve(lambda e, bank=bank, xsl=xsl: e.tensor_tensor(
                    xsl, xsl, bank[:, 0:512], ALU.add))
            res_ms[tt % 2] = self.DVE.n
            if fuse_ln is not None:
                self._ln_tile(tt, with_router=(fuse_ln == "router"))
        if fuse_ln is not None:
            self._ln_finish(with_router=(fuse_ln == "router"))

    def emit_attn(self, b, skip_ln=False, next_ln=None, w_prefetched=False):
        if not skip_ln:
            self.barrier()
            self.emit_ln(b, with_router=False, tag=f"attn_{b}")
        P = self
        P.wait("pe", self.DVE, self.DVE.n)
        P.wait_q("pe", "act")
        self.emit_attn_qkv(b, w_prefetched=w_prefetched)
        P.wait_q("sp", "sp")
        P.wait("dve", self.GPS, self.GPS.n)
        self.emit_attn_core(b)
        P.wait_q("sp", "sp")
        P.wait("sp", self.PE, self.PE.n)
        P.wait("act", self.PE, self.PE.n)
        self.emit_attn_out(b, fuse_ln=next_ln)
        self.barrier()

    def emit_all(self, phases=None):
        import os
        if phases is None:
            phases = int(os.environ.get("MK_PHASES", "99"))
        self.emit_setup()
        self.ystore_ms = {}
        if phases >= 99:
            for b in range(NB):
                prev = dict(self.ystore_ms) if b > 0 else None
                self.ystore_ms = {}
                self.emit_load_ln(b, prev_stores=prev)
                self._prefetch_attn_w = True
                self.emit_ffn(b, 0, skip_ln=True, next_ln="plain")
                self._prefetch_attn_w = False
                self._prefetch_ffn_w = True
                self.emit_attn(b, skip_ln=True, next_ln="router", w_prefetched=True)
                self._prefetch_ffn_w = False
                self.emit_ffn(b, 1, skip_ln=True, wq_prefetched=True, fuse_store=True)
            self.barrier()
            return
        for b in range(NB):
            self.barrier()
            self.emit_load_x(b)
            self.barrier()
            if phases >= 1:
                self.emit_ln(b, with_router=True, tag=f"ffn0_{b}")
                self.dma("gps", self.big_b[:, 0:NC * D].rearrange("p (c f) -> p c f", f=D),
                         self.w["ffn_wqT"].rearrange("(c p) f -> p c f", p=128))
            if phases >= 2:
                self.emit_router(b, 0)
            if phases >= 3:
                self.wait_q("pe", "gps")
                self.emit_wfold(0, "ffn_bq", 0)
                self.wait("gps", self.PE, self.PE.n)
                self.dma("gps", self.big_b[:, 0:NC * D].rearrange("p (c f) -> p c f", f=D),
                         self.w["ffn_wkT"].rearrange("(c p) f -> p c f", p=128))
                self.wait_q("pe", "gps")
                self.emit_wfold(1, "ffn_bk", 2)
                self.wait("gps", self.PE, self.PE.n)
                self.dma("gps", self.big_b[:, 0:NC * D].rearrange("p (c f) -> p c f", f=D),
                         self.w["ffn_wvT"].rearrange("(c p) f -> p c f", p=128))
                self.wait_q("pe", "gps")
                self.emit_wfold(2, "ffn_bv", 4)
                self.tap(f"wsel_{b}_0", self.wsel[:, :])
            if phases >= 4:
                self.barrier()
                self.emit_ffn_qkv(b, 0)
            if phases >= 5:
                self.barrier()
                self.emit_ffn_attn(b, 0)
            if phases >= 6:
                self.barrier()
                self.emit_ffn_out(b, 0)
                self.barrier()
            self.emit_store_y(b)
            break
        self.barrier()

    def finalize(self):
        nc = self.nc
        with nc.Block() as block:
            @block.tensor
            def _(t):
                for fn in self.q["pe"]:
                    fn(t)

            @block.scalar
            def _(s):
                for fn in self.q["act"]:
                    fn(s)

            @block.vector
            def _(v):
                for fn in self.q["dve"]:
                    fn(v)

            @block.gpsimd
            def _(g):
                for fn in self.q["gps"]:
                    fn(g)

            @block.sync
            def _(sy):
                for fn in self.q["sp"]:
                    fn(sy)
        return nc


_PROG_CACHE = {}


def build(taps=()):
    key = tuple(sorted(taps))
    if key in _PROG_CACHE:
        return _PROG_CACHE[key]
    mk = MK(taps)
    mk.emit_all()
    nc = mk.finalize()
    _PROG_CACHE[key] = (nc, mk)
    return nc, mk


def make_inputs(inputs, core):
    x = np.ascontiguousarray(np.asarray(inputs["L0_attn_x"])[2 * core:2 * core + 2].reshape(NB * S, D))
    t = lambda a: np.ascontiguousarray(np.asarray(a).T)
    m = {
        "x": x,
        "attn_wqT": t(inputs["L0_attn_Wq"]),
        "attn_wkT": t(inputs["L0_attn_Wk"]),
        "attn_wvT": t(inputs["L0_attn_Wv"]),
        "attn_wpT": t(inputs["L0_attn_Wproj"]),
        "ffn_wqT": t(inputs["L0_ffn_Wq"]),
        "ffn_wkT": t(inputs["L0_ffn_Wk"]),
        "ffn_wvT": t(inputs["L0_ffn_Wv"]),
        "ffn_wr": np.asarray(inputs["L0_ffn_Wr"]),
        "ffn_br": np.asarray(inputs["L0_ffn_br"]),
        "ffn_bq": np.asarray(inputs["L0_ffn_bq"]),
        "ffn_bk": np.asarray(inputs["L0_ffn_bk"]),
        "ffn_bv": np.asarray(inputs["L0_ffn_bv"]),
        "ffn_wo": np.asarray(inputs["L0_ffn_Wo"]),
        "ffn_bo": np.asarray(inputs["L0_ffn_bo"]),
        "lq1": np.asarray(inputs["L0_attn_lq1"]),
        "lk1": np.asarray(inputs["L0_attn_lk1"]),
        "lq2": np.asarray(inputs["L0_attn_lq2"]),
        "lk2": np.asarray(inputs["L0_attn_lk2"]),
    }
    return {k: np.ascontiguousarray(np.asarray(v, dtype=np.float32)) for k, v in m.items()}


def kernel(**inputs):
    nc, mk = build()
    in_maps = [make_inputs(inputs, c) for c in range(8)]
    res = run_bass_kernel_spmd(nc, in_maps, core_ids=list(range(8)))
    out = np.stack([r["y"].reshape(NB, S, D) for r in res.results])
    return out.reshape(16, S, D).astype(np.float32)


# version 18
# speedup vs baseline: 1.3481x; 1.0468x over previous
import math
import numpy as np
import ml_dtypes
import concourse.bass as bass
import concourse.mybir as mybir
from concourse.bass_utils import run_bass_kernel_spmd

F32 = mybir.dt.float32
BF16 = mybir.dt.bfloat16
AF = mybir.ActivationFunctionType
ALU = mybir.AluOpType
AX = mybir.AxisListType

S = 2048
D = 1024
NT = 16
NC = 8
NB = 2
HD = 64
NH = 8
FK = 4
FDH = 64
LN_EPS = 1e-5
RMS_EPS = float(np.finfo(np.float32).eps)
LAMBDA_INIT = 0.8 - 0.6 * math.exp(-0.3 * 4)
ONE_MINUS_LI = 1.0 - LAMBDA_INIT
NEG_BIG = -30000.0


def _rope_tables():
    dim = HD
    inv_freq = 1.0 / 10000.0 ** (np.arange(0, dim, 2, dtype=np.float32) / dim)
    t = np.arange(S, dtype=np.float32)
    freqs = np.outer(t, inv_freq)
    return np.cos(freqs).astype(np.float32), np.sin(freqs).astype(np.float32)


def _consts():
    c = {}
    cos, sin = _rope_tables()
    c["rope_cos"] = cos
    c["rope_sin"] = sin
    c["ident_bf"] = np.eye(128, dtype=np.float32).astype(ml_dtypes.bfloat16)
    c["ident_f32"] = np.eye(128, dtype=np.float32)
    m = np.zeros((128, 128), dtype=np.float32)
    for k in range(128):
        m[k, :k] = NEG_BIG
    c["diag_bf"] = m.astype(ml_dtypes.bfloat16)
    e = np.zeros((16, NC * 128), dtype=np.float32)
    for cc in range(NC):
        for p in range(128):
            e[2 * cc + p // 64, cc * 128 + p] = 1.0
    c["e64_bf"] = e.astype(ml_dtypes.bfloat16)
    ik = np.zeros((128, FK * FDH), dtype=np.float32)
    for p in range(128):
        for k in range(FK):
            ik[p, k * FDH + (p % 64)] = 1.0
    c["ikron_bf"] = ik.astype(ml_dtypes.bfloat16)
    c["kcols"] = np.tile(np.arange(FK, dtype=np.float32)[None, :], (16, 1))
    lt = np.zeros((16, 16), dtype=np.float32)
    for p in range(16):
        lt[p, p + 1:] = 1.0
    c["lt16"] = lt
    return c


class Sem:
    def __init__(self, handle):
        self.h = handle
        self.n = 0

    def inc(self, k=1):
        self.n += k
        return self.n


class MK:
    ENGINES = ("pe", "act", "dve", "gps", "sp")

    def __init__(self, taps=()):
        self.taps = set(taps)
        nc = self.nc = bass.Bass("TRN2", target_bir_lowering=False)
        self.q = {e: [] for e in self.ENGINES}
        self.tap_shapes = {}
        C = _consts()

        self.x_io = nc.dram_tensor("x", [NB * S, D], F32, kind="ExternalInput").ap()
        self.y_io = nc.dram_tensor("y", [NB * S, D], F32, kind="ExternalOutput").ap()
        self.w = {}
        for name, shape in [
            ("attn_wqT", [D, D]), ("attn_wkT", [D, D]), ("attn_wvT", [D, D]),
            ("attn_wpT", [D, D]),
            ("ffn_wqT", [D, D]), ("ffn_wkT", [D, D]), ("ffn_wvT", [D, D]),
            ("ffn_wr", [D, 16]), ("ffn_br", [16]),
            ("ffn_bq", [D]), ("ffn_bk", [D]), ("ffn_bv", [D]),
            ("ffn_wo", [256, D]), ("ffn_bo", [D]),
            ("lq1", [HD]), ("lk1", [HD]), ("lq2", [HD]), ("lk2", [HD]),
        ]:
            self.w[name] = nc.dram_tensor(name, shape, F32, kind="ExternalInput").ap()
        self.cst = {k: nc.inline_tensor(np.asarray(v), f"c_{k}").ap() for k, v in C.items()}

        self.lnx_d = nc.dram_tensor("lnx_d", [S, D], BF16, kind="Internal").ap()
        self.qhat_d = nc.dram_tensor("qhat_d", [S, D], BF16, kind="Internal").ap()
        self.khat_d = nc.dram_tensor("khat_d", [S, D], BF16, kind="Internal").ap()
        self.vhat_d = nc.dram_tensor("vhat_d", [S, NH * 129], BF16, kind="Internal").ap()
        self.zhat_d = nc.dram_tensor("zhat_d", [NH * S, 128], BF16, kind="Internal").ap()
        self.zffn_d = nc.dram_tensor("zffn_d", [S, 256], BF16, kind="Internal").ap()

        sbuf = lambda n, s, d: nc.alloc_sbuf_tensor(n, list(s), d).ap()
        self.x_res = sbuf("x_res", [128, NT * D], F32)
        self.reg_a = sbuf("reg_a", [128, NC * S], BF16)
        self.big_b = sbuf("big_b", [128, NC * S], BF16)
        self.qk_strm = sbuf("qk_strm", [128, 8192], BF16)
        self.wselg = sbuf("wselg", [128, 8192], BF16)
        self.vstrips = sbuf("vstrips", [128, 4160], BF16)
        self.vprod = sbuf("vprod", [128, 1032], BF16)
        self.scr = sbuf("scr", [128, D], F32)
        self.scr2 = sbuf("scr2", [128, D], F32)
        self.qk_bf = sbuf("qk_bf", [128, D], BF16)
        self.pt = sbuf("pt", [128, 2048], BF16)
        self.zstrip = sbuf("zstrip", [128, NT * 256], BF16)
        self.stats = sbuf("stats", [128, 304], F32)
        self.rowbuf = sbuf("rowbuf", [128, 16], F32)
        self.cosb_bf = sbuf("cosb_bf", [128, NT * 32], BF16)
        self.sinb_bf = sbuf("sinb_bf", [128, NT * 32], BF16)
        self.cbf = sbuf("cbf", [128, 648], BF16)
        self.c32 = sbuf("c32", [128, 272], F32)
        self.e64 = sbuf("e64", [16, NC * 128], BF16)
        self.sm = sbuf("sm", [16, 384], F32)
        self.rows = sbuf("rows", [1, 420], F32)
        self.rowsb = sbuf("rowsb", [1, 1024], BF16)

        self.pwide = [nc.alloc_psum_tensor(f"pw{i}", [128, 1024], F32).ap() for i in range(4)]
        self.pb = [self.pwide[i // 2][:, (i % 2) * 512:(i % 2) * 512 + 512] for i in range(8)]
        self.pbig = [self.pwide[2], self.pwide[3]]

        self.qsem = {
            "sp": [Sem(nc.alloc_semaphore(f"d_sp{i}")) for i in range(8)],
            "gps": [Sem(nc.alloc_semaphore(f"d_gp{i}")) for i in range(4)],
            "act": [Sem(nc.alloc_semaphore(f"d_ac{i}")) for i in range(4)],
        }
        self.qrr = {"sp": 0, "gps": 0, "act": 0}
        self.PE = Sem(nc.alloc_semaphore("s_pe"))
        self.ACT = Sem(nc.alloc_semaphore("s_act"))
        self.DVE = Sem(nc.alloc_semaphore("s_dve"))
        self.GPS = Sem(nc.alloc_semaphore("s_gps"))
        self.ALLSEM = ([self.PE, self.ACT, self.DVE, self.GPS]
                       + self.qsem["sp"] + self.qsem["gps"] + self.qsem["act"])

        self.ident_bf = self.cbf[:, 0:128]
        self.diag_bf = self.cbf[:, 128:256]
        self.ikron = self.cbf[:, 256:512]
        self.ones_row_bf = self.cbf[0:1, 516:644]
        self.ident_f32 = self.c32[:, 0:128]
        self.inv2048_col = self.c32[:, 128:129]
        self.lneps_col = self.c32[:, 129:130]
        self.rmseps_col = self.c32[:, 130:131]
        self.one_col_f32 = self.c32[:, 131:132]
        self.neg8_col = self.c32[:, 132:133]
        self.neglam_col = self.c32[:, 133:134]
        self.lnomli_col = self.c32[:, 134:135]
        self.lnzero_col = self.c32[:, 135:136]
        self.ones_row_f32 = self.c32[0:1, 140:268]

        self.wsel = self.wselg[:, 0:6144]
        self.gsel = self.wselg[:, 6144:8192]
        self.qsel = self.qk_strm[:, 0:4096]
        self.ksel = self.qk_strm[:, 4096:8192]

    def on(self, eng, fn):
        self.q[eng].append(fn)

    def wait(self, eng, sem, val):
        if val > 0:
            self.q[eng].append(lambda e, s=sem, v=val: e.wait_ge(s.h, v))

    def barrier_into(self, eng):
        for s in self.ALLSEM:
            self.wait(eng, s, s.n)

    def barrier(self):
        for e in self.ENGINES:
            self.barrier_into(e)

    def _chan(self, queue):
        chans = self.qsem[queue]
        ch = chans[self.qrr[queue] % len(chans)]
        self.qrr[queue] += 1
        return ch

    def dma(self, queue, out, in_, **kw):
        ch = self._chan(queue)
        v0 = ch.n
        self.on(queue, lambda e, o=out, i=in_, s=ch, w=v0, kw=kw: (
            e.wait_ge(s.h, w), e.dma_start(out=o, in_=i, **kw).then_inc(s.h, 16)))
        return ch, ch.inc(16)

    def dmat(self, queue, out, in_):
        ch = self._chan(queue)
        v0 = ch.n
        self.on(queue, lambda e, o=out, i=in_, s=ch, w=v0: (
            e.wait_ge(s.h, w), e.dma_start_transpose(o, i).then_inc(s.h, 16)))
        return ch, ch.inc(16)

    def wait_q(self, eng, queue):
        for ch in self.qsem[queue]:
            self.wait(eng, ch, ch.n)

    def _mk(self, eng, sem, fn, inc):
        w = sem.n

        def step(e, f=fn, s=sem, w=w):
            e.wait_ge(s.h, w)
            f(e).then_inc(s.h, 1)
        self.on(eng, step)
        return sem.inc(1)

    def dve(self, fn, inc=True):
        return self._mk("dve", self.DVE, fn, inc)

    def gps(self, fn, inc=True):
        return self._mk("gps", self.GPS, fn, inc)

    def act(self, fn, inc=True):
        return self._mk("act", self.ACT, fn, inc)

    def pe(self, fn, inc=True):
        if inc:
            self.on("pe", lambda e, f=fn, s=self.PE: f(e).then_inc(s.h, 1))
            return self.PE.inc(1)
        self.on("pe", fn)
        return self.PE.n

    def tap(self, name, ap, dtype=None):
        if name not in self.taps:
            return
        shape = [ap.shape[0], int(np.prod(ap.shape[1:]))]
        dtype = dtype or ap.dtype
        o = self.nc.dram_tensor(f"tap_{name}", shape, dtype, kind="ExternalOutput").ap()
        self.tap_shapes[f"tap_{name}"] = (shape, dtype)
        self.barrier_into("act")
        self.dma("act", o[:, :], ap)

    def tapd(self, name, dram_ap, shape, dtype):
        if name not in self.taps:
            return
        o = self.nc.dram_tensor(f"tap_{name}", list(shape), dtype, kind="ExternalOutput").ap()
        self.tap_shapes[f"tap_{name}"] = (list(shape), dtype)
        self.barrier_into("act")
        self.dma("act", o[:, :], dram_ap)

    def x_tile(self, tt):
        return self.x_res[:, tt * D:(tt + 1) * D]

    def emit_setup(self):
        P, w, cst = self, self.w, self.cst
        P.dma("sp", self.cbf[:, 0:128], cst["ident_bf"][:, :])
        P.dma("sp", self.cbf[:, 128:256], cst["diag_bf"][:, :])
        P.dma("sp", self.cbf[:, 256:512], cst["ikron_bf"][:, :])
        P.dma("sp", self.e64[:, :], cst["e64_bf"][:, :])
        P.dma("sp", self.c32[:, 0:128], cst["ident_f32"][:, :])
        P.dma("gps", self.cosb_bf.rearrange("p (t j) -> p t j", j=32),
              cst["rope_cos"].rearrange("(t p) j -> p t j", p=128))
        P.dma("gps", self.sinb_bf.rearrange("p (t j) -> p t j", j=32),
              cst["rope_sin"].rearrange("(t p) j -> p t j", p=128))
        P.dma("sp", self.sm[:, 0:4], cst["kcols"][:, :])
        P.dma("sp", self.sm[:, 4:20], cst["lt16"][:, :])
        P.dma("sp", self.stats[:, 0:128].rearrange("p (c j) -> p c j", j=16),
              w["ffn_wr"].rearrange("(c p) j -> p c j", p=128))
        self.wr_sb = self.stats[:, 0:128]
        P.dma("sp", self.rows[0:1, 0:64], w["lq1"][None, :])
        P.dma("sp", self.rows[0:1, 64:128], w["lk1"][None, :])
        P.dma("sp", self.rows[0:1, 128:192], w["lq2"][None, :])
        P.dma("sp", self.rows[0:1, 192:256], w["lk2"][None, :])
        P.dma("sp", self.rows[0:1, 256:272], w["ffn_br"][None, :])
        self.wait_q("dve", "sp")
        for ap, v in [
            (self.cbf[:, 512:516], 1.0), (self.cbf[0:1, 516:644], 1.0),
            (self.c32[:, 128:129], 1.0 / 2048.0), (self.c32[:, 129:130], LN_EPS),
            (self.c32[:, 130:131], RMS_EPS), (self.c32[:, 131:132], 1.0),
            (self.c32[:, 132:133], -8.0), (self.c32[0:1, 140:268], 1.0),
            (self.c32[:, 134:135], math.log(ONE_MINUS_LI)), (self.c32[:, 135:136], 0.0),
        ]:
            self.dve(lambda e, a=ap, vv=v: e.memset(a, vv))
        R = self.rows
        self.dve(lambda e: e.tensor_mul(R[0:1, 280:344], R[0:1, 0:64], R[0:1, 64:128]), inc=False)
        self.dve(lambda e: e.tensor_mul(R[0:1, 344:408], R[0:1, 128:192], R[0:1, 192:256]), inc=False)
        self.dve(lambda e: e.reduce_sum(R[0:1, 408:409], R[0:1, 280:344], axis=AX.X), inc=False)
        dv = self.dve(lambda e: e.reduce_sum(R[0:1, 409:410], R[0:1, 344:408], axis=AX.X))
        self.wait("act", self.DVE, dv)
        av = self.act(lambda e: e.activation(out=R[0:1, 412:414], in_=R[0:1, 408:410], func=AF.Exp))
        self.wait("dve", self.ACT, av)
        self.dve(lambda e: e.tensor_sub(R[0:1, 414:415], R[0:1, 412:413], R[0:1, 413:414]), inc=False)
        dv = self.dve(lambda e: e.tensor_scalar(R[0:1, 415:416], R[0:1, 414:415], LAMBDA_INIT, -1.0, ALU.add, ALU.mult))
        self.wait("pe", self.DVE, dv)
        pv = self.pe(lambda e: e.matmul(self.pb[0][:, 0:1], self.ones_row_f32, R[0:1, 415:416], start=True, stop=True))
        self.wait("dve", self.PE, pv)
        dv = self.dve(lambda e: e.tensor_copy(self.c32[:, 133:134], self.pb[0][:, 0:1]))
        self.wait("pe", self.DVE, dv)
        self.tap("lam", self.c32[:, 133:134])

    def emit_load_x(self, b):
        xv = self.x_io.rearrange("(b t p) f -> b p t f", b=NB, p=128)
        return self.dma("sp", self.x_res.rearrange("p (t f) -> p t f", f=D), xv[b])

    def emit_load_ln(self, b, prev_stores=None):
        P = self
        xv = self.x_io.rearrange("(b t p) f -> b t p f", b=NB, p=128)
        for tt in range(NT):
            if prev_stores is not None and prev_stores.get(tt) is not None:
                P.wait("sp", prev_stores[tt][0], prev_stores[tt][1])
            ld = P.dma("sp", self.x_tile(tt), xv[b, tt])
            P.wait("dve", ld[0], ld[1])
            P.wait("act", ld[0], ld[1])
            self._ln_tile(tt, with_router=True)
        self._ln_finish(with_router=True)

    def emit_store_y(self, b):
        yv = self.y_io.rearrange("(b t p) f -> b p t f", b=NB, p=128)
        self.barrier_into("sp")
        return self.dma("sp", yv[b], self.x_res.rearrange("p (t f) -> p t f", f=D))

    def _ln_tile(self, tt, with_router):
        P = self
        st = self.stats
        xt = self.x_tile(tt)
        P.dve(lambda e, xt=xt, tt=tt: e.reduce_sum(st[:, 192 + 2 * tt:193 + 2 * tt], xt, axis=AX.X))
        P.wait("act", self.DVE, self.DVE.n)
        P.act(lambda e, xt=xt, tt=tt: e.activation(
            out=self.scr2[:, 0:D], in_=xt, func=AF.Square,
            accum_out=st[:, 193 + 2 * tt:194 + 2 * tt]))

    def _ln_finish(self, with_router):
        P = self
        st = self.stats
        P.wait("dve", self.ACT, self.ACT.n)
        pairs = st[:, 192:224].rearrange("p (a b) -> p a b", b=2)
        mu = st[:, 224:240]
        m2 = st[:, 240:256]
        P.dve(lambda e: e.tensor_scalar(mu, pairs[:, :, 0:1], 1.0 / D, None, ALU.mult), inc=False)
        P.dve(lambda e: e.tensor_scalar(m2, pairs[:, :, 1:2], 1.0 / D, None, ALU.mult), inc=False)
        P.dve(lambda e: e.tensor_mul(st[:, 136:152], mu, mu), inc=False)
        dv = P.dve(lambda e: e.tensor_sub(m2, m2, st[:, 136:152]))
        P.wait("act", self.DVE, dv)
        P.act(lambda e: e.activation(out=st[:, 136:152], in_=m2, func=AF.Ln, bias=self.lneps_col))
        av = P.act(lambda e: e.activation(out=m2, in_=st[:, 136:152], func=AF.Exp,
                                          scale=-0.5, bias=self.lnzero_col))
        P.wait("dve", self.ACT, av)
        rstd = m2
        st_ms = [None, None, None, None]
        slots4 = (self.qk_bf[:, 0:D], self.pt[:, 0:D], self.pt[:, 1024:2048],
                  self.vprod[:, 0:D])
        for tt in range(NT):
            xt = self.x_tile(tt)
            slot = slots4[tt % 4]
            if st_ms[tt % 4] is not None:
                P.wait("dve", st_ms[tt % 4][0], st_ms[tt % 4][1])
            if with_router:
                P.dve(lambda e, xt=xt, tt=tt: e.tensor_scalar(
                    self.scr[:, 0:D], xt, mu[:, tt:tt + 1], rstd[:, tt:tt + 1],
                    ALU.subtract, ALU.mult), inc=False)
                dv = P.dve(lambda e, slot=slot: e.tensor_copy(slot, self.scr[:, 0:D]))
                P.wait("pe", self.DVE, dv)
                P.pe(lambda e, tt=tt: e.matmul(self.pb[6][0:1, 0:512], self.inv2048_col,
                                               self.scr[:, 0:512], start=(tt == 0), stop=(tt == NT - 1)), inc=False)
                P.pe(lambda e, tt=tt: e.matmul(self.pb[7][0:1, 0:512], self.inv2048_col,
                                               self.scr[:, 512:1024], start=(tt == 0), stop=(tt == NT - 1)))
                P.wait("dve", self.PE, self.PE.n)
                P.wait("sp", self.PE, self.PE.n)
            else:
                P.dve(lambda e, xt=xt, tt=tt, slot=slot: e.tensor_scalar(
                    slot, xt, mu[:, tt:tt + 1], rstd[:, tt:tt + 1],
                    ALU.subtract, ALU.mult))
            P.wait("act", self.DVE, self.DVE.n)
            st_ms[tt % 4] = P.dma("act", self.lnx_d.rearrange("(t p) f -> t p f", p=128)[tt], slot)
        self._ln_transposes()

    def _ln_transposes(self):
        P = self
        P.wait_q("act", "act")
        P.wait_q("act", "sp")
        P.wait("act", self.PE, self.PE.n)
        for c in range(NC):
            P.dmat("act", self.reg_a[:, c * S:(c + 1) * S], self.lnx_d[:, c * 128:(c + 1) * 128])

    def emit_ln(self, b, with_router, tag=""):
        for tt in range(NT):
            self._ln_tile(tt, with_router)
        self._ln_finish(with_router)
        return None

    def emit_router(self, b, ci):
        P = self
        R, SM = self.rows, self.sm
        xmrow = self.scr2[0:1, 0:1024]
        P.wait("act", self.PE, self.PE.n)
        P.act(lambda e: e.copy(xmrow[0:1, 0:512], self.pb[6][0:1, 0:512]), inc=False)
        av = P.act(lambda e: e.copy(xmrow[0:1, 512:1024], self.pb[7][0:1, 0:512]))
        self.tap(f"xmean_{b}_{ci}", xmrow)
        P.wait("pe", self.ACT, av)
        for c in range(NC):
            P.pe(lambda e, c=c: e.transpose(self.pb[5][:, c:c + 1],
                                            xmrow[0:1, c * 128:(c + 1) * 128],
                                            self.ones_row_f32[0:1, 0:1]),
                 inc=(c == NC - 1))
        P.wait("dve", self.PE, self.PE.n)
        dv = P.dve(lambda e: e.tensor_copy(self.stats[:, 136:144], self.pb[5][:, 0:8]))
        P.wait("pe", self.DVE, dv)
        for c in range(NC):
            P.pe(lambda e, c=c: e.matmul(self.pb[4][0:1, 0:16], self.stats[:, 136 + c:137 + c],
                                         self.wr_sb[:, c * 16:(c + 1) * 16],
                                         start=(c == 0), stop=False), inc=False)
        pv = P.pe(lambda e: e.matmul(self.pb[4][0:1, 0:16], self.ones_row_f32[0:1, 0:1],
                                     R[0:1, 256:272], start=False, stop=True))
        P.wait("dve", self.PE, pv)
        dv = P.dve(lambda e: e.tensor_copy(SM[0:1, 128:144], self.pb[4][0:1, 0:16]))
        logits = SM[0:1, 128:144]
        self.tap(f"logits_{b}_{ci}", logits)
        P.dve(lambda e: e.reduce_max(SM[0:1, 144:145], logits, axis=AX.X), inc=False)
        dv = P.dve(lambda e: e.tensor_scalar(SM[0:1, 145:146], SM[0:1, 144:145], -1.0, None, ALU.mult))
        P.wait("act", self.DVE, dv)
        av = P.act(lambda e: e.activation(out=SM[0:1, 148:164], in_=logits, func=AF.Exp,
                                          bias=SM[0:1, 145:146], accum_out=SM[0:1, 146:147]))
        P.wait("dve", self.ACT, av)
        P.dve(lambda e: e.reciprocal(SM[0:1, 147:148], SM[0:1, 146:147]), inc=False)
        dv = P.dve(lambda e: e.tensor_scalar(SM[0:1, 164:180], SM[0:1, 148:164], SM[0:1, 147:148], None, ALU.mult))
        dist = SM[0:1, 164:180]
        self.tap(f"dist_{b}_{ci}", dist)
        P.wait("pe", self.DVE, dv)
        P.pe(lambda e: e.matmul(self.pb[4][0:16, 16:32], self.ones_row_f32[0:1, 0:16], dist,
                                start=True, stop=True), inc=False)
        pv = P.pe(lambda e: e.transpose(self.pb[4][0:16, 32:33], dist, self.ones_row_f32[0:1, 0:1]))
        P.wait("dve", self.PE, pv)
        P.dve(lambda e: e.tensor_copy(SM[:, 20:36], self.pb[4][0:16, 16:32]), inc=False)
        dv = P.dve(lambda e: e.tensor_copy(SM[:, 36:37], self.pb[4][0:16, 32:33]))
        drow, dcol = SM[:, 20:36], SM[:, 36:37]
        dcolb = dcol.broadcast_to([16, 16])
        P.dve(lambda e: e.tensor_tensor(SM[:, 37:53], dcolb, drow, ALU.is_gt), inc=False)
        P.dve(lambda e: e.tensor_tensor(SM[:, 53:69], dcolb, drow, ALU.is_equal), inc=False)
        P.dve(lambda e: e.tensor_mul(SM[:, 53:69], SM[:, 53:69], SM[:, 4:20]), inc=False)
        dv = P.dve(lambda e: e.tensor_add(SM[:, 37:53], SM[:, 37:53], SM[:, 53:69]))
        P.wait("pe", self.DVE, dv)
        pv = P.pe(lambda e: e.matmul(self.pb[4][0:16, 34:35], SM[:, 37:53], self.one_col_f32[0:16, 0:1],
                                     start=True, stop=True))
        P.wait("dve", self.PE, pv)
        dv = P.dve(lambda e: e.tensor_copy(SM[:, 69:70], self.pb[4][0:16, 34:35]))
        rankcol = SM[:, 69:70]
        self.tap(f"rank_{b}_{ci}", rankcol)
        P.dve(lambda e: e.tensor_scalar(SM[:, 70:71], rankcol, float(FK), None, ALU.is_lt), inc=False)
        dv = P.dve(lambda e: e.tensor_mul(SM[:, 71:72], dcol, SM[:, 70:71]))
        P.wait("act", self.DVE, dv)
        av = P.act(lambda e: e.activation(out=SM[:, 72:73], in_=SM[:, 71:72], func=AF.Exp))
        P.wait("pe", self.ACT, av)
        pv = P.pe(lambda e: e.matmul(self.pb[4][0:1, 36:37], SM[:, 72:73], self.one_col_f32[0:16, 0:1],
                                     start=True, stop=True))
        P.wait("dve", self.PE, pv)
        P.dve(lambda e: e.tensor_copy(SM[0:1, 180:181], self.pb[4][0:1, 36:37]), inc=False)
        dv = P.dve(lambda e: e.reciprocal(SM[0:1, 181:182], SM[0:1, 180:181]))
        P.wait("pe", self.DVE, dv)
        pv = P.pe(lambda e: e.matmul(self.pb[4][0:16, 38:39], self.ones_row_f32[0:1, 0:16],
                                     SM[0:1, 181:182], start=True, stop=True))
        P.wait("dve", self.PE, pv)
        P.dve(lambda e: e.tensor_copy(SM[:, 73:74], self.pb[4][0:16, 38:39]), inc=False)
        dv = P.dve(lambda e: e.tensor_mul(SM[:, 74:75], SM[:, 72:73], SM[:, 73:74]))
        gatescol = SM[:, 74:75]
        self.tap(f"gates_{b}_{ci}", gatescol)
        P.dve(lambda e: e.tensor_tensor(SM[:, 75:79], rankcol.broadcast_to([16, 4]), SM[:, 0:4], ALU.is_equal), inc=False)
        P.dve(lambda e: e.tensor_scalar(SM[:, 75:79], SM[:, 75:79], gatescol, None, ALU.mult), inc=False)
        dv = P.dve(lambda e: e.tensor_copy(
            self.sm.bitcast(BF16)[0:16, 512:768].rearrange("p (a b) -> p a b", b=64),
            SM[:, 75:79][:, :, None].broadcast_to([16, 4, 64])))
        bcmp256 = self.sm.bitcast(BF16)[0:16, 512:768]
        P.wait("pe", self.DVE, dv)
        for c in range(NC):
            pv = P.pe(lambda e, c=c: e.matmul(self.pb[3][:, 0:256], self.e64[:, c * 128:(c + 1) * 128],
                                              bcmp256, start=True, stop=True))
            P.wait("dve", self.PE, pv)
            dv = P.dve(lambda e, c=c: e.tensor_tensor(self.gsel[:, c * 256:(c + 1) * 256],
                                                      self.pb[3][:, 0:256], self.ikron, ALU.mult))
            P.wait("pe", self.DVE, dv)
        self.tap(f"G_{b}_{ci}", self.gsel[:, 0:2048])

    def emit_wfold(self, wi, bias_name, bias_slot, wb=None, slot=0):
        P = self
        base = wi * NC * 256
        if wb is None:
            wb = self.big_b[:, slot * NC * D:(slot + 1) * NC * D]
        ev = [0, 0]
        for fs in range(NC):
            bank = self.pb[2 + fs % 2]
            P.wait("pe", self.DVE, ev[fs % 2])
            for mc in range(NC):
                P.pe(lambda e, fs=fs, mc=mc, bank=bank: e.matmul(
                    bank[:, 0:256],
                    wb[:, mc * D + fs * 128: mc * D + (fs + 1) * 128],
                    self.gsel[:, mc * 256:(mc + 1) * 256],
                    start=(mc == 0), stop=(mc == NC - 1)), inc=(mc == NC - 1))
            P.wait("dve", self.PE, self.PE.n)
            P.dve(lambda e, fs=fs, bank=bank: e.tensor_copy(
                self.wsel[:, base + fs * 256: base + (fs + 1) * 256], bank[:, 0:256]))
            ev[fs % 2] = self.DVE.n
        P.wait("pe", self.DVE, self.DVE.n)
        brow = self.scr2[0:1, 0:1024]
        P.wait("sp", self.DVE, self.DVE.n)
        P.wait("sp", self.PE, self.PE.n)
        P.dma("sp", brow, self.w[bias_name][None, :])
        P.wait_q("pe", "sp")
        for c in range(NC):
            P.pe(lambda e, c=c: e.transpose(self.pb[2][:, c:c + 1],
                                            brow[0:1, c * 128:(c + 1) * 128],
                                            self.ones_row_f32[0:1, 0:1]), inc=(c == NC - 1))
        P.wait("dve", self.PE, self.PE.n)
        P.dve(lambda e: e.tensor_copy(self.stats[:, 144:152], self.pb[2][:, 0:8]), inc=False)
        dv = P.dve(lambda e: e.tensor_copy(self.stats.bitcast(BF16)[:, 304:312], self.stats[:, 144:152]))
        bcol_bf = self.stats.bitcast(BF16)[:, 304:312]
        P.wait("pe", self.DVE, dv)
        for h in range(2):
            for mc in range(NC):
                P.pe(lambda e, h=h, mc=mc: e.matmul(
                    self.pb[2][:, 8 + h:9 + h],
                    self.gsel[:, mc * 256 + h * 128: mc * 256 + (h + 1) * 128],
                    bcol_bf[:, mc:mc + 1],
                    start=(mc == 0), stop=(mc == NC - 1)), inc=(h == 1 and mc == NC - 1))
        P.wait("dve", self.PE, self.PE.n)
        dv = P.dve(lambda e, sl=bias_slot: e.tensor_copy(self.rowbuf[:, sl:sl + 2], self.pb[2][:, 8:10]))
        P.wait("pe", self.DVE, dv)

    def emit_ffn_qkv(self, b, ci):
        P = self
        evac_ms = [0, 0]
        P.wait("pe", self.DVE, self.DVE.n)
        for wi, dst, bslot in ((0, self.qsel, 0), (1, self.ksel, 2)):
            base = wi * NC * 256
            for h in range(2):
                for qb in range(4):
                    bank = self.pb[qb % 2]
                    P.wait("pe", self.DVE, evac_ms[qb % 2])
                    for fc in range(NC):
                        P.pe(lambda e, h=h, qb=qb, fc=fc, bank=bank, base=base: e.matmul(
                            bank[:, 0:512],
                            self.wsel[:, base + fc * 256 + h * 128: base + fc * 256 + (h + 1) * 128],
                            self.reg_a[:, fc * S + qb * 512: fc * S + (qb + 1) * 512],
                            start=(fc == 0), stop=(fc == NC - 1)), inc=(fc == NC - 1))
                    P.wait("dve", self.PE, self.PE.n)
                    P.dve(lambda e, h=h, qb=qb, bank=bank, dst=dst, bslot=bslot: e.tensor_scalar(
                        dst[:, h * S + qb * 512:h * S + (qb + 1) * 512],
                        bank[:, 0:512], self.rowbuf[:, bslot + h:bslot + h + 1], None, ALU.add))
                    evac_ms[qb % 2] = self.DVE.n
        self.tap(f"qsel_{b}_{ci}", self.qsel[:, :])
        self.tap(f"ksel_{b}_{ci}", self.ksel[:, :])
        for h in range(2):
            P.pe(lambda e, h=h: e.transpose(self.pb[2][0:1, 16 + h * 128:16 + (h + 1) * 128],
                                            self.rowbuf[:, 4 + h:5 + h], self.ident_f32),
                 inc=(h == 1))
        P.wait("dve", self.PE, self.PE.n)
        dv = P.dve(lambda e: e.tensor_copy(self.rowsb[0:1, 0:256], self.pb[2][0:1, 16:272]))
        P.wait("pe", self.DVE, dv)
        dv = P.dve(lambda e: e.memset(self.vstrips[:, 0:4160], 1.0))
        P.wait("pe", self.DVE, dv)
        vevac_ms = [0, 0]
        for tt in range(NT):
            bank = self.pb[2 + tt % 2]
            P.wait("pe", self.DVE, vevac_ms[tt % 2])
            for fc in range(NC):
                P.pe(lambda e, tt=tt, fc=fc, bank=bank: e.matmul(
                    bank[:, 0:256],
                    self.reg_a[:, fc * S + tt * 128: fc * S + (tt + 1) * 128],
                    self.wsel[:, 2 * NC * 256 + fc * 256: 2 * NC * 256 + (fc + 1) * 256],
                    start=(fc == 0), stop=False), inc=False)
            P.pe(lambda e, bank=bank: e.matmul(bank[:, 0:256], self.ones_row_bf,
                                               self.rowsb[0:1, 0:256], start=False, stop=True))
            P.wait("dve", self.PE, self.PE.n)
            P.dve(lambda e, tt=tt, bank=bank: e.tensor_copy(
                self.vstrips.rearrange("p (h t c) -> p h t c", h=FK, t=NT)[:, :, tt, 0:64],
                bank[:, 0:256].rearrange("p (h c) -> p h c", h=FK)[:, :, :]))
            vevac_ms[tt % 2] = self.DVE.n
        self.tap(f"vffn_{b}_{ci}", self.vstrips[:, 0:4160])

    def emit_ffn_attn(self, b, ci):
        P = self
        fixup_ms = [0, 0]
        exp_ms = [0, 0]
        av_ms = {}
        for hp in range(2):
            for qh in range(8):
                o = qh % 2
                obs = (self.pb[2 * o], self.pb[2 * o + 1])
                oput = [0, 0]
                n_mm_out = 2 * 16

                def emit_scores(g):
                    pw = self.pbig[g % 2]
                    P.wait("pe", self.ACT, exp_ms[g % 2])
                    for ki, kt in enumerate((2 * g, 2 * g + 1)):
                        for hi in range(2):
                            P.pe(lambda e, hp=hp, qh=qh, kt=kt, ki=ki, hi=hi, pw=pw: e.matmul(
                                pw[:, hi * 512 + ki * 256:hi * 512 + (ki + 1) * 256],
                                self.ksel[hi * 64:(hi + 1) * 64, hp * S + kt * 128:hp * S + (kt + 1) * 128],
                                self.qsel[hi * 64:(hi + 1) * 64, hp * S + qh * 256:hp * S + (qh + 1) * 256],
                                start=(ki == 0), stop=(ki == 1),
                                tile_position=(hi * 64, 0)), inc=(ki == 1 and hi == 1))
                    return self.PE.n

                def emit_exp(g, pe_ms):
                    pw = self.pbig[g % 2]
                    P.wait("act", self.PE, pe_ms)
                    P.wait("act", self.PE, av_ms.get(g % 4, 0))
                    P.act(lambda e, pw=pw, g=g: e.activation(
                        out=self.big_b[:, (g % 4) * 1024:(g % 4) * 1024 + 1024],
                        in_=pw[:, 0:1024], func=AF.Exp, scale=0.125))
                    exp_ms[g % 2] = self.ACT.n
                    return self.ACT.n

                def emit_av(g, act_ms):
                    P.wait("pe", self.ACT, act_ms)
                    if oput[0] == 0:
                        P.wait("pe", self.DVE, fixup_ms[o])
                    for hi in range(2):
                        ob = obs[hi]
                        h = 2 * hp + hi
                        for ki in range(2):
                            kt = 2 * g + ki
                            for qi in range(2):
                                first = (oput[hi] == 0)
                                oput[hi] += 1
                                last = (oput[hi] == n_mm_out)
                                P.pe(lambda e, h=h, hi=hi, g=g, ki=ki, kt=kt, qi=qi, ob=ob,
                                     first=first, last=last: e.matmul(
                                    ob[:, qi * 65:(qi + 1) * 65],
                                    self.big_b[:, (g % 4) * 1024 + hi * 512 + ki * 256 + qi * 128:
                                               (g % 4) * 1024 + hi * 512 + ki * 256 + (qi + 1) * 128],
                                    self.vstrips[:, h * 1040 + kt * 65: h * 1040 + (kt + 1) * 65],
                                    start=first, stop=last), inc=last)
                    av_ms[g % 4] = self.PE.n
                    return self.PE.n

                sc_ms = {0: emit_scores(0), 1: emit_scores(1)}
                e_ms = {0: emit_exp(0, sc_ms[0])}
                pe_av = 0
                for g in range(8):
                    if g + 1 < 8:
                        e_ms[g + 1] = emit_exp(g + 1, sc_ms[g + 1])
                    pe_av = emit_av(g, e_ms[g])
                    if g + 2 < 8:
                        sc_ms[g + 2] = emit_scores(g + 2)
                P.wait("dve", self.PE, pe_av)
                for hi in range(2):
                    h = 2 * hp + hi
                    ob = obs[hi]
                    dn = self.stats[:, 160 + 2 * hi:162 + 2 * hi]
                    P.dve(lambda e, ob=ob, dn=dn: e.reciprocal(
                        dn, ob[:, 0:130].rearrange("p (a c) -> p a c", c=65)[:, :, 64:65]))
                    for qi in range(2):
                        qt = 2 * qh + qi
                        P.dve(lambda e, ob=ob, dn=dn, qi=qi, qt=qt, h=h: e.tensor_scalar(
                            self.zstrip[:, qt * 256 + h * 64: qt * 256 + (h + 1) * 64],
                            ob[:, qi * 65: qi * 65 + 64], dn[:, qi:qi + 1], None, ALU.mult))
                fixup_ms[o] = self.DVE.n
        self.tap(f"zffn_{b}_{ci}", self.zstrip[:, :])

    def emit_ffn_out(self, b, ci, fuse_ln=None, fuse_store=False):
        P = self
        P.wait("sp", self.DVE, self.DVE.n)
        P.dma("sp", self.zffn_d.rearrange("(t p) c -> p t c", p=128),
              self.zstrip.rearrange("p (t c) -> p t c", c=256))
        P.wait_q("sp", "sp")
        zt = self.qk_strm[:, 0:2 * S]
        P.wait("sp", self.PE, self.PE.n)
        for c in range(2):
            P.dmat("sp", zt[:, c * S:(c + 1) * S], self.zffn_d[:, c * 128:(c + 1) * 128])
        P.wait("gps", self.PE, self.PE.n)
        P.dma("gps", self.qk_strm[:, 4096:6144].rearrange("p (c f) -> p c f", f=D), self.w["ffn_wo"].rearrange("(c p) f -> p c f", p=128))
        P.dma("gps", self.rowsb[0:1, 0:1024], self.w["ffn_bo"][None, :])
        wo = self.qk_strm[:, 4096:6144]
        if getattr(self, "_prefetch_attn_w", False):
            P.wait("gps", self.PE, self.PE.n)
            P.dma("gps", self.big_b[:, 0:NC * D].rearrange("p (c f) -> p c f", f=D),
                  self.w["attn_wqT"].rearrange("(c p) f -> p c f", p=128))
            P.dma("gps", self.big_b[:, NC * D:2 * NC * D].rearrange("p (c f) -> p c f", f=D),
                  self.w["attn_wkT"].rearrange("(c p) f -> p c f", p=128))
        P.wait_q("pe", "sp")
        P.wait_q("pe", "gps")
        P.wait("pe", self.DVE, self.DVE.n)
        res_ms = [0, 0]
        for tt in range(NT):
            P.wait("pe", self.DVE, res_ms[tt % 2])
            for half in range(2):
                bank = self.pb[2 * (tt % 2) + half]
                for c in range(2):
                    P.pe(lambda e, tt=tt, half=half, c=c, bank=bank: e.matmul(
                        bank[:, 0:512],
                        zt[:, c * S + tt * 128:c * S + (tt + 1) * 128],
                        wo[:, c * 1024 + half * 512:c * 1024 + (half + 1) * 512],
                        start=(c == 0), stop=False), inc=False)
                P.pe(lambda e, half=half, bank=bank: e.matmul(
                    bank[:, 0:512], self.ones_row_bf,
                    self.rowsb[0:1, half * 512:(half + 1) * 512], start=False, stop=True),
                    inc=(half == 1))
            P.wait("dve", self.PE, self.PE.n)
            for half in range(2):
                bank = self.pb[2 * (tt % 2) + half]
                xsl = self.x_res[:, tt * D + half * 512: tt * D + (half + 1) * 512]
                P.dve(lambda e, bank=bank, xsl=xsl: e.scalar_tensor_tensor(
                    xsl, bank[:, 0:512], 0.5, xsl, ALU.mult, ALU.add))
            res_ms[tt % 2] = self.DVE.n
            if fuse_ln is not None:
                self._ln_tile(tt, with_router=(fuse_ln == "router"))
            if fuse_store:
                P.wait("sp", self.DVE, self.DVE.n)
                yv = self.y_io.rearrange("(b t p) f -> b t p f", b=NB, p=128)
                self.ystore_ms[tt] = P.dma("sp", yv[b, tt], self.x_tile(tt))
        if fuse_ln is not None:
            self._ln_finish(with_router=(fuse_ln == "router"))

    def emit_ffn(self, b, ci, skip_ln=False, next_ln=None, wq_prefetched=False,
                 fuse_store=False):
        P = self
        if not skip_ln:
            self.barrier()
            self.emit_ln(b, with_router=True, tag=f"ffn{ci}_{b}")
        if wq_prefetched:
            P.wait("gps", self.PE, self.PE.n)
            P.dma("gps", self.big_b[:, 0:NC * D].rearrange("p (c f) -> p c f", f=D),
                  self.w["ffn_wkT"].rearrange("(c p) f -> p c f", p=128))
            P.dma("gps", self.big_b[:, NC * D:2 * NC * D].rearrange("p (c f) -> p c f", f=D),
                  self.w["ffn_wvT"].rearrange("(c p) f -> p c f", p=128))
            self.emit_router(b, ci)
            P.wait_q("pe", "gps")
            self.emit_wfold(0, "ffn_bq", 0, wb=self.qk_strm[:, 0:8192])
            self.emit_wfold(1, "ffn_bk", 2, slot=0)
            self.emit_wfold(2, "ffn_bv", 4, slot=1)
        else:
            P.wait("gps", self.PE, self.PE.n)
            P.dma("gps", self.big_b[:, 0:NC * D].rearrange("p (c f) -> p c f", f=D),
                  self.w["ffn_wqT"].rearrange("(c p) f -> p c f", p=128))
            P.dma("gps", self.big_b[:, NC * D:2 * NC * D].rearrange("p (c f) -> p c f", f=D),
                  self.w["ffn_wkT"].rearrange("(c p) f -> p c f", p=128))
            self.emit_router(b, ci)
            P.wait_q("pe", "gps")
            self.emit_wfold(0, "ffn_bq", 0, slot=0)
            P.wait("gps", self.PE, self.PE.n)
            P.dma("gps", self.big_b[:, 0:NC * D].rearrange("p (c f) -> p c f", f=D),
                  self.w["ffn_wvT"].rearrange("(c p) f -> p c f", p=128))
            self.emit_wfold(1, "ffn_bk", 2, slot=1)
            P.wait_q("pe", "gps")
            self.emit_wfold(2, "ffn_bv", 4, slot=0)
        self.tap(f"wsel_{b}_{ci}", self.wsel[:, :])
        P.wait("pe", self.DVE, self.DVE.n)
        P.wait_q("pe", "act")
        self.emit_ffn_qkv(b, ci)
        P.wait("pe", self.DVE, self.DVE.n)
        P.wait("act", self.DVE, self.DVE.n)
        self.emit_ffn_attn(b, ci)
        P.wait("pe", self.DVE, self.DVE.n)
        self.emit_ffn_out(b, ci, fuse_ln=next_ln, fuse_store=fuse_store)
        self.barrier()

    def emit_attn_qkv(self, b, w_prefetched=False):
        P = self
        self.krope_ms = [0, 0]
        if not w_prefetched:
            P.wait("gps", self.PE, self.PE.n)
            P.dma("gps", self.big_b[:, 0:NC * D].rearrange("p (c f) -> p c f", f=D),
                  self.w["attn_wqT"].rearrange("(c p) f -> p c f", p=128))
            P.dma("gps", self.big_b[:, NC * D:2 * NC * D].rearrange("p (c f) -> p c f", f=D),
                  self.w["attn_wkT"].rearrange("(c p) f -> p c f", p=128))
        P.wait_q("pe", "gps")
        P.wait("pe", self.DVE, self.DVE.n)
        wbq = self.big_b[:, 0:NC * D]
        wbk = self.big_b[:, NC * D:2 * NC * D]
        ev_q = [0, 0]
        ev_k = [0, 0]
        st_q = [None, None]
        st_k = [None, None]
        slots_q = (self.qk_bf[:, 0:D], self.pt[:, 0:D])
        slots_k = (self.pt[:, 1024:2048], self.vprod[:, 0:D])
        for tt in range(NT):
            for which, wb, pwi, ev_ms, st_ms, slots, ddst in (
                (0, wbq, tt % 2, ev_q, st_q, slots_q, self.qhat_d),
                (1, wbk, 2 + tt % 2, ev_k, st_k, slots_k, self.khat_d),
            ):
                pw = self.pwide[pwi]
                b0, b1 = pw[:, 0:512], pw[:, 512:1024]
                P.wait("pe", self.ACT, ev_ms[tt % 2])
                for fc in range(NC):
                    for half, bank in ((0, b0), (1, b1)):
                        P.pe(lambda e, tt=tt, fc=fc, half=half, bank=bank, wb=wb: e.matmul(
                            bank[:, 0:512],
                            self.reg_a[:, fc * S + tt * 128:fc * S + (tt + 1) * 128],
                            wb[:, fc * D + half * 512:fc * D + (half + 1) * 512],
                            start=(fc == 0), stop=(fc == NC - 1)),
                            inc=(fc == NC - 1 and half == 1))
                ev_ms[tt % 2] = self._emit_rms_rope(tt, pw, ddst, st_ms, which,
                                                    slots[tt % 2])
        P.wait("gps", self.PE, self.PE.n)
        P.dma("gps", self.big_b[:, 0:NC * D].rearrange("p (c f) -> p c f", f=D),
              self.w["attn_wvT"].rearrange("(c p) f -> p c f", p=128))
        P.wait_q("pe", "gps")
        P.wait("dve", self.GPS, self.GPS.n)
        P.wait_q("dve", "sp")
        P.wait("pe", self.ACT, self.ACT.n)
        wb = self.big_b[:, 0:NC * D]
        vev = [0, 0]
        vst = [None, None]
        for tt in range(NT):
            b0, b1 = self.pb[0 + 2 * (tt % 2)], self.pb[1 + 2 * (tt % 2)]
            P.wait("pe", self.DVE, vev[tt % 2])
            for fc in range(NC):
                for half, bank in ((0, b0), (1, b1)):
                    P.pe(lambda e, tt=tt, fc=fc, half=half, bank=bank, wb=wb: e.matmul(
                        bank[:, 0:512],
                        self.reg_a[:, fc * S + tt * 128:fc * S + (tt + 1) * 128],
                        wb[:, fc * D + half * 512:fc * D + (half + 1) * 512],
                        start=(fc == 0), stop=(fc == NC - 1)),
                        inc=(fc == NC - 1 and half == 1))
            P.wait("dve", self.PE, self.PE.n)
            vp = (self.vprod if tt % 2 == 0 else self.vstrips)[:, 0:1032]
            if vst[tt % 2] is not None:
                P.wait("dve", vst[tt % 2][0], vst[tt % 2][1])
            P.dve(lambda e, vp=vp: e.memset(
                vp.rearrange("p (h c) -> p h c", h=NH)[:, :, 128:129], 1.0))
            P.dve(lambda e, b0=b0, vp=vp: e.tensor_copy(
                vp.rearrange("p (h c) -> p h c", h=NH)[:, 0:4, 0:128],
                b0.rearrange("p (h c) -> p h c", h=4)[:, :, :]))
            dv = P.dve(lambda e, b1=b1, vp=vp: e.tensor_copy(
                vp.rearrange("p (h c) -> p h c", h=NH)[:, 4:8, 0:128],
                b1.rearrange("p (h c) -> p h c", h=4)[:, :, :]))
            vev[tt % 2] = dv
            P.wait("sp", self.DVE, dv)
            vst[tt % 2] = P.dma("sp", self.vhat_d.rearrange("(t p) c -> t p c", p=128)[tt], vp)
        self.tap(f"vtm_attn_{b}", self.vstrips[:, :])
        self.tapd(f"qhatd_{b}", self.qhat_d[:, :], [S, D], BF16)
        self.tapd(f"khatd_{b}", self.khat_d[:, :], [S, D], BF16)
        self.tapd(f"vhatd_{b}", self.vhat_d[:, :], [S, NH * 129], BF16)

    def _emit_rms_rope(self, tt, pw, ddst, st_ms, wi, qslot):
        P = self
        if wi == 0:
            EV, ES, ev = P.dve, self.DVE, "dve"
            qf = self.scr[:, 0:D]
            sq = self.scr2[:, 0:D]
            sb = self.scr2.bitcast(BF16)
            qbf = sb[:, 0:1024].rearrange("p (h d) -> p h d", d=64)
            t1 = sb[:, 1024:1536].rearrange("p (h d) -> p h d", d=32)
            t2 = sb[:, 1536:2048].rearrange("p (h d) -> p h d", d=32)
            ms = self.stats[:, 152:168]
            lno = self.stats[:, 136:152]
            rstd = self.stats[:, 168:184]
        else:
            EV, ES, ev = P.gps, self.GPS, "gps"
            zbuf = (self.zstrip if tt % 2 == 0 else self.vstrips)
            zf = zbuf.bitcast(F32)
            qf = zf[:, 0:D]
            sq = zf[:, 1024:2048]
            sb = zbuf
            qbf = sb[:, 2048:3072].rearrange("p (h d) -> p h d", d=64)
            t1 = sb[:, 3072:3584].rearrange("p (h d) -> p h d", d=32)
            t2 = sb[:, 3584:4096].rearrange("p (h d) -> p h d", d=32)
            ms = self.stats[:, 256:272]
            lno = self.stats[:, 272:288]
            rstd = self.stats[:, 288:304]
            P.wait("act", self.GPS, self.krope_ms[tt % 2])
        P.wait("act", self.PE, self.PE.n)
        if wi == 0:
            P.wait("act", self.DVE, self.DVE.n)
        P.act(lambda e, pw=pw, qf=qf: e.copy(qf, pw[:, 0:1024]))
        av = P.act(lambda e, pw=pw, sq=sq: e.activation(out=sq, in_=pw[:, 0:1024], func=AF.Square))
        evac_av = av
        P.wait("dve", self.ACT, av)
        P.dve(lambda e, sq=sq, ms=ms: e.reduce_sum(ms, sq.rearrange("p (h d) -> p h d", d=64), axis=AX.X), inc=False)
        dv = P.dve(lambda e, ms=ms: e.tensor_scalar(ms, ms, 1.0 / 64.0, None, ALU.mult))
        P.wait("act", self.DVE, dv)
        if wi == 1:
            P.wait("act", self.GPS, self.GPS.n)
        P.act(lambda e, ms=ms, lno=lno: e.activation(out=lno, in_=ms, func=AF.Ln,
                                                     bias=self.rmseps_col))
        av = P.act(lambda e, lno=lno, rstd=rstd: e.activation(out=rstd, in_=lno,
                                                              func=AF.Exp, scale=-0.5, bias=self.lnzero_col))
        P.wait(ev, self.ACT, av)
        qv = qf.rearrange("p (h d) -> p h d", d=64)
        EV(lambda e, qv=qv, rstd=rstd, qbf=qbf: e.tensor_tensor(
            qbf, qv, rstd[:, :, None].broadcast_to([128, 16, 64]), ALU.mult), inc=False)
        cosb = self.cosb_bf[:, tt * 32:(tt + 1) * 32][:, None, :].broadcast_to([128, 16, 32])
        sinb = self.sinb_bf[:, tt * 32:(tt + 1) * 32][:, None, :].broadcast_to([128, 16, 32])
        qb = qslot.rearrange("p (h d) -> p h d", d=64)
        q1, q2 = qbf[:, :, 0:32], qbf[:, :, 32:64]
        if st_ms[tt % 2] is not None:
            P.wait(ev, st_ms[tt % 2][0], st_ms[tt % 2][1])
        EV(lambda e: e.tensor_tensor(t1, q1, cosb, ALU.mult), inc=False)
        EV(lambda e: e.tensor_tensor(t2, q2, sinb, ALU.mult), inc=False)
        EV(lambda e: e.tensor_tensor(qb[:, :, 0:32], t1, t2, ALU.add), inc=False)
        EV(lambda e: e.tensor_tensor(t1, q2, cosb, ALU.mult), inc=False)
        EV(lambda e: e.tensor_tensor(t2, q1, sinb, ALU.mult), inc=False)
        dv = EV(lambda e: e.tensor_tensor(qb[:, :, 32:64], t1, t2, ALU.subtract))
        if wi == 1:
            self.krope_ms[tt % 2] = dv
        P.wait("sp", ES, dv)
        st_ms[tt % 2] = P.dma("sp", ddst.rearrange("(t p) f -> t p f", p=128)[tt], qslot)
        return evac_av

    def emit_attn_core(self, b):
        P = self
        def issue_pair_loads(p):
            s = p % 2
            a = P.dmat("sp", self.qk_strm[:, s * 4096: s * 4096 + 2048],
                       self.qhat_d[:, p * 128:(p + 1) * 128])
            b2 = P.dmat("sp", self.qk_strm[:, s * 4096 + 2048: s * 4096 + 4096],
                        self.khat_d[:, p * 128:(p + 1) * 128])
            c = P.dma("sp", self.vstrips[:, s * 2080: s * 2080 + 2064].rearrange(
                          "p (t c) -> p t c", c=129),
                      self.vhat_d.rearrange("(t p) c -> p t c", p=128)[:, :, p * 129:(p + 1) * 129])
            return [a, b2, c]
        ready = {0: issue_pair_loads(0)}
        pe_after_pair = {}
        fixup_ms = [0, 0]
        exp_ms = [0, 0]
        av_ms = {}
        store_ms = None
        gbase = 0
        for p in range(NH):
            s = p % 2
            if p + 1 < NH:
                P.wait("sp", self.PE, pe_after_pair.get(p - 1, 0))
                ready[p + 1] = issue_pair_loads(p + 1)
            for sem_v in ready[p]:
                P.wait("pe", sem_v[0], sem_v[1])
            qT = self.qk_strm[:, s * 4096: s * 4096 + 2048]
            kT = self.qk_strm[:, s * 4096 + 2048: s * 4096 + 4096]
            vst = self.vstrips[:, s * 2080: s * 2080 + 2064]
            first_fix_of_pair = True
            tasks = [(qh, g) for qh in range(8) for g in range(qh + 1)]
            qst = {}
            for qh in range(8):
                o = qh % 2
                n_out = [0, 0]
                for kt in range(2 * qh + 2):
                    for qi in range(2):
                        if 2 * qh + qi >= kt:
                            n_out[0] += 1
                            n_out[1] += 1
                qst[qh] = {"obs": (self.pb[2 * o], self.pb[2 * o + 1]),
                           "oput": [0, 0], "n_out": n_out}

            def emit_scores(i):
                qh, g = tasks[i]
                gi = gbase + i
                kts = (2 * g, 2 * g + 1)
                pw = self.pbig[gi % 2]
                P.wait("pe", self.ACT, exp_ms[gi % 2])
                diag = [(ki, qi) for ki, kt in enumerate(kts)
                        for qi in range(2) if 2 * qh + qi == kt]
                for hi in range(2):
                    last_of_all = (hi == 1)
                    for ki, kt in enumerate(kts):
                        is_last = (not diag) and (ki == 1)
                        P.pe(lambda e, kt=kt, ki=ki, hi=hi, pw=pw, qh=qh,
                             qT=qT, kT=kT, is_last=is_last: e.matmul(
                            pw[:, hi * 512 + ki * 256:hi * 512 + (ki + 1) * 256],
                            kT[hi * 64:(hi + 1) * 64, kt * 128:(kt + 1) * 128],
                            qT[hi * 64:(hi + 1) * 64, qh * 256:(qh + 1) * 256],
                            start=(ki == 0), stop=is_last,
                            tile_position=(hi * 64, 0)),
                            inc=(is_last and last_of_all))
                    for di, (ki, qi) in enumerate(diag):
                        is_last = (di == len(diag) - 1)
                        P.pe(lambda e, pw=pw, hi=hi, ki=ki, qi=qi, is_last=is_last: e.matmul(
                            pw[:, hi * 512 + ki * 256 + qi * 128:hi * 512 + ki * 256 + (qi + 1) * 128],
                            self.ident_bf, self.diag_bf,
                            start=False, stop=is_last),
                            inc=(is_last and last_of_all))
                return self.PE.n

            def emit_exp(i, pe_ms):
                gi = gbase + i
                pw = self.pbig[gi % 2]
                P.wait("act", self.PE, pe_ms)
                P.wait("act", self.PE, av_ms.get(("pt", gi % 4), 0))
                P.act(lambda e, pw=pw, gi=gi: e.activation(
                    out=self.big_b[:, (gi % 4) * 1024:(gi % 4) * 1024 + 1024],
                    in_=pw[:, 0:1024], func=AF.Exp, scale=0.125, bias=self.neg8_col))
                exp_ms[gi % 2] = self.ACT.n
                return self.ACT.n

            def emit_av(i, act_ms):
                qh, g = tasks[i]
                gi = gbase + i
                sq = qst[qh]
                obs, oput, n_out = sq["obs"], sq["oput"], sq["n_out"]
                kts = (2 * g, 2 * g + 1)
                P.wait("pe", self.ACT, act_ms)
                if oput[0] == 0:
                    P.wait("pe", self.DVE, fixup_ms[qh % 2])
                for hi in range(2):
                    ob = obs[hi]
                    for ki, kt in enumerate(kts):
                        for qi in range(2):
                            if 2 * qh + qi < kt:
                                continue
                            first = (oput[hi] == 0)
                            oput[hi] += 1
                            last = (oput[hi] == n_out[hi])
                            P.pe(lambda e, hi=hi, gi=gi, ki=ki, kt=kt, qi=qi, ob=ob,
                                 vst=vst, first=first, last=last: e.matmul(
                                ob[:, qi * 129:(qi + 1) * 129],
                                self.big_b[:, (gi % 4) * 1024 + hi * 512 + ki * 256 + qi * 128:
                                           (gi % 4) * 1024 + hi * 512 + ki * 256 + (qi + 1) * 128],
                                vst[:, kt * 129:(kt + 1) * 129],
                                start=first, stop=last), inc=last)
                av_ms[("pt", gi % 4)] = self.PE.n
                return self.PE.n

            M = len(tasks)
            sc_ms = {0: emit_scores(0)}
            if M > 1:
                sc_ms[1] = emit_scores(1)
            e_ms = {0: emit_exp(0, sc_ms[0])}
            for i in range(M):
                if i + 1 < M:
                    e_ms[i + 1] = emit_exp(i + 1, sc_ms[i + 1])
                pe_av = emit_av(i, e_ms[i])
                if i + 2 < M:
                    sc_ms[i + 2] = emit_scores(i + 2)
                qh, g = tasks[i]
                if g != qh:
                    continue
                o = qh % 2
                obs = qst[qh]["obs"]
                P.wait("dve", self.PE, pe_av)
                if first_fix_of_pair and store_ms is not None:
                    P.wait("dve", store_ms[0], store_ms[1])
                    first_fix_of_pair = False
                ob1, ob2 = obs
                st = self.stats
                P.dve(lambda e, ob1=ob1: e.reciprocal(
                    st[:, 184:186], ob1[:, 0:258].rearrange("p (a c) -> p a c", c=129)[:, :, 128:129]))
                P.dve(lambda e, ob2=ob2: e.reciprocal(
                    st[:, 186:188], ob2[:, 0:258].rearrange("p (a c) -> p a c", c=129)[:, :, 128:129]))
                P.dve(lambda e: e.tensor_scalar(st[:, 188:190], st[:, 186:188],
                                                self.neglam_col, None, ALU.mult))
                for qi in range(2):
                    qt = 2 * qh + qi
                    zraw = self.zstrip[:, 2048 + qt * 128: 2048 + (qt + 1) * 128]
                    P.dve(lambda e, ob1=ob1, qi=qi: e.tensor_scalar(
                        self.scr2[:, qi * 128:(qi + 1) * 128],
                        ob1[:, qi * 129:qi * 129 + 128], st[:, 184 + qi:185 + qi], None, ALU.mult))
                    P.dve(lambda e, ob2=ob2, qi=qi, zraw=zraw: e.scalar_tensor_tensor(
                        zraw, ob2[:, qi * 129:qi * 129 + 128], st[:, 188 + qi:189 + qi],
                        self.scr2[:, qi * 128:(qi + 1) * 128], ALU.mult, ALU.add))
                    P.dve(lambda e, zraw=zraw: e.bn_stats(st[:, 136:142], zraw))
                    P.dve(lambda e, qt=qt: e.bn_aggr(st[:, 192 + 2 * qt:194 + 2 * qt], st[:, 136:142]))
                fixup_ms[o] = self.DVE.n
            gbase += M
            dv = self.DVE.n
            P.wait("act", self.DVE, dv)
            varv = self.stats[:, 192:224].rearrange("p (a b) -> p a b", b=2)[:, :, 1:2]
            P.act(lambda e, varv=varv: e.activation(
                out=self.stats[:, 136:152], in_=varv, func=AF.Ln, bias=self.lneps_col))
            av2 = P.act(lambda e: e.activation(
                out=self.stats[:, 224:240], in_=self.stats[:, 136:152],
                func=AF.Exp, scale=-0.5, bias=self.lnomli_col))
            P.wait("dve", self.ACT, av2)
            for qt in range(NT):
                P.dve(lambda e, qt=qt: e.tensor_scalar(
                    self.zstrip[:, qt * 128:(qt + 1) * 128],
                    self.zstrip[:, 2048 + qt * 128: 2048 + (qt + 1) * 128],
                    self.stats[:, 192 + 2 * qt:193 + 2 * qt],
                    self.stats[:, 224 + qt:225 + qt], ALU.subtract, ALU.mult))
            pe_after_pair[p] = self.PE.n
            P.wait("sp", self.DVE, self.DVE.n)
            store_ms = P.dma("sp", self.zhat_d.rearrange("(h t p) d -> h p t d", h=NH, p=128)[p],
                             self.zstrip[:, 0:2048].rearrange("p (t d) -> p t d", d=128))

    def emit_attn_out(self, b, fuse_ln=None):
        P = self
        self.tapd(f"zhatd_{b}", self.zhat_d[:, :], [NH * S, 128], BF16)
        P.wait_q("sp", "sp")
        for c in range(NC):
            P.dmat("sp", self.big_b[:, c * S:(c + 1) * S],
                   self.zhat_d.rearrange("(h t p) d -> h (t p) d", h=NH, p=128)[c])
        P.dma("gps", self.wselg[:, 0:8192].rearrange("p (c f) -> p c f", f=D), self.w["attn_wpT"].rearrange("(c p) f -> p c f", p=128))
        if getattr(self, "_prefetch_ffn_w", False):
            P.wait("gps", self.PE, self.PE.n)
            P.dma("gps", self.qk_strm[:, 0:8192].rearrange("p (c f) -> p c f", f=D),
                  self.w["ffn_wqT"].rearrange("(c p) f -> p c f", p=128))
        P.wait_q("pe", "sp")
        P.wait_q("pe", "gps")
        wp = self.wselg[:, 0:8192]
        res_ms = [0, 0]
        for tt in range(NT):
            b0, b1 = self.pb[0 + 2 * (tt % 2)], self.pb[1 + 2 * (tt % 2)]
            P.wait("pe", self.DVE, res_ms[tt % 2])
            for fc in range(NC):
                for half, bank in ((0, b0), (1, b1)):
                    P.pe(lambda e, tt=tt, fc=fc, half=half, bank=bank: e.matmul(
                        bank[:, 0:512],
                        self.big_b[:, fc * S + tt * 128:fc * S + (tt + 1) * 128],
                        wp[:, fc * D + half * 512:fc * D + (half + 1) * 512],
                        start=(fc == 0), stop=(fc == NC - 1)),
                        inc=(fc == NC - 1 and half == 1))
            P.wait("dve", self.PE, self.PE.n)
            for half, bank in ((0, b0), (1, b1)):
                xsl = self.x_res[:, tt * D + half * 512: tt * D + (half + 1) * 512]
                P.dve(lambda e, bank=bank, xsl=xsl: e.tensor_tensor(
                    xsl, xsl, bank[:, 0:512], ALU.add))
            res_ms[tt % 2] = self.DVE.n
            if fuse_ln is not None:
                self._ln_tile(tt, with_router=(fuse_ln == "router"))
        if fuse_ln is not None:
            self._ln_finish(with_router=(fuse_ln == "router"))

    def emit_attn(self, b, skip_ln=False, next_ln=None, w_prefetched=False):
        if not skip_ln:
            self.barrier()
            self.emit_ln(b, with_router=False, tag=f"attn_{b}")
        P = self
        P.wait("pe", self.DVE, self.DVE.n)
        P.wait_q("pe", "act")
        self.emit_attn_qkv(b, w_prefetched=w_prefetched)
        P.wait_q("sp", "sp")
        P.wait("dve", self.GPS, self.GPS.n)
        self.emit_attn_core(b)
        P.wait_q("sp", "sp")
        P.wait("sp", self.PE, self.PE.n)
        P.wait("act", self.PE, self.PE.n)
        self.emit_attn_out(b, fuse_ln=next_ln)
        self.barrier()

    def emit_all(self, phases=None):
        import os
        if phases is None:
            phases = int(os.environ.get("MK_PHASES", "99"))
        self.emit_setup()
        self.ystore_ms = {}
        if phases >= 99:
            for b in range(NB):
                prev = dict(self.ystore_ms) if b > 0 else None
                self.ystore_ms = {}
                self.emit_load_ln(b, prev_stores=prev)
                self._prefetch_attn_w = True
                self.emit_ffn(b, 0, skip_ln=True, next_ln="plain")
                self._prefetch_attn_w = False
                self._prefetch_ffn_w = True
                self.emit_attn(b, skip_ln=True, next_ln="router", w_prefetched=True)
                self._prefetch_ffn_w = False
                self.emit_ffn(b, 1, skip_ln=True, wq_prefetched=True, fuse_store=True)
            self.barrier()
            return
        for b in range(NB):
            self.barrier()
            self.emit_load_x(b)
            self.barrier()
            if phases >= 1:
                self.emit_ln(b, with_router=True, tag=f"ffn0_{b}")
                self.dma("gps", self.big_b[:, 0:NC * D].rearrange("p (c f) -> p c f", f=D),
                         self.w["ffn_wqT"].rearrange("(c p) f -> p c f", p=128))
            if phases >= 2:
                self.emit_router(b, 0)
            if phases >= 3:
                self.wait_q("pe", "gps")
                self.emit_wfold(0, "ffn_bq", 0)
                self.wait("gps", self.PE, self.PE.n)
                self.dma("gps", self.big_b[:, 0:NC * D].rearrange("p (c f) -> p c f", f=D),
                         self.w["ffn_wkT"].rearrange("(c p) f -> p c f", p=128))
                self.wait_q("pe", "gps")
                self.emit_wfold(1, "ffn_bk", 2)
                self.wait("gps", self.PE, self.PE.n)
                self.dma("gps", self.big_b[:, 0:NC * D].rearrange("p (c f) -> p c f", f=D),
                         self.w["ffn_wvT"].rearrange("(c p) f -> p c f", p=128))
                self.wait_q("pe", "gps")
                self.emit_wfold(2, "ffn_bv", 4)
                self.tap(f"wsel_{b}_0", self.wsel[:, :])
            if phases >= 4:
                self.barrier()
                self.emit_ffn_qkv(b, 0)
            if phases >= 5:
                self.barrier()
                self.emit_ffn_attn(b, 0)
            if phases >= 6:
                self.barrier()
                self.emit_ffn_out(b, 0)
                self.barrier()
            self.emit_store_y(b)
            break
        self.barrier()

    def finalize(self):
        nc = self.nc
        with nc.Block() as block:
            @block.tensor
            def _(t):
                for fn in self.q["pe"]:
                    fn(t)

            @block.scalar
            def _(s):
                for fn in self.q["act"]:
                    fn(s)

            @block.vector
            def _(v):
                for fn in self.q["dve"]:
                    fn(v)

            @block.gpsimd
            def _(g):
                for fn in self.q["gps"]:
                    fn(g)

            @block.sync
            def _(sy):
                for fn in self.q["sp"]:
                    fn(sy)
        return nc


_PROG_CACHE = {}


def build(taps=()):
    key = tuple(sorted(taps))
    if key in _PROG_CACHE:
        return _PROG_CACHE[key]
    mk = MK(taps)
    mk.emit_all()
    nc = mk.finalize()
    _PROG_CACHE[key] = (nc, mk)
    return nc, mk


def make_inputs(inputs, core):
    x = np.ascontiguousarray(np.asarray(inputs["L0_attn_x"])[2 * core:2 * core + 2].reshape(NB * S, D))
    t = lambda a: np.ascontiguousarray(np.asarray(a).T)
    m = {
        "x": x,
        "attn_wqT": t(inputs["L0_attn_Wq"]),
        "attn_wkT": t(inputs["L0_attn_Wk"]),
        "attn_wvT": t(inputs["L0_attn_Wv"]),
        "attn_wpT": t(inputs["L0_attn_Wproj"]),
        "ffn_wqT": t(inputs["L0_ffn_Wq"]),
        "ffn_wkT": t(inputs["L0_ffn_Wk"]),
        "ffn_wvT": t(inputs["L0_ffn_Wv"]),
        "ffn_wr": np.asarray(inputs["L0_ffn_Wr"]),
        "ffn_br": np.asarray(inputs["L0_ffn_br"]),
        "ffn_bq": np.asarray(inputs["L0_ffn_bq"]),
        "ffn_bk": np.asarray(inputs["L0_ffn_bk"]),
        "ffn_bv": np.asarray(inputs["L0_ffn_bv"]),
        "ffn_wo": np.asarray(inputs["L0_ffn_Wo"]),
        "ffn_bo": np.asarray(inputs["L0_ffn_bo"]),
        "lq1": np.asarray(inputs["L0_attn_lq1"]),
        "lk1": np.asarray(inputs["L0_attn_lk1"]),
        "lq2": np.asarray(inputs["L0_attn_lq2"]),
        "lk2": np.asarray(inputs["L0_attn_lk2"]),
    }
    return {k: np.ascontiguousarray(np.asarray(v, dtype=np.float32)) for k, v in m.items()}


def kernel(**inputs):
    nc, mk = build()
    in_maps = [make_inputs(inputs, c) for c in range(8)]
    res = run_bass_kernel_spmd(nc, in_maps, core_ids=list(range(8)))
    out = np.stack([r["y"].reshape(NB, S, D) for r in res.results])
    return out.reshape(16, S, D).astype(np.float32)


# version 19 (final)
# speedup vs baseline: 1.3605x; 1.0092x over previous
import math
import numpy as np
import ml_dtypes
import concourse.bass as bass
import concourse.mybir as mybir
from concourse.bass_utils import run_bass_kernel_spmd

F32 = mybir.dt.float32
BF16 = mybir.dt.bfloat16
AF = mybir.ActivationFunctionType
ALU = mybir.AluOpType
AX = mybir.AxisListType

S = 2048
D = 1024
NT = 16
NC = 8
NB = 2
HD = 64
NH = 8
FK = 4
FDH = 64
LN_EPS = 1e-5
RMS_EPS = float(np.finfo(np.float32).eps)
LAMBDA_INIT = 0.8 - 0.6 * math.exp(-0.3 * 4)
ONE_MINUS_LI = 1.0 - LAMBDA_INIT
NEG_BIG = -30000.0


def _rope_tables():
    dim = HD
    inv_freq = 1.0 / 10000.0 ** (np.arange(0, dim, 2, dtype=np.float32) / dim)
    t = np.arange(S, dtype=np.float32)
    freqs = np.outer(t, inv_freq)
    return np.cos(freqs).astype(np.float32), np.sin(freqs).astype(np.float32)


def _consts():
    c = {}
    cos, sin = _rope_tables()
    c["rope_cos"] = cos
    c["rope_sin"] = sin
    c["ident_bf"] = np.eye(128, dtype=np.float32).astype(ml_dtypes.bfloat16)
    c["ident_f32"] = np.eye(128, dtype=np.float32)
    m = np.zeros((128, 128), dtype=np.float32)
    for k in range(128):
        m[k, :k] = NEG_BIG
    c["diag_bf"] = m.astype(ml_dtypes.bfloat16)
    e = np.zeros((16, NC * 128), dtype=np.float32)
    for cc in range(NC):
        for p in range(128):
            e[2 * cc + p // 64, cc * 128 + p] = 1.0
    c["e64_bf"] = e.astype(ml_dtypes.bfloat16)
    ik = np.zeros((128, FK * FDH), dtype=np.float32)
    for p in range(128):
        for k in range(FK):
            ik[p, k * FDH + (p % 64)] = 1.0
    c["ikron_bf"] = ik.astype(ml_dtypes.bfloat16)
    c["kcols"] = np.tile(np.arange(FK, dtype=np.float32)[None, :], (16, 1))
    lt = np.zeros((16, 16), dtype=np.float32)
    for p in range(16):
        lt[p, p + 1:] = 1.0
    c["lt16"] = lt
    return c


class Sem:
    def __init__(self, handle):
        self.h = handle
        self.n = 0

    def inc(self, k=1):
        self.n += k
        return self.n


class MK:
    ENGINES = ("pe", "act", "dve", "gps", "sp")

    def __init__(self, taps=()):
        self.taps = set(taps)
        nc = self.nc = bass.Bass("TRN2", target_bir_lowering=False)
        self.q = {e: [] for e in self.ENGINES}
        self.tap_shapes = {}
        C = _consts()

        self.x_io = nc.dram_tensor("x", [NB * S, D], F32, kind="ExternalInput").ap()
        self.y_io = nc.dram_tensor("y", [NB * S, D], F32, kind="ExternalOutput").ap()
        self.w = {}
        for name, shape in [
            ("attn_wqT", [D, D]), ("attn_wkT", [D, D]), ("attn_wvT", [D, D]),
            ("attn_wpT", [D, D]),
            ("ffn_wqT", [D, D]), ("ffn_wkT", [D, D]), ("ffn_wvT", [D, D]),
            ("ffn_wr", [D, 16]), ("ffn_br", [16]),
            ("ffn_bq", [D]), ("ffn_bk", [D]), ("ffn_bv", [D]),
            ("ffn_wo", [256, D]), ("ffn_bo", [D]),
            ("lq1", [HD]), ("lk1", [HD]), ("lq2", [HD]), ("lk2", [HD]),
        ]:
            self.w[name] = nc.dram_tensor(name, shape, F32, kind="ExternalInput").ap()
        self.cst = {k: nc.inline_tensor(np.asarray(v), f"c_{k}").ap() for k, v in C.items()}

        self.lnx_d = nc.dram_tensor("lnx_d", [S, D], BF16, kind="Internal").ap()
        self.qhat_d = nc.dram_tensor("qhat_d", [S, D], BF16, kind="Internal").ap()
        self.khat_d = nc.dram_tensor("khat_d", [S, D], BF16, kind="Internal").ap()
        self.vhat_d = nc.dram_tensor("vhat_d", [S, NH * 129], BF16, kind="Internal").ap()
        self.zhat_d = nc.dram_tensor("zhat_d", [NH * S, 128], BF16, kind="Internal").ap()
        self.zffn_d = nc.dram_tensor("zffn_d", [S, 256], BF16, kind="Internal").ap()

        sbuf = lambda n, s, d: nc.alloc_sbuf_tensor(n, list(s), d).ap()
        self.x_res = sbuf("x_res", [128, NT * D], F32)
        self.reg_a = sbuf("reg_a", [128, NC * S], BF16)
        self.big_b = sbuf("big_b", [128, NC * S], BF16)
        self.qk_strm = sbuf("qk_strm", [128, 8192], BF16)
        self.wselg = sbuf("wselg", [128, 8192], BF16)
        self.vstrips = sbuf("vstrips", [128, 4160], BF16)
        self.vprod = sbuf("vprod", [128, 1032], BF16)
        self.scr = sbuf("scr", [128, D], F32)
        self.scr2 = sbuf("scr2", [128, D], F32)
        self.qk_bf = sbuf("qk_bf", [128, D], BF16)
        self.pt = sbuf("pt", [128, 2048], BF16)
        self.zstrip = sbuf("zstrip", [128, NT * 256], BF16)
        self.stats = sbuf("stats", [128, 304], F32)
        self.rowbuf = sbuf("rowbuf", [128, 16], F32)
        self.cosb_bf = sbuf("cosb_bf", [128, NT * 32], BF16)
        self.sinb_bf = sbuf("sinb_bf", [128, NT * 32], BF16)
        self.cbf = sbuf("cbf", [128, 648], BF16)
        self.c32 = sbuf("c32", [128, 272], F32)
        self.e64 = sbuf("e64", [16, NC * 128], BF16)
        self.sm = sbuf("sm", [16, 384], F32)
        self.rows = sbuf("rows", [1, 420], F32)
        self.rowsb = sbuf("rowsb", [1, 1024], BF16)

        self.pwide = [nc.alloc_psum_tensor(f"pw{i}", [128, 1024], F32).ap() for i in range(4)]
        self.pb = [self.pwide[i // 2][:, (i % 2) * 512:(i % 2) * 512 + 512] for i in range(8)]
        self.pbig = [self.pwide[2], self.pwide[3]]

        self.qsem = {
            "sp": [Sem(nc.alloc_semaphore(f"d_sp{i}")) for i in range(8)],
            "gps": [Sem(nc.alloc_semaphore(f"d_gp{i}")) for i in range(4)],
            "act": [Sem(nc.alloc_semaphore(f"d_ac{i}")) for i in range(4)],
        }
        self.qrr = {"sp": 0, "gps": 0, "act": 0}
        self.PE = Sem(nc.alloc_semaphore("s_pe"))
        self.ACT = Sem(nc.alloc_semaphore("s_act"))
        self.DVE = Sem(nc.alloc_semaphore("s_dve"))
        self.GPS = Sem(nc.alloc_semaphore("s_gps"))
        self.ALLSEM = ([self.PE, self.ACT, self.DVE, self.GPS]
                       + self.qsem["sp"] + self.qsem["gps"] + self.qsem["act"])

        self.ident_bf = self.cbf[:, 0:128]
        self.diag_bf = self.cbf[:, 128:256]
        self.ikron = self.cbf[:, 256:512]
        self.ones_row_bf = self.cbf[0:1, 516:644]
        self.ident_f32 = self.c32[:, 0:128]
        self.inv2048_col = self.c32[:, 128:129]
        self.lneps_col = self.c32[:, 129:130]
        self.rmseps_col = self.c32[:, 130:131]
        self.one_col_f32 = self.c32[:, 131:132]
        self.neg8_col = self.c32[:, 132:133]
        self.neglam_col = self.c32[:, 133:134]
        self.lnomli_col = self.c32[:, 134:135]
        self.lnzero_col = self.c32[:, 135:136]
        self.ones_row_f32 = self.c32[0:1, 140:268]

        self.wsel = self.wselg[:, 0:6144]
        self.gsel = self.wselg[:, 6144:8192]
        self.qsel = self.qk_strm[:, 0:4096]
        self.ksel = self.qk_strm[:, 4096:8192]

    def on(self, eng, fn):
        self.q[eng].append(fn)

    def wait(self, eng, sem, val):
        if val > 0:
            self.q[eng].append(lambda e, s=sem, v=val: e.wait_ge(s.h, v))

    def barrier_into(self, eng):
        for s in self.ALLSEM:
            self.wait(eng, s, s.n)

    def barrier(self):
        for e in self.ENGINES:
            self.barrier_into(e)

    def _chan(self, queue):
        chans = self.qsem[queue]
        ch = chans[self.qrr[queue] % len(chans)]
        self.qrr[queue] += 1
        return ch

    def dma(self, queue, out, in_, **kw):
        ch = self._chan(queue)
        v0 = ch.n
        self.on(queue, lambda e, o=out, i=in_, s=ch, w=v0, kw=kw: (
            e.wait_ge(s.h, w), e.dma_start(out=o, in_=i, **kw).then_inc(s.h, 16)))
        return ch, ch.inc(16)

    def dmat(self, queue, out, in_):
        ch = self._chan(queue)
        v0 = ch.n
        self.on(queue, lambda e, o=out, i=in_, s=ch, w=v0: (
            e.wait_ge(s.h, w), e.dma_start_transpose(o, i).then_inc(s.h, 16)))
        return ch, ch.inc(16)

    def wait_q(self, eng, queue):
        for ch in self.qsem[queue]:
            self.wait(eng, ch, ch.n)

    def _mk(self, eng, sem, fn, inc):
        w = sem.n

        def step(e, f=fn, s=sem, w=w):
            e.wait_ge(s.h, w)
            f(e).then_inc(s.h, 1)
        self.on(eng, step)
        return sem.inc(1)

    def dve(self, fn, inc=True):
        return self._mk("dve", self.DVE, fn, inc)

    def gps(self, fn, inc=True):
        return self._mk("gps", self.GPS, fn, inc)

    def act(self, fn, inc=True):
        return self._mk("act", self.ACT, fn, inc)

    def pe(self, fn, inc=True):
        if inc:
            self.on("pe", lambda e, f=fn, s=self.PE: f(e).then_inc(s.h, 1))
            return self.PE.inc(1)
        self.on("pe", fn)
        return self.PE.n

    def tap(self, name, ap, dtype=None):
        if name not in self.taps:
            return
        shape = [ap.shape[0], int(np.prod(ap.shape[1:]))]
        dtype = dtype or ap.dtype
        o = self.nc.dram_tensor(f"tap_{name}", shape, dtype, kind="ExternalOutput").ap()
        self.tap_shapes[f"tap_{name}"] = (shape, dtype)
        self.barrier_into("act")
        self.dma("act", o[:, :], ap)

    def tapd(self, name, dram_ap, shape, dtype):
        if name not in self.taps:
            return
        o = self.nc.dram_tensor(f"tap_{name}", list(shape), dtype, kind="ExternalOutput").ap()
        self.tap_shapes[f"tap_{name}"] = (list(shape), dtype)
        self.barrier_into("act")
        self.dma("act", o[:, :], dram_ap)

    def x_tile(self, tt):
        return self.x_res[:, tt * D:(tt + 1) * D]

    def emit_setup(self):
        P, w, cst = self, self.w, self.cst
        P.dma("sp", self.cbf[:, 0:128], cst["ident_bf"][:, :])
        P.dma("sp", self.cbf[:, 128:256], cst["diag_bf"][:, :])
        P.dma("sp", self.cbf[:, 256:512], cst["ikron_bf"][:, :])
        P.dma("sp", self.e64[:, :], cst["e64_bf"][:, :])
        P.dma("sp", self.c32[:, 0:128], cst["ident_f32"][:, :])
        P.dma("gps", self.cosb_bf.rearrange("p (t j) -> p t j", j=32),
              cst["rope_cos"].rearrange("(t p) j -> p t j", p=128))
        P.dma("gps", self.sinb_bf.rearrange("p (t j) -> p t j", j=32),
              cst["rope_sin"].rearrange("(t p) j -> p t j", p=128))
        P.dma("sp", self.sm[:, 0:4], cst["kcols"][:, :])
        P.dma("sp", self.sm[:, 4:20], cst["lt16"][:, :])
        P.dma("sp", self.stats[:, 0:128].rearrange("p (c j) -> p c j", j=16),
              w["ffn_wr"].rearrange("(c p) j -> p c j", p=128))
        self.wr_sb = self.stats[:, 0:128]
        P.dma("sp", self.rows[0:1, 0:64], w["lq1"][None, :])
        P.dma("sp", self.rows[0:1, 64:128], w["lk1"][None, :])
        P.dma("sp", self.rows[0:1, 128:192], w["lq2"][None, :])
        P.dma("sp", self.rows[0:1, 192:256], w["lk2"][None, :])
        P.dma("sp", self.rows[0:1, 256:272], w["ffn_br"][None, :])
        self.wait_q("dve", "sp")
        for ap, v in [
            (self.cbf[:, 512:516], 1.0), (self.cbf[0:1, 516:644], 1.0),
            (self.c32[:, 128:129], 1.0 / 2048.0), (self.c32[:, 129:130], LN_EPS),
            (self.c32[:, 130:131], RMS_EPS), (self.c32[:, 131:132], 1.0),
            (self.c32[:, 132:133], -8.0), (self.c32[0:1, 140:268], 1.0),
            (self.c32[:, 134:135], math.log(ONE_MINUS_LI)), (self.c32[:, 135:136], 0.0),
        ]:
            self.dve(lambda e, a=ap, vv=v: e.memset(a, vv))
        R = self.rows
        self.dve(lambda e: e.tensor_mul(R[0:1, 280:344], R[0:1, 0:64], R[0:1, 64:128]), inc=False)
        self.dve(lambda e: e.tensor_mul(R[0:1, 344:408], R[0:1, 128:192], R[0:1, 192:256]), inc=False)
        self.dve(lambda e: e.reduce_sum(R[0:1, 408:409], R[0:1, 280:344], axis=AX.X), inc=False)
        dv = self.dve(lambda e: e.reduce_sum(R[0:1, 409:410], R[0:1, 344:408], axis=AX.X))
        self.wait("act", self.DVE, dv)
        av = self.act(lambda e: e.activation(out=R[0:1, 412:414], in_=R[0:1, 408:410], func=AF.Exp))
        self.wait("dve", self.ACT, av)
        self.dve(lambda e: e.tensor_sub(R[0:1, 414:415], R[0:1, 412:413], R[0:1, 413:414]), inc=False)
        dv = self.dve(lambda e: e.tensor_scalar(R[0:1, 415:416], R[0:1, 414:415], LAMBDA_INIT, -1.0, ALU.add, ALU.mult))
        self.wait("pe", self.DVE, dv)
        pv = self.pe(lambda e: e.matmul(self.pb[0][:, 0:1], self.ones_row_f32, R[0:1, 415:416], start=True, stop=True))
        self.wait("dve", self.PE, pv)
        dv = self.dve(lambda e: e.tensor_copy(self.c32[:, 133:134], self.pb[0][:, 0:1]))
        self.wait("pe", self.DVE, dv)
        self.tap("lam", self.c32[:, 133:134])

    def emit_load_x(self, b):
        xv = self.x_io.rearrange("(b t p) f -> b p t f", b=NB, p=128)
        return self.dma("sp", self.x_res.rearrange("p (t f) -> p t f", f=D), xv[b])

    def emit_load_ln(self, b, prev_stores=None):
        P = self
        xv = self.x_io.rearrange("(b t p) f -> b t p f", b=NB, p=128)
        for tt in range(NT):
            if prev_stores is not None and prev_stores.get(tt) is not None:
                P.wait("sp", prev_stores[tt][0], prev_stores[tt][1])
            ld = P.dma("sp", self.x_tile(tt), xv[b, tt])
            P.wait("dve", ld[0], ld[1])
            P.wait("act", ld[0], ld[1])
            self._ln_tile(tt, with_router=True)
        self._ln_finish(with_router=True)

    def emit_store_y(self, b):
        yv = self.y_io.rearrange("(b t p) f -> b p t f", b=NB, p=128)
        self.barrier_into("sp")
        return self.dma("sp", yv[b], self.x_res.rearrange("p (t f) -> p t f", f=D))

    def _ln_tile(self, tt, with_router):
        P = self
        st = self.stats
        xt = self.x_tile(tt)
        P.dve(lambda e, xt=xt, tt=tt: e.reduce_sum(st[:, 192 + 2 * tt:193 + 2 * tt], xt, axis=AX.X))
        P.wait("act", self.DVE, self.DVE.n)
        P.act(lambda e, xt=xt, tt=tt: e.activation(
            out=self.scr2[:, 0:D], in_=xt, func=AF.Square,
            accum_out=st[:, 193 + 2 * tt:194 + 2 * tt]))

    def _ln_finish(self, with_router):
        P = self
        st = self.stats
        P.wait("dve", self.ACT, self.ACT.n)
        pairs = st[:, 192:224].rearrange("p (a b) -> p a b", b=2)
        mu = st[:, 224:240]
        m2 = st[:, 240:256]
        P.dve(lambda e: e.tensor_scalar(mu, pairs[:, :, 0:1], 1.0 / D, None, ALU.mult), inc=False)
        P.dve(lambda e: e.tensor_scalar(m2, pairs[:, :, 1:2], 1.0 / D, None, ALU.mult), inc=False)
        P.dve(lambda e: e.tensor_mul(st[:, 136:152], mu, mu), inc=False)
        dv = P.dve(lambda e: e.tensor_sub(m2, m2, st[:, 136:152]))
        P.wait("act", self.DVE, dv)
        P.act(lambda e: e.activation(out=st[:, 136:152], in_=m2, func=AF.Ln, bias=self.lneps_col))
        av = P.act(lambda e: e.activation(out=m2, in_=st[:, 136:152], func=AF.Exp,
                                          scale=-0.5, bias=self.lnzero_col))
        P.wait("dve", self.ACT, av)
        rstd = m2
        st_ms = [None, None, None, None]
        slots4 = (self.qk_bf[:, 0:D], self.pt[:, 0:D], self.pt[:, 1024:2048],
                  self.vprod[:, 0:D])
        for tt in range(NT):
            xt = self.x_tile(tt)
            slot = slots4[tt % 4]
            if st_ms[tt % 4] is not None:
                P.wait("dve", st_ms[tt % 4][0], st_ms[tt % 4][1])
            if with_router:
                P.dve(lambda e, xt=xt, tt=tt: e.tensor_scalar(
                    self.scr[:, 0:D], xt, mu[:, tt:tt + 1], rstd[:, tt:tt + 1],
                    ALU.subtract, ALU.mult), inc=False)
                dv = P.dve(lambda e, slot=slot: e.tensor_copy(slot, self.scr[:, 0:D]))
                P.wait("pe", self.DVE, dv)
                P.pe(lambda e, tt=tt: e.matmul(self.pb[6][0:1, 0:512], self.inv2048_col,
                                               self.scr[:, 0:512], start=(tt == 0), stop=(tt == NT - 1)), inc=False)
                P.pe(lambda e, tt=tt: e.matmul(self.pb[7][0:1, 0:512], self.inv2048_col,
                                               self.scr[:, 512:1024], start=(tt == 0), stop=(tt == NT - 1)))
                P.wait("dve", self.PE, self.PE.n)
                P.wait("sp", self.PE, self.PE.n)
            else:
                P.dve(lambda e, xt=xt, tt=tt, slot=slot: e.tensor_scalar(
                    slot, xt, mu[:, tt:tt + 1], rstd[:, tt:tt + 1],
                    ALU.subtract, ALU.mult))
            P.wait("act", self.DVE, self.DVE.n)
            st_ms[tt % 4] = P.dma("act", self.lnx_d.rearrange("(t p) f -> t p f", p=128)[tt], slot)
        self._ln_transposes()

    def _ln_transposes(self):
        P = self
        P.wait_q("act", "act")
        P.wait_q("act", "sp")
        P.wait("act", self.PE, self.PE.n)
        for c in range(NC):
            P.dmat("act", self.reg_a[:, c * S:(c + 1) * S], self.lnx_d[:, c * 128:(c + 1) * 128])

    def emit_ln(self, b, with_router, tag=""):
        for tt in range(NT):
            self._ln_tile(tt, with_router)
        self._ln_finish(with_router)
        return None

    def emit_router(self, b, ci):
        P = self
        R, SM = self.rows, self.sm
        xmrow = self.scr2[0:1, 0:1024]
        P.wait("act", self.PE, self.PE.n)
        P.act(lambda e: e.copy(xmrow[0:1, 0:512], self.pb[6][0:1, 0:512]), inc=False)
        av = P.act(lambda e: e.copy(xmrow[0:1, 512:1024], self.pb[7][0:1, 0:512]))
        self.tap(f"xmean_{b}_{ci}", xmrow)
        P.wait("pe", self.ACT, av)
        for c in range(NC):
            P.pe(lambda e, c=c: e.transpose(self.pb[5][:, c:c + 1],
                                            xmrow[0:1, c * 128:(c + 1) * 128],
                                            self.ones_row_f32[0:1, 0:1]),
                 inc=(c == NC - 1))
        P.wait("dve", self.PE, self.PE.n)
        dv = P.dve(lambda e: e.tensor_copy(self.stats[:, 136:144], self.pb[5][:, 0:8]))
        P.wait("pe", self.DVE, dv)
        for c in range(NC):
            P.pe(lambda e, c=c: e.matmul(self.pb[4][0:1, 0:16], self.stats[:, 136 + c:137 + c],
                                         self.wr_sb[:, c * 16:(c + 1) * 16],
                                         start=(c == 0), stop=False), inc=False)
        pv = P.pe(lambda e: e.matmul(self.pb[4][0:1, 0:16], self.ones_row_f32[0:1, 0:1],
                                     R[0:1, 256:272], start=False, stop=True))
        P.wait("dve", self.PE, pv)
        dv = P.dve(lambda e: e.tensor_copy(SM[0:1, 128:144], self.pb[4][0:1, 0:16]))
        logits = SM[0:1, 128:144]
        self.tap(f"logits_{b}_{ci}", logits)
        P.dve(lambda e: e.reduce_max(SM[0:1, 144:145], logits, axis=AX.X), inc=False)
        dv = P.dve(lambda e: e.tensor_scalar(SM[0:1, 145:146], SM[0:1, 144:145], -1.0, None, ALU.mult))
        P.wait("act", self.DVE, dv)
        av = P.act(lambda e: e.activation(out=SM[0:1, 148:164], in_=logits, func=AF.Exp,
                                          bias=SM[0:1, 145:146], accum_out=SM[0:1, 146:147]))
        P.wait("dve", self.ACT, av)
        P.dve(lambda e: e.reciprocal(SM[0:1, 147:148], SM[0:1, 146:147]), inc=False)
        dv = P.dve(lambda e: e.tensor_scalar(SM[0:1, 164:180], SM[0:1, 148:164], SM[0:1, 147:148], None, ALU.mult))
        dist = SM[0:1, 164:180]
        self.tap(f"dist_{b}_{ci}", dist)
        P.wait("pe", self.DVE, dv)
        P.pe(lambda e: e.matmul(self.pb[4][0:16, 16:32], self.ones_row_f32[0:1, 0:16], dist,
                                start=True, stop=True), inc=False)
        pv = P.pe(lambda e: e.transpose(self.pb[4][0:16, 32:33], dist, self.ones_row_f32[0:1, 0:1]))
        P.wait("dve", self.PE, pv)
        P.dve(lambda e: e.tensor_copy(SM[:, 20:36], self.pb[4][0:16, 16:32]), inc=False)
        dv = P.dve(lambda e: e.tensor_copy(SM[:, 36:37], self.pb[4][0:16, 32:33]))
        drow, dcol = SM[:, 20:36], SM[:, 36:37]
        dcolb = dcol.broadcast_to([16, 16])
        P.dve(lambda e: e.tensor_tensor(SM[:, 37:53], dcolb, drow, ALU.is_gt), inc=False)
        P.dve(lambda e: e.tensor_tensor(SM[:, 53:69], dcolb, drow, ALU.is_equal), inc=False)
        P.dve(lambda e: e.tensor_mul(SM[:, 53:69], SM[:, 53:69], SM[:, 4:20]), inc=False)
        dv = P.dve(lambda e: e.tensor_add(SM[:, 37:53], SM[:, 37:53], SM[:, 53:69]))
        P.wait("pe", self.DVE, dv)
        pv = P.pe(lambda e: e.matmul(self.pb[4][0:16, 34:35], SM[:, 37:53], self.one_col_f32[0:16, 0:1],
                                     start=True, stop=True))
        P.wait("dve", self.PE, pv)
        dv = P.dve(lambda e: e.tensor_copy(SM[:, 69:70], self.pb[4][0:16, 34:35]))
        rankcol = SM[:, 69:70]
        self.tap(f"rank_{b}_{ci}", rankcol)
        P.dve(lambda e: e.tensor_scalar(SM[:, 70:71], rankcol, float(FK), None, ALU.is_lt), inc=False)
        dv = P.dve(lambda e: e.tensor_mul(SM[:, 71:72], dcol, SM[:, 70:71]))
        P.wait("act", self.DVE, dv)
        av = P.act(lambda e: e.activation(out=SM[:, 72:73], in_=SM[:, 71:72], func=AF.Exp))
        P.wait("pe", self.ACT, av)
        pv = P.pe(lambda e: e.matmul(self.pb[4][0:1, 36:37], SM[:, 72:73], self.one_col_f32[0:16, 0:1],
                                     start=True, stop=True))
        P.wait("dve", self.PE, pv)
        P.dve(lambda e: e.tensor_copy(SM[0:1, 180:181], self.pb[4][0:1, 36:37]), inc=False)
        dv = P.dve(lambda e: e.reciprocal(SM[0:1, 181:182], SM[0:1, 180:181]))
        P.wait("pe", self.DVE, dv)
        pv = P.pe(lambda e: e.matmul(self.pb[4][0:16, 38:39], self.ones_row_f32[0:1, 0:16],
                                     SM[0:1, 181:182], start=True, stop=True))
        P.wait("dve", self.PE, pv)
        P.dve(lambda e: e.tensor_copy(SM[:, 73:74], self.pb[4][0:16, 38:39]), inc=False)
        dv = P.dve(lambda e: e.tensor_mul(SM[:, 74:75], SM[:, 72:73], SM[:, 73:74]))
        gatescol = SM[:, 74:75]
        self.tap(f"gates_{b}_{ci}", gatescol)
        P.dve(lambda e: e.tensor_tensor(SM[:, 75:79], rankcol.broadcast_to([16, 4]), SM[:, 0:4], ALU.is_equal), inc=False)
        P.dve(lambda e: e.tensor_scalar(SM[:, 75:79], SM[:, 75:79], gatescol, None, ALU.mult), inc=False)
        dv = P.dve(lambda e: e.tensor_copy(
            self.sm.bitcast(BF16)[0:16, 512:768].rearrange("p (a b) -> p a b", b=64),
            SM[:, 75:79][:, :, None].broadcast_to([16, 4, 64])))
        bcmp256 = self.sm.bitcast(BF16)[0:16, 512:768]
        P.wait("pe", self.DVE, dv)
        for c in range(NC):
            pv = P.pe(lambda e, c=c: e.matmul(self.pb[3][:, 0:256], self.e64[:, c * 128:(c + 1) * 128],
                                              bcmp256, start=True, stop=True))
            P.wait("dve", self.PE, pv)
            dv = P.dve(lambda e, c=c: e.tensor_tensor(self.gsel[:, c * 256:(c + 1) * 256],
                                                      self.pb[3][:, 0:256], self.ikron, ALU.mult))
            P.wait("pe", self.DVE, dv)
        self.tap(f"G_{b}_{ci}", self.gsel[:, 0:2048])

    def emit_wfold(self, wi, bias_name, bias_slot, wb=None, slot=0):
        P = self
        base = wi * NC * 256
        if wb is None:
            wb = self.big_b[:, slot * NC * D:(slot + 1) * NC * D]
        ev = [0, 0]
        for fs in range(NC):
            bank = self.pb[2 + fs % 2]
            P.wait("pe", self.DVE, ev[fs % 2])
            for mc in range(NC):
                P.pe(lambda e, fs=fs, mc=mc, bank=bank: e.matmul(
                    bank[:, 0:256],
                    wb[:, mc * D + fs * 128: mc * D + (fs + 1) * 128],
                    self.gsel[:, mc * 256:(mc + 1) * 256],
                    start=(mc == 0), stop=(mc == NC - 1)), inc=(mc == NC - 1))
            P.wait("dve", self.PE, self.PE.n)
            P.dve(lambda e, fs=fs, bank=bank: e.tensor_copy(
                self.wsel[:, base + fs * 256: base + (fs + 1) * 256], bank[:, 0:256]))
            ev[fs % 2] = self.DVE.n
        P.wait("pe", self.DVE, self.DVE.n)
        brow = self.scr2[0:1, 0:1024]
        P.wait("sp", self.DVE, self.DVE.n)
        P.wait("sp", self.PE, self.PE.n)
        P.dma("sp", brow, self.w[bias_name][None, :])
        P.wait_q("pe", "sp")
        for c in range(NC):
            P.pe(lambda e, c=c: e.transpose(self.pb[2][:, c:c + 1],
                                            brow[0:1, c * 128:(c + 1) * 128],
                                            self.ones_row_f32[0:1, 0:1]), inc=(c == NC - 1))
        P.wait("dve", self.PE, self.PE.n)
        P.dve(lambda e: e.tensor_copy(self.stats[:, 144:152], self.pb[2][:, 0:8]), inc=False)
        dv = P.dve(lambda e: e.tensor_copy(self.stats.bitcast(BF16)[:, 304:312], self.stats[:, 144:152]))
        bcol_bf = self.stats.bitcast(BF16)[:, 304:312]
        P.wait("pe", self.DVE, dv)
        for h in range(2):
            for mc in range(NC):
                P.pe(lambda e, h=h, mc=mc: e.matmul(
                    self.pb[2][:, 8 + h:9 + h],
                    self.gsel[:, mc * 256 + h * 128: mc * 256 + (h + 1) * 128],
                    bcol_bf[:, mc:mc + 1],
                    start=(mc == 0), stop=(mc == NC - 1)), inc=(h == 1 and mc == NC - 1))
        P.wait("dve", self.PE, self.PE.n)
        dv = P.dve(lambda e, sl=bias_slot: e.tensor_copy(self.rowbuf[:, sl:sl + 2], self.pb[2][:, 8:10]))
        P.wait("pe", self.DVE, dv)

    def emit_ffn_qkv(self, b, ci):
        P = self
        evac_ms = [0, 0]
        P.wait("pe", self.DVE, self.DVE.n)
        for wi, dst, bslot in ((0, self.qsel, 0), (1, self.ksel, 2)):
            base = wi * NC * 256
            for h in range(2):
                for qb in range(4):
                    bank = self.pb[qb % 2]
                    P.wait("pe", self.DVE, evac_ms[qb % 2])
                    for fc in range(NC):
                        P.pe(lambda e, h=h, qb=qb, fc=fc, bank=bank, base=base: e.matmul(
                            bank[:, 0:512],
                            self.wsel[:, base + fc * 256 + h * 128: base + fc * 256 + (h + 1) * 128],
                            self.reg_a[:, fc * S + qb * 512: fc * S + (qb + 1) * 512],
                            start=(fc == 0), stop=(fc == NC - 1)), inc=(fc == NC - 1))
                    P.wait("dve", self.PE, self.PE.n)
                    P.dve(lambda e, h=h, qb=qb, bank=bank, dst=dst, bslot=bslot: e.tensor_scalar(
                        dst[:, h * S + qb * 512:h * S + (qb + 1) * 512],
                        bank[:, 0:512], self.rowbuf[:, bslot + h:bslot + h + 1], None, ALU.add))
                    evac_ms[qb % 2] = self.DVE.n
        self.tap(f"qsel_{b}_{ci}", self.qsel[:, :])
        self.tap(f"ksel_{b}_{ci}", self.ksel[:, :])
        for h in range(2):
            P.pe(lambda e, h=h: e.transpose(self.pb[2][0:1, 16 + h * 128:16 + (h + 1) * 128],
                                            self.rowbuf[:, 4 + h:5 + h], self.ident_f32),
                 inc=(h == 1))
        P.wait("dve", self.PE, self.PE.n)
        dv = P.dve(lambda e: e.tensor_copy(self.rowsb[0:1, 0:256], self.pb[2][0:1, 16:272]))
        P.wait("pe", self.DVE, dv)
        dv = P.dve(lambda e: e.memset(self.vstrips[:, 0:4160], 1.0))
        P.wait("pe", self.DVE, dv)
        vevac_ms = [0, 0]
        for tt in range(NT):
            bank = self.pb[2 + tt % 2]
            P.wait("pe", self.DVE, vevac_ms[tt % 2])
            for fc in range(NC):
                P.pe(lambda e, tt=tt, fc=fc, bank=bank: e.matmul(
                    bank[:, 0:256],
                    self.reg_a[:, fc * S + tt * 128: fc * S + (tt + 1) * 128],
                    self.wsel[:, 2 * NC * 256 + fc * 256: 2 * NC * 256 + (fc + 1) * 256],
                    start=(fc == 0), stop=False), inc=False)
            P.pe(lambda e, bank=bank: e.matmul(bank[:, 0:256], self.ones_row_bf,
                                               self.rowsb[0:1, 0:256], start=False, stop=True))
            P.wait("dve", self.PE, self.PE.n)
            P.dve(lambda e, tt=tt, bank=bank: e.tensor_copy(
                self.vstrips.rearrange("p (h t c) -> p h t c", h=FK, t=NT)[:, :, tt, 0:64],
                bank[:, 0:256].rearrange("p (h c) -> p h c", h=FK)[:, :, :]))
            vevac_ms[tt % 2] = self.DVE.n
        self.tap(f"vffn_{b}_{ci}", self.vstrips[:, 0:4160])

    def emit_ffn_attn(self, b, ci):
        P = self
        fixup_ms = [0, 0]
        exp_ms = [0, 0]
        av_ms = {}
        tasks = [(hp, qh, g) for hp in range(2) for qh in range(8) for g in range(8)]
        bst = {}
        for hp in range(2):
            for qh in range(8):
                o = qh % 2
                bst[(hp, qh)] = {"obs": (self.pb[2 * o], self.pb[2 * o + 1]),
                                 "oput": [0, 0]}
        n_mm_out = 2 * 16

        def emit_scores(i):
            hp, qh, g = tasks[i]
            pw = self.pbig[i % 2]
            P.wait("pe", self.ACT, exp_ms[i % 2])
            for ki, kt in enumerate((2 * g, 2 * g + 1)):
                for hi in range(2):
                    P.pe(lambda e, hp=hp, qh=qh, kt=kt, ki=ki, hi=hi, pw=pw: e.matmul(
                        pw[:, hi * 512 + ki * 256:hi * 512 + (ki + 1) * 256],
                        self.ksel[hi * 64:(hi + 1) * 64, hp * S + kt * 128:hp * S + (kt + 1) * 128],
                        self.qsel[hi * 64:(hi + 1) * 64, hp * S + qh * 256:hp * S + (qh + 1) * 256],
                        start=(ki == 0), stop=(ki == 1),
                        tile_position=(hi * 64, 0)), inc=(ki == 1 and hi == 1))
            return self.PE.n

        def emit_exp(i, pe_ms):
            pw = self.pbig[i % 2]
            P.wait("act", self.PE, pe_ms)
            P.wait("act", self.PE, av_ms.get(i % 4, 0))
            P.act(lambda e, pw=pw, i=i: e.activation(
                out=self.big_b[:, (i % 4) * 1024:(i % 4) * 1024 + 1024],
                in_=pw[:, 0:1024], func=AF.Exp, scale=0.125))
            exp_ms[i % 2] = self.ACT.n
            return self.ACT.n

        def emit_av(i, act_ms):
            hp, qh, g = tasks[i]
            sb = bst[(hp, qh)]
            obs, oput = sb["obs"], sb["oput"]
            P.wait("pe", self.ACT, act_ms)
            if oput[0] == 0:
                P.wait("pe", self.DVE, fixup_ms[qh % 2])
            for hi in range(2):
                ob = obs[hi]
                h = 2 * hp + hi
                for ki in range(2):
                    kt = 2 * g + ki
                    for qi in range(2):
                        first = (oput[hi] == 0)
                        oput[hi] += 1
                        last = (oput[hi] == n_mm_out)
                        P.pe(lambda e, h=h, hi=hi, i=i, ki=ki, kt=kt, qi=qi, ob=ob,
                             first=first, last=last: e.matmul(
                            ob[:, qi * 65:(qi + 1) * 65],
                            self.big_b[:, (i % 4) * 1024 + hi * 512 + ki * 256 + qi * 128:
                                       (i % 4) * 1024 + hi * 512 + ki * 256 + (qi + 1) * 128],
                            self.vstrips[:, h * 1040 + kt * 65: h * 1040 + (kt + 1) * 65],
                            start=first, stop=last), inc=last)
            av_ms[i % 4] = self.PE.n
            return self.PE.n

        M = len(tasks)
        sc_ms = {0: emit_scores(0), 1: emit_scores(1)}
        e_ms = {0: emit_exp(0, sc_ms[0])}
        for i in range(M):
            if i + 1 < M:
                e_ms[i + 1] = emit_exp(i + 1, sc_ms[i + 1])
            pe_av = emit_av(i, e_ms[i])
            if i + 2 < M:
                sc_ms[i + 2] = emit_scores(i + 2)
            hp, qh, g = tasks[i]
            if g != 7:
                continue
            o = qh % 2
            obs = bst[(hp, qh)]["obs"]
            P.wait("dve", self.PE, pe_av)
            for hi in range(2):
                h = 2 * hp + hi
                ob = obs[hi]
                dn = self.stats[:, 160 + 2 * hi:162 + 2 * hi]
                P.dve(lambda e, ob=ob, dn=dn: e.reciprocal(
                    dn, ob[:, 0:130].rearrange("p (a c) -> p a c", c=65)[:, :, 64:65]))
                for qi in range(2):
                    qt = 2 * qh + qi
                    P.dve(lambda e, ob=ob, dn=dn, qi=qi, qt=qt, h=h: e.tensor_scalar(
                        self.zstrip[:, qt * 256 + h * 64: qt * 256 + (h + 1) * 64],
                        ob[:, qi * 65: qi * 65 + 64], dn[:, qi:qi + 1], None, ALU.mult))
            fixup_ms[o] = self.DVE.n
        self.tap(f"zffn_{b}_{ci}", self.zstrip[:, :])

    def emit_ffn_out(self, b, ci, fuse_ln=None, fuse_store=False):
        P = self
        P.wait("sp", self.DVE, self.DVE.n)
        P.dma("sp", self.zffn_d.rearrange("(t p) c -> p t c", p=128),
              self.zstrip.rearrange("p (t c) -> p t c", c=256))
        P.wait_q("sp", "sp")
        zt = self.qk_strm[:, 0:2 * S]
        P.wait("sp", self.PE, self.PE.n)
        for c in range(2):
            P.dmat("sp", zt[:, c * S:(c + 1) * S], self.zffn_d[:, c * 128:(c + 1) * 128])
        P.wait("gps", self.PE, self.PE.n)
        P.dma("gps", self.qk_strm[:, 4096:6144].rearrange("p (c f) -> p c f", f=D), self.w["ffn_wo"].rearrange("(c p) f -> p c f", p=128))
        P.dma("gps", self.rowsb[0:1, 0:1024], self.w["ffn_bo"][None, :])
        wo = self.qk_strm[:, 4096:6144]
        if getattr(self, "_prefetch_attn_w", False):
            P.wait("gps", self.PE, self.PE.n)
            P.dma("gps", self.big_b[:, 0:NC * D].rearrange("p (c f) -> p c f", f=D),
                  self.w["attn_wqT"].rearrange("(c p) f -> p c f", p=128))
            P.dma("gps", self.big_b[:, NC * D:2 * NC * D].rearrange("p (c f) -> p c f", f=D),
                  self.w["attn_wkT"].rearrange("(c p) f -> p c f", p=128))
        P.wait_q("pe", "sp")
        P.wait_q("pe", "gps")
        P.wait("pe", self.DVE, self.DVE.n)
        res_ms = [0, 0]
        for tt in range(NT):
            P.wait("pe", self.DVE, res_ms[tt % 2])
            for half in range(2):
                bank = self.pb[2 * (tt % 2) + half]
                for c in range(2):
                    P.pe(lambda e, tt=tt, half=half, c=c, bank=bank: e.matmul(
                        bank[:, 0:512],
                        zt[:, c * S + tt * 128:c * S + (tt + 1) * 128],
                        wo[:, c * 1024 + half * 512:c * 1024 + (half + 1) * 512],
                        start=(c == 0), stop=False), inc=False)
                P.pe(lambda e, half=half, bank=bank: e.matmul(
                    bank[:, 0:512], self.ones_row_bf,
                    self.rowsb[0:1, half * 512:(half + 1) * 512], start=False, stop=True),
                    inc=(half == 1))
            P.wait("dve", self.PE, self.PE.n)
            for half in range(2):
                bank = self.pb[2 * (tt % 2) + half]
                xsl = self.x_res[:, tt * D + half * 512: tt * D + (half + 1) * 512]
                P.dve(lambda e, bank=bank, xsl=xsl: e.scalar_tensor_tensor(
                    xsl, bank[:, 0:512], 0.5, xsl, ALU.mult, ALU.add))
            res_ms[tt % 2] = self.DVE.n
            if fuse_ln is not None:
                self._ln_tile(tt, with_router=(fuse_ln == "router"))
            if fuse_store:
                P.wait("sp", self.DVE, self.DVE.n)
                yv = self.y_io.rearrange("(b t p) f -> b t p f", b=NB, p=128)
                self.ystore_ms[tt] = P.dma("sp", yv[b, tt], self.x_tile(tt))
        if fuse_ln is not None:
            self._ln_finish(with_router=(fuse_ln == "router"))

    def emit_ffn(self, b, ci, skip_ln=False, next_ln=None, wq_prefetched=False,
                 fuse_store=False):
        P = self
        if not skip_ln:
            self.barrier()
            self.emit_ln(b, with_router=True, tag=f"ffn{ci}_{b}")
        if wq_prefetched:
            P.wait("gps", self.PE, self.PE.n)
            P.dma("gps", self.big_b[:, 0:NC * D].rearrange("p (c f) -> p c f", f=D),
                  self.w["ffn_wkT"].rearrange("(c p) f -> p c f", p=128))
            P.dma("gps", self.big_b[:, NC * D:2 * NC * D].rearrange("p (c f) -> p c f", f=D),
                  self.w["ffn_wvT"].rearrange("(c p) f -> p c f", p=128))
            self.emit_router(b, ci)
            P.wait_q("pe", "gps")
            self.emit_wfold(0, "ffn_bq", 0, wb=self.qk_strm[:, 0:8192])
            self.emit_wfold(1, "ffn_bk", 2, slot=0)
            self.emit_wfold(2, "ffn_bv", 4, slot=1)
        else:
            P.wait("gps", self.PE, self.PE.n)
            P.dma("gps", self.big_b[:, 0:NC * D].rearrange("p (c f) -> p c f", f=D),
                  self.w["ffn_wqT"].rearrange("(c p) f -> p c f", p=128))
            P.dma("gps", self.big_b[:, NC * D:2 * NC * D].rearrange("p (c f) -> p c f", f=D),
                  self.w["ffn_wkT"].rearrange("(c p) f -> p c f", p=128))
            self.emit_router(b, ci)
            P.wait_q("pe", "gps")
            self.emit_wfold(0, "ffn_bq", 0, slot=0)
            P.wait("gps", self.PE, self.PE.n)
            P.dma("gps", self.big_b[:, 0:NC * D].rearrange("p (c f) -> p c f", f=D),
                  self.w["ffn_wvT"].rearrange("(c p) f -> p c f", p=128))
            self.emit_wfold(1, "ffn_bk", 2, slot=1)
            P.wait_q("pe", "gps")
            self.emit_wfold(2, "ffn_bv", 4, slot=0)
        self.tap(f"wsel_{b}_{ci}", self.wsel[:, :])
        P.wait("pe", self.DVE, self.DVE.n)
        P.wait_q("pe", "act")
        self.emit_ffn_qkv(b, ci)
        P.wait("pe", self.DVE, self.DVE.n)
        P.wait("act", self.DVE, self.DVE.n)
        self.emit_ffn_attn(b, ci)
        P.wait("pe", self.DVE, self.DVE.n)
        self.emit_ffn_out(b, ci, fuse_ln=next_ln, fuse_store=fuse_store)
        self.barrier()

    def emit_attn_qkv(self, b, w_prefetched=False):
        P = self
        self.krope_ms = [0, 0]
        if not w_prefetched:
            P.wait("gps", self.PE, self.PE.n)
            P.dma("gps", self.big_b[:, 0:NC * D].rearrange("p (c f) -> p c f", f=D),
                  self.w["attn_wqT"].rearrange("(c p) f -> p c f", p=128))
            P.dma("gps", self.big_b[:, NC * D:2 * NC * D].rearrange("p (c f) -> p c f", f=D),
                  self.w["attn_wkT"].rearrange("(c p) f -> p c f", p=128))
        P.wait_q("pe", "gps")
        P.wait("pe", self.DVE, self.DVE.n)
        wbq = self.big_b[:, 0:NC * D]
        wbk = self.big_b[:, NC * D:2 * NC * D]
        ev_q = [0, 0]
        ev_k = [0, 0]
        st_q = [None, None]
        st_k = [None, None]
        slots_q = (self.qk_bf[:, 0:D], self.pt[:, 0:D])
        slots_k = (self.pt[:, 1024:2048], self.vprod[:, 0:D])
        for tt in range(NT):
            for which, wb, pwi, ev_ms, st_ms, slots, ddst in (
                (0, wbq, tt % 2, ev_q, st_q, slots_q, self.qhat_d),
                (1, wbk, 2 + tt % 2, ev_k, st_k, slots_k, self.khat_d),
            ):
                pw = self.pwide[pwi]
                b0, b1 = pw[:, 0:512], pw[:, 512:1024]
                P.wait("pe", self.ACT, ev_ms[tt % 2])
                for fc in range(NC):
                    for half, bank in ((0, b0), (1, b1)):
                        P.pe(lambda e, tt=tt, fc=fc, half=half, bank=bank, wb=wb: e.matmul(
                            bank[:, 0:512],
                            self.reg_a[:, fc * S + tt * 128:fc * S + (tt + 1) * 128],
                            wb[:, fc * D + half * 512:fc * D + (half + 1) * 512],
                            start=(fc == 0), stop=(fc == NC - 1)),
                            inc=(fc == NC - 1 and half == 1))
                ev_ms[tt % 2] = self._emit_rms_rope(tt, pw, ddst, st_ms, which,
                                                    slots[tt % 2])
        P.wait("gps", self.PE, self.PE.n)
        P.dma("gps", self.big_b[:, 0:NC * D].rearrange("p (c f) -> p c f", f=D),
              self.w["attn_wvT"].rearrange("(c p) f -> p c f", p=128))
        P.wait_q("pe", "gps")
        P.wait("dve", self.GPS, self.GPS.n)
        P.wait_q("dve", "sp")
        P.wait("pe", self.ACT, self.ACT.n)
        wb = self.big_b[:, 0:NC * D]
        vev = [0, 0]
        vst = [None, None]
        for tt in range(NT):
            b0, b1 = self.pb[0 + 2 * (tt % 2)], self.pb[1 + 2 * (tt % 2)]
            P.wait("pe", self.DVE, vev[tt % 2])
            for fc in range(NC):
                for half, bank in ((0, b0), (1, b1)):
                    P.pe(lambda e, tt=tt, fc=fc, half=half, bank=bank, wb=wb: e.matmul(
                        bank[:, 0:512],
                        self.reg_a[:, fc * S + tt * 128:fc * S + (tt + 1) * 128],
                        wb[:, fc * D + half * 512:fc * D + (half + 1) * 512],
                        start=(fc == 0), stop=(fc == NC - 1)),
                        inc=(fc == NC - 1 and half == 1))
            P.wait("dve", self.PE, self.PE.n)
            vp = (self.vprod if tt % 2 == 0 else self.vstrips)[:, 0:1032]
            if vst[tt % 2] is not None:
                P.wait("dve", vst[tt % 2][0], vst[tt % 2][1])
            P.dve(lambda e, vp=vp: e.memset(
                vp.rearrange("p (h c) -> p h c", h=NH)[:, :, 128:129], 1.0))
            P.dve(lambda e, b0=b0, vp=vp: e.tensor_copy(
                vp.rearrange("p (h c) -> p h c", h=NH)[:, 0:4, 0:128],
                b0.rearrange("p (h c) -> p h c", h=4)[:, :, :]))
            dv = P.dve(lambda e, b1=b1, vp=vp: e.tensor_copy(
                vp.rearrange("p (h c) -> p h c", h=NH)[:, 4:8, 0:128],
                b1.rearrange("p (h c) -> p h c", h=4)[:, :, :]))
            vev[tt % 2] = dv
            P.wait("sp", self.DVE, dv)
            vst[tt % 2] = P.dma("sp", self.vhat_d.rearrange("(t p) c -> t p c", p=128)[tt], vp)
        self.tap(f"vtm_attn_{b}", self.vstrips[:, :])
        self.tapd(f"qhatd_{b}", self.qhat_d[:, :], [S, D], BF16)
        self.tapd(f"khatd_{b}", self.khat_d[:, :], [S, D], BF16)
        self.tapd(f"vhatd_{b}", self.vhat_d[:, :], [S, NH * 129], BF16)

    def _emit_rms_rope(self, tt, pw, ddst, st_ms, wi, qslot):
        P = self
        if wi == 0:
            EV, ES, ev = P.dve, self.DVE, "dve"
            qf = self.scr[:, 0:D]
            sq = self.scr2[:, 0:D]
            sb = self.scr2.bitcast(BF16)
            qbf = sb[:, 0:1024].rearrange("p (h d) -> p h d", d=64)
            t1 = sb[:, 1024:1536].rearrange("p (h d) -> p h d", d=32)
            t2 = sb[:, 1536:2048].rearrange("p (h d) -> p h d", d=32)
            ms = self.stats[:, 152:168]
            lno = self.stats[:, 136:152]
            rstd = self.stats[:, 168:184]
        else:
            EV, ES, ev = P.gps, self.GPS, "gps"
            zbuf = (self.zstrip if tt % 2 == 0 else self.vstrips)
            zf = zbuf.bitcast(F32)
            qf = zf[:, 0:D]
            sq = zf[:, 1024:2048]
            sb = zbuf
            qbf = sb[:, 2048:3072].rearrange("p (h d) -> p h d", d=64)
            t1 = sb[:, 3072:3584].rearrange("p (h d) -> p h d", d=32)
            t2 = sb[:, 3584:4096].rearrange("p (h d) -> p h d", d=32)
            ms = self.stats[:, 256:272]
            lno = self.stats[:, 272:288]
            rstd = self.stats[:, 288:304]
            P.wait("act", self.GPS, self.krope_ms[tt % 2])
        P.wait("act", self.PE, self.PE.n)
        if wi == 0:
            P.wait("act", self.DVE, self.DVE.n)
        P.act(lambda e, pw=pw, qf=qf: e.copy(qf, pw[:, 0:1024]))
        av = P.act(lambda e, pw=pw, sq=sq: e.activation(out=sq, in_=pw[:, 0:1024], func=AF.Square))
        evac_av = av
        P.wait("dve", self.ACT, av)
        P.dve(lambda e, sq=sq, ms=ms: e.reduce_sum(ms, sq.rearrange("p (h d) -> p h d", d=64), axis=AX.X), inc=False)
        dv = P.dve(lambda e, ms=ms: e.tensor_scalar(ms, ms, 1.0 / 64.0, None, ALU.mult))
        P.wait("act", self.DVE, dv)
        if wi == 1:
            P.wait("act", self.GPS, self.GPS.n)
        P.act(lambda e, ms=ms, lno=lno: e.activation(out=lno, in_=ms, func=AF.Ln,
                                                     bias=self.rmseps_col))
        av = P.act(lambda e, lno=lno, rstd=rstd: e.activation(out=rstd, in_=lno,
                                                              func=AF.Exp, scale=-0.5, bias=self.lnzero_col))
        P.wait(ev, self.ACT, av)
        qv = qf.rearrange("p (h d) -> p h d", d=64)
        EV(lambda e, qv=qv, rstd=rstd, qbf=qbf: e.tensor_tensor(
            qbf, qv, rstd[:, :, None].broadcast_to([128, 16, 64]), ALU.mult), inc=False)
        cosb = self.cosb_bf[:, tt * 32:(tt + 1) * 32][:, None, :].broadcast_to([128, 16, 32])
        sinb = self.sinb_bf[:, tt * 32:(tt + 1) * 32][:, None, :].broadcast_to([128, 16, 32])
        qb = qslot.rearrange("p (h d) -> p h d", d=64)
        q1, q2 = qbf[:, :, 0:32], qbf[:, :, 32:64]
        if st_ms[tt % 2] is not None:
            P.wait(ev, st_ms[tt % 2][0], st_ms[tt % 2][1])
        EV(lambda e: e.tensor_tensor(t1, q1, cosb, ALU.mult), inc=False)
        EV(lambda e: e.tensor_tensor(t2, q2, sinb, ALU.mult), inc=False)
        EV(lambda e: e.tensor_tensor(qb[:, :, 0:32], t1, t2, ALU.add), inc=False)
        EV(lambda e: e.tensor_tensor(t1, q2, cosb, ALU.mult), inc=False)
        EV(lambda e: e.tensor_tensor(t2, q1, sinb, ALU.mult), inc=False)
        dv = EV(lambda e: e.tensor_tensor(qb[:, :, 32:64], t1, t2, ALU.subtract))
        if wi == 1:
            self.krope_ms[tt % 2] = dv
        P.wait("sp", ES, dv)
        st_ms[tt % 2] = P.dma("sp", ddst.rearrange("(t p) f -> t p f", p=128)[tt], qslot)
        return evac_av

    def emit_attn_core(self, b):
        P = self
        def issue_pair_loads(p):
            s = p % 2
            a = P.dmat("sp", self.qk_strm[:, s * 4096: s * 4096 + 2048],
                       self.qhat_d[:, p * 128:(p + 1) * 128])
            b2 = P.dmat("sp", self.qk_strm[:, s * 4096 + 2048: s * 4096 + 4096],
                        self.khat_d[:, p * 128:(p + 1) * 128])
            c = P.dma("sp", self.vstrips[:, s * 2080: s * 2080 + 2064].rearrange(
                          "p (t c) -> p t c", c=129),
                      self.vhat_d.rearrange("(t p) c -> p t c", p=128)[:, :, p * 129:(p + 1) * 129])
            return [a, b2, c]
        ready = {0: issue_pair_loads(0)}
        pe_after_pair = {}
        fixup_ms = [0, 0]
        exp_ms = [0, 0]
        av_ms = {}
        store_ms = None
        gbase = 0
        for p in range(NH):
            s = p % 2
            if p + 1 < NH:
                P.wait("sp", self.PE, pe_after_pair.get(p - 1, 0))
                ready[p + 1] = issue_pair_loads(p + 1)
            for sem_v in ready[p]:
                P.wait("pe", sem_v[0], sem_v[1])
            qT = self.qk_strm[:, s * 4096: s * 4096 + 2048]
            kT = self.qk_strm[:, s * 4096 + 2048: s * 4096 + 4096]
            vst = self.vstrips[:, s * 2080: s * 2080 + 2064]
            first_fix_of_pair = True
            tasks = [(qh, g) for qh in range(8) for g in range(qh + 1)]
            qst = {}
            for qh in range(8):
                o = qh % 2
                n_out = [0, 0]
                for kt in range(2 * qh + 2):
                    for qi in range(2):
                        if 2 * qh + qi >= kt:
                            n_out[0] += 1
                            n_out[1] += 1
                qst[qh] = {"obs": (self.pb[2 * o], self.pb[2 * o + 1]),
                           "oput": [0, 0], "n_out": n_out}

            def emit_scores(i):
                qh, g = tasks[i]
                gi = gbase + i
                kts = (2 * g, 2 * g + 1)
                pw = self.pbig[gi % 2]
                P.wait("pe", self.ACT, exp_ms[gi % 2])
                diag = [(ki, qi) for ki, kt in enumerate(kts)
                        for qi in range(2) if 2 * qh + qi == kt]
                for hi in range(2):
                    last_of_all = (hi == 1)
                    for ki, kt in enumerate(kts):
                        is_last = (not diag) and (ki == 1)
                        P.pe(lambda e, kt=kt, ki=ki, hi=hi, pw=pw, qh=qh,
                             qT=qT, kT=kT, is_last=is_last: e.matmul(
                            pw[:, hi * 512 + ki * 256:hi * 512 + (ki + 1) * 256],
                            kT[hi * 64:(hi + 1) * 64, kt * 128:(kt + 1) * 128],
                            qT[hi * 64:(hi + 1) * 64, qh * 256:(qh + 1) * 256],
                            start=(ki == 0), stop=is_last,
                            tile_position=(hi * 64, 0)),
                            inc=(is_last and last_of_all))
                    for di, (ki, qi) in enumerate(diag):
                        is_last = (di == len(diag) - 1)
                        P.pe(lambda e, pw=pw, hi=hi, ki=ki, qi=qi, is_last=is_last: e.matmul(
                            pw[:, hi * 512 + ki * 256 + qi * 128:hi * 512 + ki * 256 + (qi + 1) * 128],
                            self.ident_bf, self.diag_bf,
                            start=False, stop=is_last),
                            inc=(is_last and last_of_all))
                return self.PE.n

            def emit_exp(i, pe_ms):
                gi = gbase + i
                pw = self.pbig[gi % 2]
                P.wait("act", self.PE, pe_ms)
                P.wait("act", self.PE, av_ms.get(("pt", gi % 4), 0))
                P.act(lambda e, pw=pw, gi=gi: e.activation(
                    out=self.big_b[:, (gi % 4) * 1024:(gi % 4) * 1024 + 1024],
                    in_=pw[:, 0:1024], func=AF.Exp, scale=0.125, bias=self.neg8_col))
                exp_ms[gi % 2] = self.ACT.n
                return self.ACT.n

            def emit_av(i, act_ms):
                qh, g = tasks[i]
                gi = gbase + i
                sq = qst[qh]
                obs, oput, n_out = sq["obs"], sq["oput"], sq["n_out"]
                kts = (2 * g, 2 * g + 1)
                P.wait("pe", self.ACT, act_ms)
                if oput[0] == 0:
                    P.wait("pe", self.DVE, fixup_ms[qh % 2])
                for hi in range(2):
                    ob = obs[hi]
                    for ki, kt in enumerate(kts):
                        for qi in range(2):
                            if 2 * qh + qi < kt:
                                continue
                            first = (oput[hi] == 0)
                            oput[hi] += 1
                            last = (oput[hi] == n_out[hi])
                            P.pe(lambda e, hi=hi, gi=gi, ki=ki, kt=kt, qi=qi, ob=ob,
                                 vst=vst, first=first, last=last: e.matmul(
                                ob[:, qi * 129:(qi + 1) * 129],
                                self.big_b[:, (gi % 4) * 1024 + hi * 512 + ki * 256 + qi * 128:
                                           (gi % 4) * 1024 + hi * 512 + ki * 256 + (qi + 1) * 128],
                                vst[:, kt * 129:(kt + 1) * 129],
                                start=first, stop=last), inc=last)
                av_ms[("pt", gi % 4)] = self.PE.n
                return self.PE.n

            M = len(tasks)
            sc_ms = {0: emit_scores(0)}
            if M > 1:
                sc_ms[1] = emit_scores(1)
            e_ms = {0: emit_exp(0, sc_ms[0])}
            for i in range(M):
                if i + 1 < M:
                    e_ms[i + 1] = emit_exp(i + 1, sc_ms[i + 1])
                pe_av = emit_av(i, e_ms[i])
                if i + 2 < M:
                    sc_ms[i + 2] = emit_scores(i + 2)
                qh, g = tasks[i]
                if g != qh:
                    continue
                o = qh % 2
                obs = qst[qh]["obs"]
                P.wait("dve", self.PE, pe_av)
                if first_fix_of_pair and store_ms is not None:
                    P.wait("dve", store_ms[0], store_ms[1])
                    first_fix_of_pair = False
                ob1, ob2 = obs
                st = self.stats
                P.dve(lambda e, ob1=ob1: e.reciprocal(
                    st[:, 184:186], ob1[:, 0:258].rearrange("p (a c) -> p a c", c=129)[:, :, 128:129]))
                P.dve(lambda e, ob2=ob2: e.reciprocal(
                    st[:, 186:188], ob2[:, 0:258].rearrange("p (a c) -> p a c", c=129)[:, :, 128:129]))
                P.dve(lambda e: e.tensor_scalar(st[:, 188:190], st[:, 186:188],
                                                self.neglam_col, None, ALU.mult))
                for qi in range(2):
                    qt = 2 * qh + qi
                    zraw = self.zstrip[:, 2048 + qt * 128: 2048 + (qt + 1) * 128]
                    P.dve(lambda e, ob1=ob1, qi=qi: e.tensor_scalar(
                        self.scr2[:, qi * 128:(qi + 1) * 128],
                        ob1[:, qi * 129:qi * 129 + 128], st[:, 184 + qi:185 + qi], None, ALU.mult))
                    P.dve(lambda e, ob2=ob2, qi=qi, zraw=zraw: e.scalar_tensor_tensor(
                        zraw, ob2[:, qi * 129:qi * 129 + 128], st[:, 188 + qi:189 + qi],
                        self.scr2[:, qi * 128:(qi + 1) * 128], ALU.mult, ALU.add))
                    P.dve(lambda e, zraw=zraw: e.bn_stats(st[:, 136:142], zraw))
                    P.dve(lambda e, qt=qt: e.bn_aggr(st[:, 192 + 2 * qt:194 + 2 * qt], st[:, 136:142]))
                fixup_ms[o] = self.DVE.n
            gbase += M
            dv = self.DVE.n
            P.wait("act", self.DVE, dv)
            varv = self.stats[:, 192:224].rearrange("p (a b) -> p a b", b=2)[:, :, 1:2]
            P.act(lambda e, varv=varv: e.activation(
                out=self.stats[:, 136:152], in_=varv, func=AF.Ln, bias=self.lneps_col))
            av2 = P.act(lambda e: e.activation(
                out=self.stats[:, 224:240], in_=self.stats[:, 136:152],
                func=AF.Exp, scale=-0.5, bias=self.lnomli_col))
            P.wait("dve", self.ACT, av2)
            for qt in range(NT):
                P.dve(lambda e, qt=qt: e.tensor_scalar(
                    self.zstrip[:, qt * 128:(qt + 1) * 128],
                    self.zstrip[:, 2048 + qt * 128: 2048 + (qt + 1) * 128],
                    self.stats[:, 192 + 2 * qt:193 + 2 * qt],
                    self.stats[:, 224 + qt:225 + qt], ALU.subtract, ALU.mult))
            pe_after_pair[p] = self.PE.n
            P.wait("sp", self.DVE, self.DVE.n)
            store_ms = P.dma("sp", self.zhat_d.rearrange("(h t p) d -> h p t d", h=NH, p=128)[p],
                             self.zstrip[:, 0:2048].rearrange("p (t d) -> p t d", d=128))

    def emit_attn_out(self, b, fuse_ln=None):
        P = self
        self.tapd(f"zhatd_{b}", self.zhat_d[:, :], [NH * S, 128], BF16)
        P.wait_q("sp", "sp")
        for c in range(NC):
            P.dmat("sp", self.big_b[:, c * S:(c + 1) * S],
                   self.zhat_d.rearrange("(h t p) d -> h (t p) d", h=NH, p=128)[c])
        P.dma("gps", self.wselg[:, 0:8192].rearrange("p (c f) -> p c f", f=D), self.w["attn_wpT"].rearrange("(c p) f -> p c f", p=128))
        if getattr(self, "_prefetch_ffn_w", False):
            P.wait("gps", self.PE, self.PE.n)
            P.dma("gps", self.qk_strm[:, 0:8192].rearrange("p (c f) -> p c f", f=D),
                  self.w["ffn_wqT"].rearrange("(c p) f -> p c f", p=128))
        P.wait_q("pe", "sp")
        P.wait_q("pe", "gps")
        wp = self.wselg[:, 0:8192]
        res_ms = [0, 0]
        for tt in range(NT):
            b0, b1 = self.pb[0 + 2 * (tt % 2)], self.pb[1 + 2 * (tt % 2)]
            P.wait("pe", self.DVE, res_ms[tt % 2])
            for fc in range(NC):
                for half, bank in ((0, b0), (1, b1)):
                    P.pe(lambda e, tt=tt, fc=fc, half=half, bank=bank: e.matmul(
                        bank[:, 0:512],
                        self.big_b[:, fc * S + tt * 128:fc * S + (tt + 1) * 128],
                        wp[:, fc * D + half * 512:fc * D + (half + 1) * 512],
                        start=(fc == 0), stop=(fc == NC - 1)),
                        inc=(fc == NC - 1 and half == 1))
            P.wait("dve", self.PE, self.PE.n)
            for half, bank in ((0, b0), (1, b1)):
                xsl = self.x_res[:, tt * D + half * 512: tt * D + (half + 1) * 512]
                P.dve(lambda e, bank=bank, xsl=xsl: e.tensor_tensor(
                    xsl, xsl, bank[:, 0:512], ALU.add))
            res_ms[tt % 2] = self.DVE.n
            if fuse_ln is not None:
                self._ln_tile(tt, with_router=(fuse_ln == "router"))
        if fuse_ln is not None:
            self._ln_finish(with_router=(fuse_ln == "router"))

    def emit_attn(self, b, skip_ln=False, next_ln=None, w_prefetched=False):
        if not skip_ln:
            self.barrier()
            self.emit_ln(b, with_router=False, tag=f"attn_{b}")
        P = self
        P.wait("pe", self.DVE, self.DVE.n)
        P.wait_q("pe", "act")
        self.emit_attn_qkv(b, w_prefetched=w_prefetched)
        P.wait_q("sp", "sp")
        P.wait("dve", self.GPS, self.GPS.n)
        self.emit_attn_core(b)
        P.wait_q("sp", "sp")
        P.wait("sp", self.PE, self.PE.n)
        P.wait("act", self.PE, self.PE.n)
        self.emit_attn_out(b, fuse_ln=next_ln)
        self.barrier()

    def emit_all(self, phases=None):
        import os
        if phases is None:
            phases = int(os.environ.get("MK_PHASES", "99"))
        self.emit_setup()
        self.ystore_ms = {}
        if phases >= 99:
            for b in range(NB):
                prev = dict(self.ystore_ms) if b > 0 else None
                self.ystore_ms = {}
                self.emit_load_ln(b, prev_stores=prev)
                self._prefetch_attn_w = True
                self.emit_ffn(b, 0, skip_ln=True, next_ln="plain")
                self._prefetch_attn_w = False
                self._prefetch_ffn_w = True
                self.emit_attn(b, skip_ln=True, next_ln="router", w_prefetched=True)
                self._prefetch_ffn_w = False
                self.emit_ffn(b, 1, skip_ln=True, wq_prefetched=True, fuse_store=True)
            self.barrier()
            return
        for b in range(NB):
            self.barrier()
            self.emit_load_x(b)
            self.barrier()
            if phases >= 1:
                self.emit_ln(b, with_router=True, tag=f"ffn0_{b}")
                self.dma("gps", self.big_b[:, 0:NC * D].rearrange("p (c f) -> p c f", f=D),
                         self.w["ffn_wqT"].rearrange("(c p) f -> p c f", p=128))
            if phases >= 2:
                self.emit_router(b, 0)
            if phases >= 3:
                self.wait_q("pe", "gps")
                self.emit_wfold(0, "ffn_bq", 0)
                self.wait("gps", self.PE, self.PE.n)
                self.dma("gps", self.big_b[:, 0:NC * D].rearrange("p (c f) -> p c f", f=D),
                         self.w["ffn_wkT"].rearrange("(c p) f -> p c f", p=128))
                self.wait_q("pe", "gps")
                self.emit_wfold(1, "ffn_bk", 2)
                self.wait("gps", self.PE, self.PE.n)
                self.dma("gps", self.big_b[:, 0:NC * D].rearrange("p (c f) -> p c f", f=D),
                         self.w["ffn_wvT"].rearrange("(c p) f -> p c f", p=128))
                self.wait_q("pe", "gps")
                self.emit_wfold(2, "ffn_bv", 4)
                self.tap(f"wsel_{b}_0", self.wsel[:, :])
            if phases >= 4:
                self.barrier()
                self.emit_ffn_qkv(b, 0)
            if phases >= 5:
                self.barrier()
                self.emit_ffn_attn(b, 0)
            if phases >= 6:
                self.barrier()
                self.emit_ffn_out(b, 0)
                self.barrier()
            self.emit_store_y(b)
            break
        self.barrier()

    def finalize(self):
        nc = self.nc
        with nc.Block() as block:
            @block.tensor
            def _(t):
                for fn in self.q["pe"]:
                    fn(t)

            @block.scalar
            def _(s):
                for fn in self.q["act"]:
                    fn(s)

            @block.vector
            def _(v):
                for fn in self.q["dve"]:
                    fn(v)

            @block.gpsimd
            def _(g):
                for fn in self.q["gps"]:
                    fn(g)

            @block.sync
            def _(sy):
                for fn in self.q["sp"]:
                    fn(sy)
        return nc


_PROG_CACHE = {}


def build(taps=()):
    key = tuple(sorted(taps))
    if key in _PROG_CACHE:
        return _PROG_CACHE[key]
    mk = MK(taps)
    mk.emit_all()
    nc = mk.finalize()
    _PROG_CACHE[key] = (nc, mk)
    return nc, mk


def make_inputs(inputs, core):
    x = np.ascontiguousarray(np.asarray(inputs["L0_attn_x"])[2 * core:2 * core + 2].reshape(NB * S, D))
    t = lambda a: np.ascontiguousarray(np.asarray(a).T)
    m = {
        "x": x,
        "attn_wqT": t(inputs["L0_attn_Wq"]),
        "attn_wkT": t(inputs["L0_attn_Wk"]),
        "attn_wvT": t(inputs["L0_attn_Wv"]),
        "attn_wpT": t(inputs["L0_attn_Wproj"]),
        "ffn_wqT": t(inputs["L0_ffn_Wq"]),
        "ffn_wkT": t(inputs["L0_ffn_Wk"]),
        "ffn_wvT": t(inputs["L0_ffn_Wv"]),
        "ffn_wr": np.asarray(inputs["L0_ffn_Wr"]),
        "ffn_br": np.asarray(inputs["L0_ffn_br"]),
        "ffn_bq": np.asarray(inputs["L0_ffn_bq"]),
        "ffn_bk": np.asarray(inputs["L0_ffn_bk"]),
        "ffn_bv": np.asarray(inputs["L0_ffn_bv"]),
        "ffn_wo": np.asarray(inputs["L0_ffn_Wo"]),
        "ffn_bo": np.asarray(inputs["L0_ffn_bo"]),
        "lq1": np.asarray(inputs["L0_attn_lq1"]),
        "lk1": np.asarray(inputs["L0_attn_lk1"]),
        "lq2": np.asarray(inputs["L0_attn_lq2"]),
        "lk2": np.asarray(inputs["L0_attn_lk2"]),
    }
    return {k: np.ascontiguousarray(np.asarray(v, dtype=np.float32)) for k, v in m.items()}


def kernel(**inputs):
    nc, mk = build()
    in_maps = [make_inputs(inputs, c) for c in range(8)]
    res = run_bass_kernel_spmd(nc, in_maps, core_ids=list(range(8)))
    out = np.stack([r["y"].reshape(NB, S, D) for r in res.results])
    return out.reshape(16, S, D).astype(np.float32)
